# version 09
# baseline (speedup 1.0000x reference)
import math
import numpy as np
from contextlib import ExitStack

import concourse.bass as bass
import concourse.mybir as mybir
from concourse.bass_utils import run_bass_kernel_spmd
from concourse import library_config

F32 = mybir.dt.float32
BF16 = mybir.dt.bfloat16
I32 = mybir.dt.int32
AF = mybir.ActivationFunctionType
ALU = mybir.AluOpType

NCORES = 8
S = 2048
DM = 1024
H = 16
D = 64
NEC = DM // 128
NTC = S // 128
NB = 4
EXP = 15
IDIM = 512
NIC = IDIM // 128
EPS = 1e-5

ENGINES = ("tensor", "vector", "scalar", "gpsimd", "sync")

N_HEADS = H
STOP_AFTER = 4
TAPS = set()
TAP_TENSORS = {}

_MS = {}


class Ctr:
    __slots__ = ("h", "n")

    def __init__(self, h):
        self.h = h
        self.n = 0


class Prog:
    def __init__(self, block):
        self.block = block
        self.q = {e: [] for e in ENGINES}

    def op(self, eng, make, ctr=None, k=1):
        if ctr is not None:
            ctr.n += k
            self.q[eng].append(lambda h, m=make, s=ctr.h, kk=k: m(h).then_inc(s, kk))
            return ctr.n
        self.q[eng].append(lambda h, m=make: m(h))
        return None

    def wait(self, eng, ctr, val):
        if val > ctr.n:
            raise RuntimeError(f"wait on future count {val} > {ctr.n} ({eng})")
        self.q[eng].append(lambda h, s=ctr.h, v=val: h.wait_ge(s, v))

    def raw(self, eng, fn):
        self.q[eng].append(fn)

    def flush(self):
        for e in ENGINES:
            ops = self.q[e]
            if not ops:
                continue
            self.q[e] = []
            deco = getattr(self.block, e)

            def body(h, ops=ops):
                for fn in ops:
                    fn(h)
            deco(body)


class DmaPool:

    def __init__(self, p, eng, ctrs):
        self.p = p
        self.eng = eng
        self.ctrs = ctrs
        self.i = 0

    def issue(self, maker):
        c = self.ctrs[self.i % len(self.ctrs)]
        self.i += 1
        if c.n > 0:
            self.p.wait(self.eng, c, c.n)
        self.p.op(self.eng, maker, c, 16)
        return (c, c.n)

    def wait(self, eng, tok):
        self.p.wait(eng, tok[0], tok[1])

    def barrier(self, eng):
        for c in self.ctrs:
            if c.n > 0:
                self.p.wait(eng, c, c.n)


class Env:
    pass


def _bc2(ap, n):
    return ap.unsqueeze(2).to_broadcast((2, 2, n))


def _tap_sbuf(p, E, ctr, val, name, tiles):
    if name not in TAPS:
        return
    if not isinstance(tiles, (list, tuple)):
        tiles = [tiles]
    if name not in TAP_TENSORS:
        ap0 = tiles[0][:, :]
        shape = [len(tiles)] + list(ap0.shape) if len(tiles) > 1 else list(ap0.shape)
        TAP_TENSORS[name] = E.nc.dram_tensor("dbg_" + name, shape, ap0.dtype,
                                             kind="ExternalOutput")
    dram = TAP_TENSORS[name]
    p.wait("sync", ctr, val)
    toks = []
    for i, t in enumerate(tiles):
        toks.append(E.SY.issue(lambda h, i=i, t=t: h.dma_start(
            out=dram.ap()[i] if len(tiles) > 1 else dram.ap(),
            in_=t[:, :])))
    for eng in ("tensor", "vector", "scalar", "gpsimd"):
        for tok in toks:
            E.SY.wait(eng, tok)


def build_nc():
    TAP_TENSORS.clear()
    _MS.clear()
    nc = bass.Bass("TRN2", target_bir_lowering=False)
    E = Env()
    E.nc = nc

    E.xin = nc.dram_tensor("x", [S, DM], F32, kind="ExternalInput")
    E.w_d = {
        "q": nc.dram_tensor("wq", [DM, DM], F32, kind="ExternalInput"),
        "k": nc.dram_tensor("wk", [DM, DM], F32, kind="ExternalInput"),
        "v": nc.dram_tensor("wv", [DM, DM], F32, kind="ExternalInput"),
    }
    E.wo_d = nc.dram_tensor("wo", [DM, DM], F32, kind="ExternalInput")
    E.al_d = [nc.dram_tensor(n, [H, H, 2], F32, kind="ExternalInput")
              for n in ("aq", "ak", "av")]
    E.co_d = nc.dram_tensor("co", [H, H * 2], F32, kind="ExternalInput")
    E.rw_d = nc.dram_tensor("rw", [DM, 16], F32, kind="ExternalInput")
    E.rb_d = nc.dram_tensor("rb", [16], F32, kind="ExternalInput")
    E.gw_d = nc.dram_tensor("gw", [EXP, DM, IDIM], F32, kind="ExternalInput")
    E.uw_d = nc.dram_tensor("uw", [EXP, DM, IDIM], F32, kind="ExternalInput")
    E.dw_d = nc.dram_tensor("dw", [EXP, IDIM, DM], F32, kind="ExternalInput")
    E.cw_d = nc.dram_tensor("cw", [IDIM, DM, 3], F32, kind="ExternalInput")
    E.sup_d = nc.dram_tensor("sup", [DM, IDIM], F32, kind="ExternalInput")
    E.sdn_d = nc.dram_tensor("sdn", [IDIM, DM], F32, kind="ExternalInput")
    E.out_d = nc.dram_tensor("out", [S, DM], F32, kind="ExternalOutput")
    E.wp_d = nc.dram_tensor("wp_scratch", [3, H, 128, DM], BF16, kind="Internal")
    E.wop_d = nc.dram_tensor("wop_scratch", [H, 128, DM], BF16, kind="Internal")

    def tap(name, shape, dtype=F32):
        if name in TAPS:
            TAP_TENSORS[name] = nc.dram_tensor("dbg_" + name, list(shape), dtype,
                                               kind="ExternalOutput")
    E.tap = tap

    with ExitStack() as G:
        al = G.enter_context
        E.ct = Ctr(al(nc.semaphore("c_t")))
        E.cv = Ctr(al(nc.semaphore("c_v")))
        E.cs = Ctr(al(nc.semaphore("c_s")))
        E.cg = Ctr(al(nc.semaphore("c_g")))

        E.xln = [al(nc.sbuf_tensor(f"xln{i}", [128, S + 2], BF16)) for i in range(NEC)]
        E.identb = al(nc.sbuf_tensor("identb", [128, 128], BF16))
        E.identf = al(nc.sbuf_tensor("identf", [128, 128], F32))
        E.onesb = al(nc.sbuf_tensor("onesb", [128, 2], BF16))
        E.ones1b = al(nc.sbuf_tensor("ones1b", [1, 128], BF16))
        E.oc64 = al(nc.sbuf_tensor("oc64", [128, 66], BF16))
        E.ones1f = al(nc.sbuf_tensor("ones1f", [1, 128], F32))
        E.fsc = al(nc.sbuf_tensor("fsc", [128, 64], F32))

        with nc.Block() as block:
            p = Prog(block)
            E.SY = DmaPool(p, "sync",
                           [Ctr(al(nc.semaphore(f"dsy{i}"))) for i in range(3)])
            E.GP = DmaPool(p, "gpsimd",
                           [Ctr(al(nc.semaphore(f"dgp{i}"))) for i in range(3)])

            with ExitStack() as S02:
                sal = S02.enter_context
                E.xbf = [sal(nc.sbuf_tensor(f"xbf{i}", [128, S], BF16))
                         for i in range(NEC)]
                E.selc = sal(nc.sbuf_tensor("selc", [128, 128], F32))
                E.attnT = [sal(nc.sbuf_tensor(f"attnT{i}", [128, S], BF16))
                           for i in range(H)]

                with ExitStack() as S01:
                    E.masks = S01.enter_context(
                        nc.sbuf_tensor("masks", [128, 4 * 2048], BF16))
                    E.amix = S01.enter_context(
                        nc.sbuf_tensor("amix", [2, 3 * 256], BF16))
                    E.sel2 = S01.enter_context(
                        nc.sbuf_tensor("sel2", [2, 128], BF16))
                    E.diag64 = S01.enter_context(
                        nc.sbuf_tensor("diag64", [128, 128], BF16))

                    _phase0(p, E)
                    p.flush()

                    if STOP_AFTER >= 1:
                        _phase1(p, E)
                        p.flush()

                if STOP_AFTER >= 2:
                    _phase2(p, E)
                    p.flush()

            if STOP_AFTER >= 3:
                with ExitStack() as S34:
                    E.acc = [S34.enter_context(
                        nc.sbuf_tensor(f"acc{i}", [128, S], BF16))
                        for i in range(NEC)]
                    _phase3(p, E)
                    p.flush()
                    _phase4(p, E)
                    p.flush()
            else:
                _dummy_out(p, E)
                p.flush()

    return nc


def _phase0(p, E):
    nc = E.nc
    ct, cv, cs, cg = E.ct, E.cv, E.cs, E.cg
    SY, GP = E.SY, E.GP
    with ExitStack() as S0:
        al = S0.enter_context
        xint = [al(nc.sbuf_tensor(f"xint{i}", [128, DM], F32)) for i in range(2)]
        wtmp = al(nc.sbuf_tensor("wtmp", [128, DM], BF16))
        wstg = [al(nc.sbuf_tensor(f"wstg{i}", [128, DM], BF16)) for i in range(2)]
        stile = [al(nc.sbuf_tensor(f"stile{i}", [128, 128], BF16))
                 for i in range(16)]
        co2 = al(nc.sbuf_tensor("co2", [2, 256], BF16))
        psA = al(nc.psum_tensor("p0A", [128, 1024], F32))
        psV = al(nc.psum_tensor("p0V", [128, 1024], F32))
        psT = al(nc.psum_tensor("p0T", [128, 1024], BF16))

        with ExitStack() as Sio:
            iosc = Sio.enter_context(nc.sbuf_tensor("iosc", [128, 2560], I32))

            def g_setup(h):
                h.iota(iosc[:, 0:128], [[-1, 128]], base=0, channel_multiplier=1)
                h.iota(iosc[:, 128:256], [[-1, 128]], base=128, channel_multiplier=1)
                h.iota(iosc[0:2, 2304:2432], [[1, 128]], base=0,
                       channel_multiplier=-64)
                h.iota(iosc[64:66, 2304:2432], [[1, 128]], base=0,
                       channel_multiplier=-64)
                for j in range(4):
                    h.iota(iosc[:, 256 + j * 512: 256 + (j + 1) * 512],
                           [[1, 512]], base=-128 * j, channel_multiplier=-1)
                h.memset(E.onesb[:, :], 1.0)
                h.memset(E.oc64[:, :], 0.0)
                h.memset(E.ones1b[0:1, :], 1.0)
                h.memset(E.ones1f[0:1, :], 1.0)
                h.memset(E.sel2[:, :], 0.0)
                h.memset(E.selc[:, :], 0.0)
                h.memset(E.fsc[:, :], 0.0)
                return h.drain()
            p.op("gpsimd", g_setup, cg)

            def g_setup2(h):
                h.memset(E.oc64[:, 1:2], 1.0)
                for ec in range(NEC):
                    h.memset(E.xln[ec][:, 0:2], 0.0)
                return h.drain()
            p.op("gpsimd", g_setup2, cg)
            g_consts = cg.n

            for idx, t in enumerate(E.al_d):
                GP.issue(lambda h, t=t, idx=idx: h.dma_start(
                    out=E.amix[0:2, idx * 256:(idx + 1) * 256],
                    in_=bass.AP(t, 0, [[32, 2], [64, 8], [1, 32]])))
            GP.issue(lambda h: h.dma_start(
                out=co2[0:2, :],
                in_=bass.AP(E.co_d, 0, [[32, 2], [64, 8], [1, 32]])))
            GP.issue(lambda h: h.dma_start(
                out=E.fsc[0:16, 0:1],
                in_=bass.AP(E.rb_d, 0, [[1, 16], [1, 1]])))
            gp_small = [(c, c.n) for c in GP.ctrs]

            p.wait("vector", cg, g_consts)

            def v_setup(h):
                h.tensor_scalar(E.identb[:, :], iosc[:, 0:128], 0, None,
                                op0=ALU.is_equal)
                h.tensor_scalar(E.identf[:, :], iosc[:, 0:128], 0, None,
                                op0=ALU.is_equal)
                h.tensor_scalar(iosc[0:2, 2432:2560], iosc[0:2, 2304:2432], 64,
                                None, op0=ALU.is_lt)
                h.tensor_scalar(iosc[0:2, 2304:2432], iosc[0:2, 2304:2432], 0,
                                None, op0=ALU.is_ge)
                h.tensor_scalar(iosc[64:66, 2432:2560], iosc[64:66, 2304:2432],
                                64, None, op0=ALU.is_lt)
                h.tensor_scalar(iosc[64:66, 2304:2432], iosc[64:66, 2304:2432],
                                0, None, op0=ALU.is_ge)
                return h.drain()
            p.op("vector", v_setup, cv)

            def v_setup2(h):
                h.tensor_tensor(E.sel2[0:2, :], iosc[0:2, 2304:2432],
                                iosc[0:2, 2432:2560], op=ALU.mult)
                h.tensor_tensor(E.selc[64:66, :], iosc[64:66, 2304:2432],
                                iosc[64:66, 2432:2560], op=ALU.mult)
                h.tensor_scalar(E.diag64[:, :], iosc[:, 128:256], 128, None,
                                op0=ALU.is_equal)
                h.tensor_scalar(stile[0][:, :], iosc[:, 128:256], 64, None,
                                op0=ALU.is_equal)
                h.tensor_scalar(stile[1][:, :], iosc[:, 128:256], 192, None,
                                op0=ALU.is_equal)
                h.drain()
                h.tensor_tensor(E.diag64[:, :], E.diag64[:, :], stile[0][:, :],
                                op=ALU.add)
                h.drain()
                h.tensor_tensor(E.diag64[:, :], E.diag64[:, :], stile[1][:, :],
                                op=ALU.add)
                h.drain()
                last = None
                for j in range(4):
                    srcm = iosc[:, 256 + j * 512: 256 + (j + 1) * 512]
                    for pl in range(4):
                        thr = 1 if pl == 3 else 0
                        last = h.tensor_scalar(
                            E.masks[:, j * 2048 + pl * 512:
                                    j * 2048 + (pl + 1) * 512],
                            srcm, thr, None, op0=ALU.is_ge)
                return last
            p.op("vector", v_setup2, cv)
            v_consts = cv.n

            for c, v in gp_small:
                p.wait("vector", c, v)

            def v_alpha(h):
                return h.tensor_scalar_mul(E.amix[0:2, 0:256], E.amix[0:2, 0:256],
                                           1.0 / math.sqrt(D))
            p.op("vector", v_alpha, cv)
            v_alpha_done = cv.n

            xtr_t0 = ct.n
            xtr_v0 = cv.n
            p.wait("tensor", cv, v_consts)
            for tc in range(NTC):
                buf = tc % 2
                if tc >= 2:
                    p.wait("sync", ct, xtr_t0 + (tc - 1))

                xtok = SY.issue(lambda h, tc=tc, buf=buf: h.dma_start(
                    out=xint[buf][:, :],
                    in_=E.xin.ap()[tc * 128:(tc + 1) * 128, :]))
                SY.wait("tensor", xtok)
                if tc >= 1:
                    p.wait("tensor", cv, xtr_v0 + (tc - 1) * 2 + 2)

                def t_xtr(h, buf=buf):
                    last = None
                    for ec in range(NEC):
                        half = ec // 4
                        last = h.transpose(
                            psA[:, half * 512 + (ec % 4) * 128:
                                half * 512 + (ec % 4) * 128 + 128],
                            xint[buf][:, ec * 128:(ec + 1) * 128], E.identf[:, :])
                    return last
                p.op("tensor", t_xtr, ct)

                p.wait("vector", ct, ct.n)
                for half in range(2):
                    def v_xe(h, tc=tc, half=half):
                        last = None
                        for k4 in range(4):
                            ec = half * 4 + k4
                            last = h.tensor_copy(
                                E.xbf[ec][:, tc * 128:(tc + 1) * 128],
                                psA[:, half * 512 + k4 * 128:
                                    half * 512 + (k4 + 1) * 128])
                        return last
                    p.op("vector", v_xe, cv)
            _MS["x_done_v"] = cv.n
            _tap_sbuf(p, E, cv, cv.n, "xbf", E.xbf)
        iosc_free_v = cv.n

        with ExitStack() as Swo:
            wal = Swo.enter_context
            wonat = [wal(nc.sbuf_tensor(f"wonat{i}", [128, DM], BF16))
                     for i in range(NEC)]
            wot = [wal(nc.sbuf_tensor(f"wot{i}", [128, DM], BF16))
                   for i in range(NEC)]
            p.wait("gpsimd", cv, iosc_free_v)
            wo_tok = {}
            for fc in range(NEC):
                wo_tok[fc] = GP.issue(lambda h, fc=fc: h.dma_start(
                    out=wonat[fc][:, :],
                    in_=E.wo_d.ap()[fc * 128:(fc + 1) * 128, :]))

            for fc in range(NEC):
                GP.wait("tensor", wo_tok[fc])
            for hc in range(NEC):
                def t_wot(h, hc=hc):
                    last = None
                    for fc in range(NEC):
                        last = h.transpose(psT[:, fc * 128:(fc + 1) * 128],
                                           wonat[fc][:, hc * 128:(hc + 1) * 128],
                                           E.identb[:, :])
                    return last
                p.op("tensor", t_wot, ct)
                p.wait("vector", ct, ct.n)

                def v_wot(h, hc=hc):
                    return h.tensor_copy(wot[hc][:, :], psT[:, :])
                p.op("vector", v_wot, cv)
                p.wait("tensor", cv, cv.n)

            p.wait("tensor", cg, g_consts)
            viter_done = {-1: cv.n}
            for i in range(H + 1):
                p.wait("tensor", cv, viter_done[i - 1])
                vbc_done = kr_done = None
                if i < H:
                    def t_vbc8(h, c=i):
                        last = None
                        for mc in range(NEC):
                            last = h.matmul(psV[:, mc * 128:(mc + 1) * 128],
                                            E.sel2[0:2, :],
                                            _bc2(co2[0:2, mc * 32 + 2 * c:
                                                     mc * 32 + 2 * c + 2], 64),
                                            start=True, stop=True)
                        return last
                    p.op("tensor", t_vbc8, ct)
                    vbc_done = ct.n
                if i - 1 >= 0:
                    def t_kr8(h, sb=((i - 1) % 2) * 8):
                        last = None
                        for mc in range(NEC):
                            h.matmul(psA[:, 0:512], stile[sb + mc][:, :],
                                     wot[mc][:, 0:512],
                                     start=(mc == 0), stop=(mc == NEC - 1))
                            last = h.matmul(psA[:, 512:1024],
                                            stile[sb + mc][:, :],
                                            wot[mc][:, 512:1024],
                                            start=(mc == 0), stop=(mc == NEC - 1))
                        return last
                    p.op("tensor", t_kr8, ct)
                    kr_done = ct.n

                if i < H:
                    p.wait("vector", ct, vbc_done)

                    def v_st8(h, sb=(i % 2) * 8):
                        last = None
                        for mc in range(NEC):
                            last = h.tensor_tensor(stile[sb + mc][:, :],
                                                   E.diag64[:, :],
                                                   psV[:, mc * 128:(mc + 1) * 128],
                                                   op=ALU.mult)
                        return last
                    p.op("vector", v_st8, cv)
                if i - 1 >= 0:
                    c = i - 1
                    p.wait("vector", ct, kr_done)
                    stg = wstg[c % 2]
                    if c >= 2:
                        SY.wait("vector", _MS[f"wo_c{c - 2}_dma"])

                    def v_ev(h, stg=stg):
                        h.tensor_copy(stg[:, 0:512], psA[:, 0:512])
                        return h.tensor_copy(stg[:, 512:1024], psA[:, 512:1024])
                    p.op("vector", v_ev, cv)
                    p.wait("sync", cv, cv.n)
                    _MS[f"wo_c{c}_dma"] = SY.issue(
                        lambda h, c=c, stg=stg: h.dma_start(out=E.wop_d.ap()[c],
                                                            in_=stg[:, :]))
                viter_done[i] = cv.n
        wo_free_t = ct.n

        prev_dma = {}
        for ti, t in enumerate("qkv"):
            with ExitStack() as Swt:
                wnat = [Swt.enter_context(
                    nc.sbuf_tensor(f"wnat_{t}{i}", [128, DM], BF16))
                    for i in range(NEC)]
                p.wait("gpsimd", ct, wo_free_t if ti == 0 else _MS[f"wprep_t_{ti - 1}"])
                wn_tok = {}
                for mc in range(NEC):
                    wn_tok[mc] = GP.issue(lambda h, t=t, mc=mc: h.dma_start(
                        out=wnat[mc][:, :],
                        in_=E.w_d[t].ap()[mc * 128:(mc + 1) * 128, :]))
                for mc in range(NEC):
                    GP.wait("tensor", wn_tok[mc])

                viter_done = {-1: cv.n}
                for i in range(H + 2):
                    p.wait("tensor", cv, viter_done[i - 1])
                    vbc_done = kr_done = tr_done = None
                    if i < H:
                        def t_vbc8(h, ti=ti, hd=i):
                            last = None
                            for mc in range(NEC):
                                last = h.matmul(
                                    psV[:, mc * 128:(mc + 1) * 128],
                                    E.sel2[0:2, :],
                                    _bc2(E.amix[0:2,
                                                ti * 256 + mc * 32 + 2 * hd:
                                                ti * 256 + mc * 32 + 2 * hd + 2],
                                         64),
                                    start=True, stop=True)
                            return last
                        p.op("tensor", t_vbc8, ct)
                        vbc_done = ct.n
                    if 0 <= i - 2 < H:
                        def t_tr(h):
                            last = None
                            for ec in range(NEC):
                                last = h.transpose(
                                    psT[:, ec * 128:(ec + 1) * 128],
                                    wtmp[:, ec * 128:(ec + 1) * 128],
                                    E.identb[:, :])
                            return last
                        p.op("tensor", t_tr, ct)
                        tr_done = ct.n
                    if 0 <= i - 1 < H:
                        def t_kr8(h, sb=((i - 1) % 2) * 8):
                            last = None
                            for mc in range(NEC):
                                h.matmul(psA[:, 0:512], stile[sb + mc][:, :],
                                         wnat[mc][:, 0:512],
                                         start=(mc == 0), stop=(mc == NEC - 1))
                                last = h.matmul(psA[:, 512:1024],
                                                stile[sb + mc][:, :],
                                                wnat[mc][:, 512:1024],
                                                start=(mc == 0),
                                                stop=(mc == NEC - 1))
                            return last
                        p.op("tensor", t_kr8, ct)
                        kr_done = ct.n

                    if i < H:
                        p.wait("vector", ct, vbc_done)

                        def v_st8(h, sb=(i % 2) * 8):
                            last = None
                            for mc in range(NEC):
                                last = h.tensor_tensor(
                                    stile[sb + mc][:, :], E.diag64[:, :],
                                    psV[:, mc * 128:(mc + 1) * 128],
                                    op=ALU.mult)
                            return last
                        p.op("vector", v_st8, cv)
                    if 0 <= i - 2 < H:
                        hd = i - 2
                        p.wait("vector", ct, tr_done)
                        stg = wstg[hd % 2]
                        k2 = (ti * H + hd) - 2
                        if k2 >= 0:
                            SY.wait("vector", prev_dma[k2])
                        else:
                            SY.wait("vector", _MS[f"wo_c{H + k2}_dma"])

                        def v_stg(h, stg=stg):
                            return h.tensor_copy(stg[:, :], psT[:, :])
                        p.op("vector", v_stg, cv)
                        p.wait("sync", cv, cv.n)
                        tok = SY.issue(
                            lambda h, ti=ti, hd=hd, stg=stg: h.dma_start(
                                out=E.wp_d.ap()[ti, hd], in_=stg[:, :]))
                        prev_dma[ti * H + hd] = tok
                        _MS[f"w{t}_h{hd}_dma"] = tok
                    if 0 <= i - 1 < H:
                        p.wait("vector", ct, kr_done)

                        def v_ev(h):
                            h.tensor_copy(wtmp[:, 0:512], psA[:, 0:512])
                            return h.tensor_copy(wtmp[:, 512:1024],
                                                 psA[:, 512:1024])
                        p.op("vector", v_ev, cv)
                    viter_done[i] = cv.n
                _MS[f"wprep_t_{ti}"] = ct.n

        _MS["p0_v"] = cv.n
        _MS["p0_t"] = ct.n


def _phase1(p, E):
    nc = E.nc
    ct, cv, cs, cg = E.ct, E.cv, E.cs, E.cg
    SY, GP = E.SY, E.GP
    avd = {}
    with ExitStack() as S1:
        al = S1.enter_context
        wsl = [[al(nc.sbuf_tensor(f"wsl{t}{b}", [128, DM], BF16)) for b in range(2)]
               for t in range(3)]
        qh = al(nc.sbuf_tensor("qh", [128, S], BF16))
        kh = al(nc.sbuf_tensor("kh", [128, S], BF16))
        vh = al(nc.sbuf_tensor("vh", [128, S], BF16))
        qsw = al(nc.sbuf_tensor("qsw", [128, S], BF16))
        vtok = al(nc.sbuf_tensor("vtok", [128, S], BF16))
        probs = al(nc.sbuf_tensor("probs", [128, 3 * 2048], BF16))
        ccv = al(nc.sbuf_tensor("ccv", [128, 512], F32))
        cinv = al(nc.sbuf_tensor("cinv", [128, 512], F32))
        paF = al(nc.psum_tensor("p1F", [128, 3584], F32))
        paB = al(nc.psum_tensor("p1B", [128, 1024], BF16))

        p.wait("vector", cv, _MS["p0_v"])

        def v_cc0(h):
            return h.memset(ccv[:, :], 1.0)
        p.op("vector", v_cc0, cv)

        tgt = {0: qh, 1: kh, 2: vh}
        for hd in range(N_HEADS):
            buf = hd % 2
            wsl_toks = []
            for ti in range(3):
                SY.wait("sync", _MS[f"w{'qkv'[ti]}_h{hd}_dma"])
                if hd >= 2:
                    p.wait("sync", ct, _MS[f"proj_t_h{hd - 2}"])
                wsl_toks.append(SY.issue(lambda h, ti=ti, hd=hd, buf=buf: h.dma_start(
                    out=wsl[ti][buf][:, :], in_=E.wp_d.ap()[ti, hd])))

            for tok in wsl_toks:
                SY.wait("tensor", tok)
            pevq = []
            for ti in range(3):
                dst = tgt[ti]
                for blk in range(NB):
                    pslot = 1024 + (blk % 2) * 512
                    if len(pevq) >= 2:
                        p.wait("tensor", cv, pevq[-2])
                    else:
                        p.wait("tensor", cv, cv.n)

                    def t_proj(h, ti=ti, blk=blk, pslot=pslot, buf=buf):
                        last = None
                        for ec in range(NEC):
                            last = h.matmul(paF[:, pslot:pslot + 512],
                                            wsl[ti][buf][:, ec * 128:(ec + 1) * 128],
                                            E.xbf[ec][:, blk * 512:(blk + 1) * 512],
                                            start=(ec == 0), stop=(ec == NEC - 1))
                        return last
                    p.op("tensor", t_proj, ct)
                    p.wait("vector", ct, ct.n)

                    def v_proj(h, dst=dst, blk=blk, pslot=pslot):
                        return h.tensor_copy(dst[:, blk * 512:(blk + 1) * 512],
                                             paF[:, pslot:pslot + 512])
                    p.op("vector", v_proj, cv)
                    pevq.append(cv.n)
            _MS[f"proj_t_h{hd}"] = ct.n
            proj_v = cv.n
            p.wait("tensor", cv, proj_v)

            p.wait("gpsimd", cv, proj_v)
            sw1 = GP.issue(lambda h: h.dma_start(out=qsw[0:64, :],
                                                 in_=qh[64:128, :]))
            sw2 = GP.issue(lambda h: h.dma_start(out=qsw[64:128, :],
                                                 in_=qh[0:64, :]))

            for grp in range(4):
                def t_vtr(h, grp=grp):
                    last = None
                    for c4 in range(4):
                        c = grp * 4 + c4
                        last = h.transpose(paB[:, c4 * 128:(c4 + 1) * 128],
                                           vh[:, c * 128:(c + 1) * 128],
                                           E.identb[:, :])
                    return last
                p.op("tensor", t_vtr, ct)
                p.wait("vector", ct, ct.n)

                def v_vtr(h, grp=grp):
                    return h.tensor_copy(vtok[:, grp * 512:(grp + 1) * 512],
                                         paB[:, 0:512])
                p.op("vector", v_vtr, cv)
                p.wait("tensor", cv, cv.n)
            vtok_v = cv.n

            if hd == 0 and "qkvh0" in TAPS:
                GP.wait("sync", sw1)
                GP.wait("sync", sw2)
                _tap_sbuf(p, E, cv, vtok_v, "qkvh0", [qh, kh, vh, qsw, vtok])

            GP.wait("tensor", sw1)
            GP.wait("tensor", sw2)
            p.wait("scalar", cv, vtok_v)
            for nb in range(NB):
                nkc = 4 * nb + 4
                ready = {}

                def emit_av(c, nkc=nkc):
                    slot = c % 3
                    rc, rv = ready[c]
                    p.wait("tensor", rc, rv)

                    def t_av(h, c=c, nkc=nkc, slot=slot):
                        last = None
                        for pp in range(2):
                            st = (c == 0) and (pp == 0)
                            sp = (c == nkc - 1) and (pp == 1)
                            vs = vtok[:, c * 128 + pp * 64: c * 128 + pp * 64 + 64]
                            o0 = slot * 2048 + (0 if pp == 0 else 3) * 512
                            mv0 = probs[:, o0:o0 + 512]
                            h.matmul(paF[0:64, 2048:2560], vs, mv0,
                                     start=st, stop=sp, tile_position=(0, 0))
                            h.matmul(paF[64:128, 3072:3584], E.oc64[:, 1:65], mv0,
                                     start=st, stop=(c == nkc - 1) and (pp == 1)
                                     and False, tile_position=(0, 64))
                            o1 = slot * 2048 + (2 if pp == 0 else 1) * 512
                            mv1 = probs[:, o1:o1 + 512]
                            h.matmul(paF[64:128, 2560:3072], vs, mv1,
                                     start=st, stop=sp, tile_position=(0, 64))
                            last = h.matmul(paF[64:128, 3072:3584],
                                            E.oc64[:, 0:64], mv1,
                                            start=False, stop=sp,
                                            tile_position=(0, 64))
                        return last
                    p.op("tensor", t_av, ct)
                    avd[(hd, nb, c)] = ct.n

                eh1 = {}
                eh2 = {}
                for c in range(nkc):
                    slot = c % 3
                    if c > 0:
                        p.wait("tensor", cs, eh1[c - 1])

                    def t_sc1(h, c=c, nb=nb):
                        h.matmul(paF[:, 0:512], kh[0:64, c * 128:(c + 1) * 128],
                                 qh[0:64, nb * 512:(nb + 1) * 512],
                                 start=True, stop=True, tile_position=(0, 0))
                        return h.matmul(paF[:, 512:1024],
                                        kh[64:128, c * 128:(c + 1) * 128],
                                        qh[64:128, nb * 512:(nb + 1) * 512],
                                        start=True, stop=True,
                                        tile_position=(64, 0))
                    p.op("tensor", t_sc1, ct)
                    sc1_t = ct.n

                    if c > 0:
                        p.wait("tensor", cs, eh2[c - 1])

                    def t_sc2(h, c=c, nb=nb):
                        h.matmul(paF[:, 1024:1536], kh[0:64, c * 128:(c + 1) * 128],
                                 qsw[0:64, nb * 512:(nb + 1) * 512],
                                 start=True, stop=True, tile_position=(0, 0))
                        return h.matmul(paF[:, 1536:2048],
                                        kh[64:128, c * 128:(c + 1) * 128],
                                        qsw[64:128, nb * 512:(nb + 1) * 512],
                                        start=True, stop=True,
                                        tile_position=(64, 0))
                    p.op("tensor", t_sc2, ct)
                    sc2_t = ct.n

                    p.wait("scalar", ct, sc1_t)
                    if c >= 3:
                        p.wait("scalar", ct, avd[(hd, nb, c - 3)])

                    def s_exp1(h, slot=slot):
                        return h.activation(
                            out=probs[:, slot * 2048: slot * 2048 + 1024],
                            in_=paF[:, 0:1024], func=AF.Exp)
                    p.op("scalar", s_exp1, cs)
                    eh1[c] = cs.n

                    p.wait("scalar", ct, sc2_t)

                    def s_exp2(h, slot=slot):
                        return h.activation(
                            out=probs[:, slot * 2048 + 1024: slot * 2048 + 2048],
                            in_=paF[:, 1024:2048], func=AF.Exp)
                    p.op("scalar", s_exp2, cs)
                    eh2[c] = cs.n

                    j = c - 4 * nb
                    if j >= 0:
                        p.wait("vector", cs, eh2[c])

                        def v_mask(h, slot=slot, j=j):
                            return h.tensor_tensor(
                                probs[:, slot * 2048:(slot + 1) * 2048],
                                probs[:, slot * 2048:(slot + 1) * 2048],
                                E.masks[:, j * 2048:(j + 1) * 2048], op=ALU.mult)
                        p.op("vector", v_mask, cv)
                        ready[c] = (cv, cv.n)
                    else:
                        ready[c] = (cs, eh2[c])

                    if c > 0:
                        emit_av(c - 1)
                emit_av(nkc - 1)

                p.wait("vector", ct, ct.n)

                def v_recip(h):
                    return h.reciprocal(ccv[64:66, :], paF[64:66, 3072:3584])
                p.op("vector", v_recip, cv)
                p.wait("tensor", cv, cv.n)

                def t_cinv(h):
                    return h.matmul(paF[:, 0:512], E.selc[:, :], ccv[:, :],
                                    start=True, stop=True)
                p.op("tensor", t_cinv, ct)
                p.wait("vector", ct, ct.n)

                def v_attn(h, nb=nb, hd=hd):
                    h.tensor_copy(cinv[:, :], paF[:, 0:512])
                    h.drain()
                    h.tensor_tensor(E.attnT[hd][0:64, nb * 512:(nb + 1) * 512],
                                    paF[0:64, 2048:2560], cinv[0:64, :], op=ALU.mult)
                    return h.tensor_tensor(
                        E.attnT[hd][64:128, nb * 512:(nb + 1) * 512],
                        paF[64:128, 2560:3072], cinv[64:128, :], op=ALU.mult)
                p.op("vector", v_attn, cv)
                p.wait("tensor", cv, cv.n)

            if hd == 0:
                _tap_sbuf(p, E, cv, cv.n, "attnh0", [E.attnT[0]])

        _MS["p1_v"] = cv.n
        _MS["p1_t"] = ct.n


def _phase2(p, E):
    nc = E.nc
    ct, cv, cs, cg = E.ct, E.cv, E.cs, E.cg
    SY, GP = E.SY, E.GP
    with ExitStack() as S2:
        al = S2.enter_context
        wos = [al(nc.sbuf_tensor(f"wos{i}", [128, DM], BF16)) for i in range(2)]
        x1 = [al(nc.sbuf_tensor(f"x1_{i}", [128, S], BF16)) for i in range(NEC)]
        x1s = [al(nc.sbuf_tensor(f"x1s{i}", [128, S], BF16)) for i in range(2)]
        stat = al(nc.sbuf_tensor("stat", [1, 2 * S], F32))
        mbc = al(nc.sbuf_tensor("mbc", [128, S], F32))
        rbc = al(nc.sbuf_tensor("rbc", [128, S], F32))
        p2F = al(nc.psum_tensor("p2F", [128, 4096], F32))
        p2S = p2F

        p.wait("vector", cv, _MS["p1_v"])
        p.wait("tensor", cv, _MS["p1_v"])
        p.wait("sync", cv, _MS["p1_v"])
        for fp in range(4):
            for oc in range(H):
                b = oc % 2
                if fp * H + oc >= 2:
                    k2 = fp * H + oc - 2
                    p.wait("sync", ct, _MS[f"wo_mm_{k2 // H}_{k2 % H}"])

                wtok = SY.issue(lambda h, oc=oc, b=b: h.dma_start(
                    out=wos[b][:, :], in_=E.wop_d.ap()[oc]))
                SY.wait("tensor", wtok)
                if oc == 0 and fp > 0:
                    p.wait("tensor", cv, _MS[f"x1ev_{fp - 1}"])

                def t_wo(h, fp=fp, oc=oc, b=b):
                    last = None
                    for half in range(2):
                        fc = 2 * fp + half
                        for nb in range(NB):
                            last = h.matmul(
                                p2F[:, half * 2048 + nb * 512:
                                    half * 2048 + (nb + 1) * 512],
                                wos[b][:, fc * 128:(fc + 1) * 128],
                                E.attnT[oc][:, nb * 512:(nb + 1) * 512],
                                start=(oc == 0), stop=(oc == H - 1))
                    return last
                p.op("tensor", t_wo, ct)
                _MS[f"wo_mm_{fp}_{oc}"] = ct.n
            p.wait("vector", ct, ct.n)

            def v_x1(h, fp=fp):
                last = None
                for half in range(2):
                    fc = 2 * fp + half
                    last = h.tensor_tensor(x1[fc][:, :],
                                           p2F[:, half * 2048:(half + 1) * 2048],
                                           E.xbf[fc][:, :], op=ALU.add)
                return last
            p.op("vector", v_x1, cv)
            _MS[f"x1ev_{fp}"] = cv.n
        x1_v = cv.n
        _tap_sbuf(p, E, cv, x1_v, "x1", x1)

        p.wait("scalar", cv, x1_v)
        p.wait("tensor", cv, x1_v)
        for ec in range(NEC):
            def t_sum(h, ec=ec):
                last = None
                for nb in range(NB):
                    last = h.matmul(p2S[0:1, nb * 512:(nb + 1) * 512],
                                    E.onesb[:, 0:1],
                                    x1[ec][:, nb * 512:(nb + 1) * 512],
                                    start=(ec == 0), stop=(ec == NEC - 1))
                return last
            p.op("tensor", t_sum, ct)
        sum_t = ct.n
        sqmm_done = {}
        for ec in range(NEC):
            b = ec % 2
            if ec >= 2:
                p.wait("scalar", ct, sqmm_done[ec - 2])

            def s_sq(h, ec=ec, b=b):
                return h.activation(out=x1s[b][:, :], in_=x1[ec][:, :],
                                    func=AF.Square)
            p.op("scalar", s_sq, cs)
            p.wait("tensor", cs, cs.n)

            def t_sq(h, ec=ec, b=b):
                last = None
                for nb in range(NB):
                    last = h.matmul(p2S[0:1, S + nb * 512: S + (nb + 1) * 512],
                                    E.onesb[:, 0:1],
                                    x1s[b][:, nb * 512:(nb + 1) * 512],
                                    start=(ec == 0), stop=(ec == NEC - 1))
                return last
            p.op("tensor", t_sq, ct)
            sqmm_done[ec] = ct.n
        p.wait("vector", ct, ct.n)

        def v_stat(h):
            h.tensor_copy(stat[0:1, 0:S], p2S[0:1, 0:S])
            return h.tensor_copy(stat[0:1, S:2 * S], p2S[0:1, S:2 * S])
        p.op("vector", v_stat, cv)
        p.wait("tensor", cv, cv.n)

        def v_ln1a(h):
            h.drain()
            h.tensor_scalar_mul(stat[0:1, 0:S], stat[0:1, 0:S], 1.0 / DM)
            h.tensor_scalar_mul(stat[0:1, S:2 * S], stat[0:1, S:2 * S], 1.0 / DM)
            h.drain()
            h.tensor_tensor(mbc[0:1, :], stat[0:1, 0:S], stat[0:1, 0:S],
                            op=ALU.mult)
            h.drain()
            h.tensor_tensor(stat[0:1, S:2 * S], stat[0:1, S:2 * S],
                            mbc[0:1, :], op=ALU.subtract)
            h.drain()
            return h.tensor_scalar_add(stat[0:1, S:2 * S],
                                       stat[0:1, S:2 * S], EPS)
        p.op("vector", v_ln1a, cv)
        p.wait("scalar", cv, cv.n)

        def s_rstd(h):
            h.activation(out=mbc[0:1, :], in_=stat[0:1, S:2 * S], func=AF.Ln)
            h.drain()
            return h.activation(out=stat[0:1, S:2 * S],
                                in_=mbc[0:1, :], func=AF.Exp, scale=-0.5)
        p.op("scalar", s_rstd, cs)
        p.wait("tensor", cs, cs.n)

        def t_bc(h):
            last = None
            for blk in range(4):
                last = h.matmul(p2F[:, blk * 512:(blk + 1) * 512], E.ones1f[0:1, :],
                                stat[0:1, blk * 512:(blk + 1) * 512],
                                start=True, stop=True)
                last = h.matmul(p2F[:, 2048 + blk * 512: 2048 + (blk + 1) * 512],
                                E.ones1f[0:1, :],
                                stat[0:1, S + blk * 512: S + (blk + 1) * 512],
                                start=True, stop=True)
            return last
        p.op("tensor", t_bc, ct)
        p.wait("vector", ct, ct.n)

        def v_bc(h):
            last = None
            for blk in range(4):
                last = h.tensor_copy(mbc[:, blk * 512:(blk + 1) * 512],
                                     p2F[:, blk * 512:(blk + 1) * 512])
                last = h.tensor_copy(rbc[:, blk * 512:(blk + 1) * 512],
                                     p2F[:, 2048 + blk * 512: 2048 + (blk + 1) * 512])
            return last
        p.op("vector", v_bc, cv)
        p.raw("vector", lambda h: h.drain())

        for ec in range(NEC):
            def v_ap1(h, ec=ec):
                return h.tensor_tensor(x1[ec][:, :], x1[ec][:, :], mbc[:, :],
                                       op=ALU.subtract)
            p.op("vector", v_ap1, cv)
            p.raw("vector", lambda h: h.drain())

            def v_ap2(h, ec=ec):
                return h.tensor_tensor(E.xln[ec][:, 2:2 + S], x1[ec][:, :],
                                       rbc[:, :], op=ALU.mult)
            p.op("vector", v_ap2, cv)
        _MS["p2_v"] = cv.n
        _MS["p2_t"] = ct.n
        _MS["p2_s"] = cs.n
        _tap_sbuf(p, E, cv, cv.n, "xln", E.xln)


def _phase3(p, E):
    nc = E.nc
    ct, cv, cs, cg = E.ct, E.cv, E.cs, E.cg
    SY, GP = E.SY, E.GP

    def xs(ec, lo, n):
        return E.xln[ec][:, 2 + lo: 2 + lo + n]

    with ExitStack() as S3:
        al = S3.enter_context
        rwt = al(nc.sbuf_tensor("rwt", [128, 128], BF16))
        combT = al(nc.sbuf_tensor("combT", [16, 2048], F32))
        cstage = al(nc.sbuf_tensor("cstage", [1, 2048], BF16))
        SC = ExitStack()
        cal = SC.enter_context
        cwT = [cal(nc.sbuf_tensor(f"cwT{k}", [128, 4096], BF16)) for k in range(3)]
        sgate = [cal(nc.sbuf_tensor(f"sgate{i}", [128, S], BF16))
                 for i in range(NIC)]
        smid = [cal(nc.sbuf_tensor(f"smid{i}", [128, S], BF16))
                for i in range(NIC)]
        supt = cal(nc.sbuf_tensor("supt", [128, 4096], BF16))
        sdnt = cal(nc.sbuf_tensor("sdnt", [128, 4096], BF16))
        SR = ExitStack()
        rl = SR.enter_context(nc.sbuf_tensor("rl", [16, 2048], F32))
        rt = SR.enter_context(nc.sbuf_tensor("rt", [128, 256], F32))
        comb = SR.enter_context(nc.sbuf_tensor("comb", [128, 256], F32))
        rsc = SR.enter_context(nc.sbuf_tensor("rsc", [128, 64], F32))
        eq1 = SR.enter_context(nc.sbuf_tensor("eq1", [128, 256], F32))
        eq2 = SR.enter_context(nc.sbuf_tensor("eq2", [128, 256], F32))
        eq3 = SR.enter_context(nc.sbuf_tensor("eq3", [128, 256], F32))
        c15b = SR.enter_context(nc.sbuf_tensor("c15b", [128, S], BF16))

        p.wait("gpsimd", cv, _MS["p2_v"])
        p.wait("gpsimd", cs, _MS["p2_s"])
        with ExitStack() as SCW:
            cnat = [SCW.enter_context(nc.sbuf_tensor(f"cnat{i}", [128, DM * 3],
                                                     BF16)) for i in range(NIC)]
            p3Bt = SCW.enter_context(nc.psum_tensor("p3Bt", [128, 1024], BF16))
            cw_toks = []
            for ic in range(NIC):
                cw_toks.append(GP.issue(lambda h, ic=ic: h.dma_start(
                    out=cnat[ic][:, :],
                    in_=E.cw_d.ap()[ic * 128:(ic + 1) * 128])))
            cw_toks.append(GP.issue(lambda h: h.dma_start(
                out=supt[:, :],
                in_=bass.AP(E.sup_d, 0, [[512, 128], [65536, 8], [1, 512]]))))
            cw_toks.append(GP.issue(lambda h: h.dma_start(
                out=sdnt[:, :],
                in_=bass.AP(E.sdn_d, 0, [[1024, 128], [131072, 4], [1, 1024]]))))
            rwtok = GP.issue(lambda h: h.dma_start(
                out=rwt[:, :],
                in_=bass.AP(E.rw_d, 0, [[16, 128], [2048, 8], [1, 16]])))

            p.wait("tensor", cv, _MS["p2_v"])
            p.wait("tensor", cs, _MS["p2_s"])
            for tok in cw_toks:
                GP.wait("tensor", tok)
            for k in range(3):
                for hc in range(NEC):
                    def t_cwt(h, k=k, hc=hc):
                        last = None
                        for ic in range(NIC):
                            base = cnat[ic][:, :]
                            src = bass.AP(base.tensor, hc * 384 + k,
                                          [[DM * 3, 128], [3, 128]])
                            last = h.transpose(p3Bt[:, ic * 128:(ic + 1) * 128],
                                               src, E.identb[:, :])
                        return last
                    p.op("tensor", t_cwt, ct)
                    p.wait("vector", ct, ct.n)

                    def v_cwt(h, k=k, hc=hc):
                        return h.tensor_copy(cwT[k][:, hc * 512:(hc + 1) * 512],
                                             p3Bt[:, 0:512])
                    p.op("vector", v_cwt, cv)
                    p.wait("tensor", cv, cv.n)
        cwt_done_v = cv.n

        p3F = al(nc.psum_tensor("p3F", [128, 4096], F32))
        GP.wait("tensor", rwtok)
        p.wait("tensor", cv, cwt_done_v)

        def t_router(h):
            last = None
            for nb in range(NB):
                for ec in range(NEC):
                    last = h.matmul(p3F[0:16, nb * 512:(nb + 1) * 512],
                                    rwt[:, ec * 16:(ec + 1) * 16],
                                    xs(ec, nb * 512, 512),
                                    start=(ec == 0), stop=(ec == NEC - 1))
            return last
        p.op("tensor", t_router, ct)
        p.wait("vector", ct, ct.n)

        def v_rbias(h):
            return h.tensor_scalar(rl[0:16, :], p3F[0:16, 0:2048],
                                   E.fsc[0:16, 0:1], None, op0=ALU.add)
        p.op("vector", v_rbias, cv)
        rl_done = cv.n
        _tap_sbuf(p, E, cv, rl_done, "logits", [rl])

        p.wait("tensor", cv, rl_done)

        def t_rtr(h):
            last = None
            for tc in range(NTC):
                last = h.transpose(p3F[:, 2048 + tc * 16: 2048 + (tc + 1) * 16],
                                   rl[0:16, tc * 128:(tc + 1) * 128],
                                   E.identf[0:16, 0:16])
            return last
        p.op("tensor", t_rtr, ct)
        p.wait("vector", ct, ct.n)

        def v_rt(h):
            return h.tensor_copy(rt[:, :], p3F[:, 2048:2304])
        p.op("vector", v_rt, cv)

        def s_sm(h):
            return h.activation(out=comb[:, :], in_=rt[:, :], func=AF.Exp)
        p.wait("scalar", cv, cv.n)
        p.op("scalar", s_sm, cs)
        p.wait("vector", cs, cs.n)

        def v_sum(h):
            h.reduce_sum(rsc[:, 0:16],
                         comb[:, :].rearrange("p (t e) -> p t e", e=16),
                         axis=mybir.AxisListType.X)
            h.drain()
            h.reciprocal(rsc[:, 16:32], rsc[:, 0:16])
            h.drain()
            return h.tensor_tensor(
                comb[:, :].rearrange("p (t e) -> p t e", e=16),
                comb[:, :].rearrange("p (t e) -> p t e", e=16),
                rsc[:, 16:32].unsqueeze(2).to_broadcast((128, 16, 16)),
                op=ALU.mult)
        p.op("vector", v_sum, cv)

        def v_top8(h):
            last = None
            for tc in range(NTC):
                last = h.max(out=rsc[:, 32 + 0:32 + 8] if False else
                             rt[:, tc * 16: tc * 16 + 8],
                             in_=comb[:, tc * 16:(tc + 1) * 16])
            return last
        p.raw("vector", lambda h: h.drain())
        p.op("vector", v_top8, cv)
        p.raw("vector", lambda h: h.drain())

        def v_comb(h):
            m1 = rt[:, :].rearrange("p (t e) -> p t e", e=16)[:, :, 0:1]
            m2 = rt[:, :].rearrange("p (t e) -> p t e", e=16)[:, :, 1:2]
            pr = comb[:, :].rearrange("p (t e) -> p t e", e=16)
            h.tensor_tensor(rsc[:, 32:48].rearrange("p (t o) -> p t o", o=1),
                            m1, m2, op=ALU.add)
            h.drain()
            h.tensor_scalar_add(rsc[:, 32:48], rsc[:, 32:48], 1e-6)
            h.drain()
            h.reciprocal(rsc[:, 48:64], rsc[:, 32:48])
            h.drain()
            h.tensor_tensor(rsc[:, 0:16].rearrange("p (t o) -> p t o", o=1), m1,
                            rsc[:, 48:64].rearrange("p (t o) -> p t o", o=1),
                            op=ALU.mult)
            h.tensor_tensor(rsc[:, 16:32].rearrange("p (t o) -> p t o", o=1), m2,
                            rsc[:, 48:64].rearrange("p (t o) -> p t o", o=1),
                            op=ALU.mult)
            h.drain()
            h.tensor_tensor(eq1[:, :].rearrange("p (t e) -> p t e", e=16), pr,
                            m1.to_broadcast((128, 16, 16)), op=ALU.is_equal)
            h.tensor_tensor(eq2[:, :].rearrange("p (t e) -> p t e", e=16), pr,
                            m2.to_broadcast((128, 16, 16)), op=ALU.is_equal)
            h.drain()
            h.tensor_tensor(eq3[:, :], eq2[:, :], eq1[:, :], op=ALU.mult)
            h.drain()
            h.tensor_tensor(eq2[:, :], eq2[:, :], eq3[:, :], op=ALU.subtract)
            h.drain()
            h.tensor_tensor(eq1[:, :].rearrange("p (t e) -> p t e", e=16),
                            eq1[:, :].rearrange("p (t e) -> p t e", e=16),
                            rsc[:, 0:16].unsqueeze(2).to_broadcast((128, 16, 16)),
                            op=ALU.mult)
            h.tensor_tensor(eq2[:, :].rearrange("p (t e) -> p t e", e=16),
                            eq2[:, :].rearrange("p (t e) -> p t e", e=16),
                            rsc[:, 16:32].unsqueeze(2).to_broadcast((128, 16, 16)),
                            op=ALU.mult)
            h.drain()
            return h.tensor_tensor(comb[:, :], eq1[:, :], eq2[:, :], op=ALU.add)
        p.op("vector", v_comb, cv)
        comb_done = cv.n
        _tap_sbuf(p, E, cv, comb_done, "combine", [comb])

        p.wait("tensor", cv, comb_done)

        def t_ctr(h):
            last = None
            for tc in range(NTC):
                last = h.transpose(p3F[0:16, 2048 + tc * 128: 2048 + (tc + 1) * 128],
                                   comb[:, tc * 16:(tc + 1) * 16], E.identf[:, :])
            return last
        p.op("tensor", t_ctr, ct)
        p.wait("vector", ct, ct.n)

        def v_ct(h):
            return h.tensor_copy(combT[0:16, :], p3F[0:16, 2048:4096])
        p.op("vector", v_ct, cv)
        combT_done = cv.n

        p.wait("gpsimd", cv, combT_done)

        c15tok = GP.issue(lambda h: h.dma_start(out=cstage[0:1, :],
                                                in_=combT[15:16, :]))
        GP.wait("tensor", c15tok)

        def t_c15b(h):
            last = None
            for blk in range(4):
                last = h.matmul(p3F[:, blk * 512:(blk + 1) * 512], E.ones1b[0:1, :],
                                cstage[0:1, blk * 512:(blk + 1) * 512],
                                start=True, stop=True)
            return last
        p.op("tensor", t_c15b, ct)
        p.wait("vector", ct, ct.n)

        def v_c15b(h):
            last = None
            for blk in range(4):
                last = h.tensor_copy(c15b[:, blk * 512:(blk + 1) * 512],
                                     p3F[:, blk * 512:(blk + 1) * 512])
            return last
        p.op("vector", v_c15b, cv)
        p.raw("vector", lambda h: h.drain())
        for ec in range(NEC):
            def v_acc0(h, ec=ec):
                return h.tensor_tensor(E.acc[ec][:, :], xs(ec, 0, S), c15b[:, :],
                                       op=ALU.mult)
            p.op("vector", v_acc0, cv)
        acc_init = cv.n
        SR.close()

        if True:
            p.wait("tensor", cv, acc_init)
            for ic in range(NIC):
                for nb in range(NB):
                    kk = ic * NB + nb
                    if kk >= 4:
                        p.wait("tensor", cs, _MS[f"sg_{kk - 4}"])

                    def t_conv(h, ic=ic, nb=nb, kk=kk):
                        last = None
                        first = True
                        pb = (kk % 4) * 512
                        for k in range(3):
                            for hc in range(NEC):
                                last = h.matmul(
                                    p3F[:, pb:pb + 512],
                                    cwT[k][:, hc * 512 + ic * 128:
                                           hc * 512 + ic * 128 + 128],
                                    E.xln[hc][:, nb * 512 + k: nb * 512 + k + 512],
                                    start=first, stop=(k == 2 and hc == NEC - 1))
                                first = False
                        return last
                    p.op("tensor", t_conv, ct)
                    p.wait("scalar", ct, ct.n)

                    def s_sg(h, ic=ic, nb=nb, kk=kk):
                        pb = (kk % 4) * 512
                        return h.activation(
                            out=sgate[ic][:, nb * 512:(nb + 1) * 512],
                            in_=p3F[:, pb:pb + 512], func=AF.Silu)
                    p.op("scalar", s_sg, cs)
                    _MS[f"sg_{kk}"] = cs.n
            _tap_sbuf(p, E, cs, cs.n, "sgate", sgate)

            for ic in range(NIC):
                for nb in range(NB):
                    kk = ic * NB + nb
                    if kk >= 2:
                        p.wait("tensor", cv, _MS[f"smid_{kk - 2}"])

                    def t_sup(h, ic=ic, nb=nb, kk=kk):
                        last = None
                        pb = 2048 + (kk % 2) * 512
                        for ec in range(NEC):
                            last = h.matmul(
                                p3F[:, pb:pb + 512],
                                supt[:, ec * 512 + ic * 128: ec * 512 + ic * 128 + 128],
                                xs(ec, nb * 512, 512),
                                start=(ec == 0), stop=(ec == NEC - 1))
                        return last
                    p.op("tensor", t_sup, ct)
                    p.wait("vector", ct, ct.n)
                    p.wait("vector", cs, _MS[f"sg_{kk}"])

                    def v_smid(h, ic=ic, nb=nb, kk=kk):
                        pb = 2048 + (kk % 2) * 512
                        h.tensor_copy(smid[ic][:, nb * 512:(nb + 1) * 512],
                                      p3F[:, pb:pb + 512])
                        h.drain()
                        return h.tensor_tensor(
                            smid[ic][:, nb * 512:(nb + 1) * 512],
                            smid[ic][:, nb * 512:(nb + 1) * 512],
                            sgate[ic][:, nb * 512:(nb + 1) * 512], op=ALU.mult)
                    p.op("vector", v_smid, cv)
                    _MS[f"smid_{kk}"] = cv.n
            smid_done = cv.n

            p.wait("tensor", cv, smid_done)
            p.wait("tensor", cv, acc_init)
            for fp in range(4):
                if fp > 0:
                    p.wait("tensor", cv, cv.n)

                def t_sdn(h, fp=fp):
                    last = None
                    for half in range(2):
                        fc = 2 * fp + half
                        for nb in range(NB):
                            for ic in range(NIC):
                                last = h.matmul(
                                    p3F[:, half * 2048 + nb * 512:
                                        half * 2048 + (nb + 1) * 512],
                                    sdnt[:, ic * 1024 + fc * 128:
                                         ic * 1024 + fc * 128 + 128],
                                    smid[ic][:, nb * 512:(nb + 1) * 512],
                                    start=(ic == 0), stop=(ic == NIC - 1))
                    return last
                p.op("tensor", t_sdn, ct)
                p.wait("vector", ct, ct.n)

                def v_sdn(h, fp=fp):
                    last = None
                    for half in range(2):
                        fc = 2 * fp + half
                        last = h.tensor_tensor(E.acc[fc][:, :], E.acc[fc][:, :],
                                               p3F[:, half * 2048:(half + 1) * 2048],
                                               op=ALU.add)
                    return last
                p.op("vector", v_sdn, cv)
        shared_done = cv.n
        _tap_sbuf(p, E, cv, shared_done, "acc_shared", E.acc)
        SC.close()

        with ExitStack() as SE:
            eal = SE.enter_context
            pool = [{
                "g": eal(nc.sbuf_tensor(f"egw{s_}", [128, 4096], BF16)),
                "u": eal(nc.sbuf_tensor(f"euw{s_}", [128, 4096], BF16)),
                "d": eal(nc.sbuf_tensor(f"edw{s_}", [128, 4096], BF16)),
            } for s_ in range(4)]
            z = [eal(nc.sbuf_tensor(f"z{i}", [128, 2048], BF16)) for i in range(3)]
            sgz = [eal(nc.sbuf_tensor(f"sgz{i}", [128, 2048], BF16)) for i in range(2)]
            cbc3 = [eal(nc.sbuf_tensor(f"cbc3_{i}", [128, S], BF16)) for i in range(3)]

            def load_expert(e):
                slot = pool[e % 4]
                if e >= 4:
                    p.wait("gpsimd", ct, _MS[f"edone_{e - 4}"])
                toks = [
                    GP.issue(lambda h, e=e, slot=slot: h.dma_start(
                        out=slot["g"][:, :],
                        in_=bass.AP(E.gw_d, e * DM * IDIM,
                                    [[IDIM, 128], [128 * IDIM, 8], [1, IDIM]]))),
                    GP.issue(lambda h, e=e, slot=slot: h.dma_start(
                        out=slot["u"][:, :],
                        in_=bass.AP(E.uw_d, e * DM * IDIM,
                                    [[IDIM, 128], [128 * IDIM, 8], [1, IDIM]]))),
                    GP.issue(lambda h, e=e, slot=slot: h.dma_start(
                        out=slot["d"][:, :],
                        in_=bass.AP(E.dw_d, e * IDIM * DM,
                                    [[DM, 128], [128 * DM, 4], [1, DM]]))),
                ]
                _MS[f"eload_{e}"] = toks

            def stage_cbc(e, slot, gate_cv):
                p.wait("gpsimd", cv, gate_cv)
                p.wait("gpsimd", ct, ct.n)
                tok = GP.issue(lambda h, e=e: h.dma_start(
                    out=cstage[0:1, :], in_=combT[e:e + 1, :]))
                GP.wait("tensor", tok)
                p.wait("tensor", cv, cv.n)

                def t_cb(h):
                    last = None
                    for blk in range(4):
                        last = h.matmul(p3F[:, blk * 512:(blk + 1) * 512],
                                        E.ones1b[0:1, :],
                                        cstage[0:1, blk * 512:(blk + 1) * 512],
                                        start=True, stop=True)
                    return last
                p.op("tensor", t_cb, ct)
                p.wait("vector", ct, ct.n)

                def v_cb(h, slot=slot):
                    last = None
                    for blk in range(4):
                        last = h.tensor_copy(cbc3[slot][:, blk * 512:(blk + 1) * 512],
                                             p3F[:, blk * 512:(blk + 1) * 512])
                    return last
                p.op("vector", v_cb, cv)
                _MS[f"cbc_{e}"] = cv.n

            p.wait("gpsimd", ct, ct.n)
            p.wait("gpsimd", cv, cv.n)
            p.wait("gpsimd", cs, cs.n)
            for e in range(4):
                load_expert(e)

            p.wait("tensor", cv, shared_done)
            prev_silu = cs.n
            prev_zt1 = cv.n
            prev_dn = cv.n
            zt2q = []
            for g in range(5):
                if g == 0:
                    for ei in range(3):
                        stage_cbc(3 * g + ei, ei, cv.n)
                for nb in range(NB):
                    for ei in range(3):
                        e = 3 * g + ei
                        slot = pool[e % 4]
                        for tok in _MS[f"eload_{e}"]:
                            GP.wait("tensor", tok)
                        p.wait("tensor", cs, prev_silu)
                        p.wait("tensor", cv, max(_MS[f"cbc_{3 * g + 2}"], prev_dn))

                        def t_g(h, nb=nb, slot=slot):
                            last = None
                            for ic in range(NIC):
                                for ec in range(NEC):
                                    last = h.matmul(
                                        p3F[:, ic * 512:(ic + 1) * 512],
                                        slot["g"][:, ec * 512 + ic * 128:
                                                  ec * 512 + ic * 128 + 128],
                                        xs(ec, nb * 512, 512),
                                        start=(ec == 0), stop=(ec == NEC - 1))
                            return last
                        p.op("tensor", t_g, ct)
                        tg_done = ct.n
                        p.wait("tensor", cv, prev_zt1)

                        def t_u(h, nb=nb, slot=slot):
                            last = None
                            for ic in range(NIC):
                                for ec in range(NEC):
                                    last = h.matmul(
                                        p3F[:, 2048 + ic * 512:
                                            2048 + (ic + 1) * 512],
                                        slot["u"][:, ec * 512 + ic * 128:
                                                  ec * 512 + ic * 128 + 128],
                                        xs(ec, nb * 512, 512),
                                        start=(ec == 0), stop=(ec == NEC - 1))
                            return last
                        p.op("tensor", t_u, ct)
                        tu_done = ct.n
                        p.wait("scalar", ct, tg_done)
                        sb = sgz[ei % 2]
                        if len(zt2q) >= 2:
                            p.wait("scalar", cv, zt2q[-2])

                        def s_silu(h, sb=sb):
                            return h.activation(out=sb[:, :], in_=p3F[:, 0:2048],
                                                func=AF.Silu)
                        p.op("scalar", s_silu, cs)
                        prev_silu = cs.n
                        p.wait("vector", ct, tu_done)

                        def v_z1(h, nb=nb, ei=ei):
                            zt = z[ei]
                            return h.tensor_tensor(
                                zt[:, :], p3F[:, 2048:4096],
                                cbc3[ei][:, nb * 512:(nb + 1) * 512]
                                .unsqueeze(1).to_broadcast((128, 4, 512)),
                                op=ALU.mult)
                        p.op("vector", v_z1, cv)
                        prev_zt1 = cv.n
                        p.wait("vector", cs, prev_silu)

                        def v_z2(h, ei=ei, sb=sb):
                            zt = z[ei]
                            h.drain()
                            return h.tensor_tensor(zt[:, :], zt[:, :], sb[:, :],
                                                   op=ALU.mult)
                        p.op("vector", v_z2, cv)
                        _MS[f"z_{e}_{nb}"] = cv.n
                        zt2q.append(cv.n)
                    zv = cv.n
                    p.wait("tensor", cv, zv)
                    for half4 in range(2):
                        def t_dn(h, g=g, nb=nb, half4=half4):
                            last = None
                            for fq in range(4):
                                fc = half4 * 4 + fq
                                first = True
                                for ei in range(3):
                                    e = 3 * g + ei
                                    slot = pool[e % 4]
                                    for ic in range(NIC):
                                        last = h.matmul(
                                            p3F[:, fq * 512 + half4 * 2048:
                                                fq * 512 + half4 * 2048 + 512],
                                            slot["d"][:, ic * 1024 + fc * 128:
                                                      ic * 1024 + fc * 128 + 128],
                                            z[ei][:, ic * 512:(ic + 1) * 512],
                                            start=first,
                                            stop=(ei == 2 and ic == NIC - 1))
                                        first = False
                            return last
                        p.op("tensor", t_dn, ct)
                    p.wait("vector", ct, ct.n)

                    def v_dn(h, nb=nb):
                        last = None
                        for fc in range(NEC):
                            last = h.tensor_tensor(
                                E.acc[fc][:, nb * 512:(nb + 1) * 512],
                                E.acc[fc][:, nb * 512:(nb + 1) * 512],
                                p3F[:, fc * 512:(fc + 1) * 512], op=ALU.add)
                        return last
                    p.op("vector", v_dn, cv)
                    prev_dn = cv.n
                for ei in range(3):
                    _MS[f"edone_{3 * g + ei}"] = ct.n
                if g + 1 < 5:
                    for ei in range(3):
                        stage_cbc(3 * (g + 1) + ei, ei, cv.n)
                for e_next in (3 * g + 4, 3 * g + 5, 3 * g + 6):
                    if e_next < EXP:
                        load_expert(e_next)
        _MS["p3_v"] = cv.n
        _tap_sbuf(p, E, cv, cv.n, "ffnacc", E.acc)


def _phase4(p, E):
    nc = E.nc
    ct, cv, cs, cg = E.ct, E.cv, E.cs, E.cg
    SY = E.SY
    with ExitStack() as S4:
        al = S4.enter_context
        x2t = [al(nc.sbuf_tensor(f"x2t{i}", [128, DM], BF16)) for i in range(2)]
        sqt = [al(nc.sbuf_tensor(f"sqt{i}", [128, DM], BF16)) for i in range(2)]
        otile = [al(nc.sbuf_tensor(f"otile{i}", [128, DM], F32)) for i in range(2)]
        st4 = al(nc.sbuf_tensor("st4", [128, 16], F32))
        p4B = al(nc.psum_tensor("p4B", [128, 1024], BF16))

        p.wait("vector", cv, _MS["p3_v"])
        for ec in range(NEC):
            def v_x2(h, ec=ec):
                return h.tensor_tensor(E.acc[ec][:, :], E.acc[ec][:, :],
                                       E.xln[ec][:, 2:2 + S], op=ALU.add)
            p.op("vector", v_x2, cv)
        x2_v = cv.n
        p.wait("tensor", cv, x2_v)

        for tc in range(NTC):
            b = tc % 2

            def t_tr(h, tc=tc):
                last = None
                for ec in range(NEC):
                    last = h.transpose(p4B[:, ec * 128:(ec + 1) * 128],
                                       E.acc[ec][:, tc * 128:(tc + 1) * 128],
                                       E.identb[:, :])
                return last
            if tc >= 1:
                p.wait("tensor", cv, _MS[f"p4ev_{tc - 1}"])
            p.op("tensor", t_tr, ct)
            p.wait("vector", ct, ct.n)

            def v_ev(h, b=b):
                return h.tensor_copy(x2t[b][:, :], p4B[:, 0:1024])
            p.op("vector", v_ev, cv)
            _MS[f"p4ev_{tc}"] = cv.n

            o = (tc % 2) * 8

            def v_stats(h, b=b, o=o):
                h.drain()
                h.tensor_tensor(sqt[b][:, :], x2t[b][:, :], x2t[b][:, :],
                                op=ALU.mult)
                h.reduce_sum(st4[:, o:o + 1], x2t[b][:, :],
                             axis=mybir.AxisListType.X)
                h.drain()
                h.reduce_sum(st4[:, o + 1:o + 2], sqt[b][:, :],
                             axis=mybir.AxisListType.X)
                h.drain()
                h.tensor_scalar_mul(st4[:, o + 2:o + 3], st4[:, o:o + 1], 1.0 / DM)
                h.tensor_scalar_mul(st4[:, o + 3:o + 4], st4[:, o + 1:o + 2],
                                    1.0 / DM)
                h.drain()
                h.tensor_tensor(st4[:, o + 4:o + 5], st4[:, o + 2:o + 3],
                                st4[:, o + 2:o + 3], op=ALU.mult)
                h.drain()
                h.tensor_tensor(st4[:, o + 5:o + 6], st4[:, o + 3:o + 4],
                                st4[:, o + 4:o + 5], op=ALU.subtract)
                h.drain()
                return h.tensor_scalar_add(st4[:, o + 5:o + 6],
                                           st4[:, o + 5:o + 6], EPS)
            p.op("vector", v_stats, cv)
            p.wait("scalar", cv, cv.n)

            def s_rstd(h, o=o):
                h.activation(out=st4[:, o + 6:o + 7], in_=st4[:, o + 5:o + 6],
                             func=AF.Ln)
                h.drain()
                return h.activation(out=st4[:, o + 6:o + 7],
                                    in_=st4[:, o + 6:o + 7], func=AF.Exp,
                                    scale=-0.5)
            p.op("scalar", s_rstd, cs)
            p.wait("vector", cs, cs.n)

            def v_nb(h, o=o):
                h.tensor_tensor(st4[:, o + 7:o + 8], st4[:, o + 2:o + 3],
                                st4[:, o + 6:o + 7], op=ALU.mult)
                h.drain()
                return h.tensor_scalar_mul(st4[:, o + 7:o + 8],
                                           st4[:, o + 7:o + 8], -1.0)
            p.op("vector", v_nb, cv)
            if tc >= 2:
                SY.wait("vector", _MS[f"p4dma_{tc - 2}"])

            def v_apply(h, b=b, o=o):
                h.drain()
                return h.tensor_scalar(otile[b][:, :], x2t[b][:, :],
                                       st4[:, o + 6:o + 7], st4[:, o + 7:o + 8],
                                       op0=ALU.mult, op1=ALU.add)
            p.op("vector", v_apply, cv)
            _MS[f"p4ap_{tc}"] = cv.n
            p.wait("sync", cv, cv.n)
            _MS[f"p4dma_{tc}"] = SY.issue(
                lambda h, tc=tc, b=b: h.dma_start(
                    out=E.out_d.ap()[tc * 128:(tc + 1) * 128, :],
                    in_=otile[b][:, :]))

        SY.barrier("sync")


def _dummy_out(p, E):
    nc = E.nc
    with ExitStack() as SD:
        otile = SD.enter_context(nc.sbuf_tensor("odum", [128, DM], F32))

        def v_z(h):
            return h.memset(otile[:, :], 0.0)
        p.op("vector", v_z, E.cv)
        p.wait("sync", E.cv, E.cv.n)
        for tc in range(NTC):
            E.SY.issue(lambda h, tc=tc: h.dma_start(
                out=E.out_d.ap()[tc * 128:(tc + 1) * 128, :],
                in_=otile[:, :]))
        E.SY.barrier("sync")


def kernel(**inputs):
    def f32(name):
        return np.ascontiguousarray(np.asarray(inputs[name], dtype=np.float32))

    x = f32("L0_attn_hidden_states")
    base = {
        "wq": f32("L0_attn_Wq"), "wk": f32("L0_attn_Wk"), "wv": f32("L0_attn_Wv"),
        "wo": f32("L0_attn_Wo"),
        "aq": f32("L0_attn_alpha_q"), "ak": f32("L0_attn_alpha_k"),
        "av": f32("L0_attn_alpha_v"), "co": f32("L0_attn_collapse"),
        "rw": f32("L0_ffn_router_w"), "rb": f32("L0_ffn_router_bias"),
        "gw": f32("L0_ffn_expert_gate_w"), "uw": f32("L0_ffn_expert_up_w"),
        "dw": f32("L0_ffn_expert_down_w"), "cw": f32("L0_ffn_conv_w"),
        "sup": f32("L0_ffn_shared_up_w"), "sdn": f32("L0_ffn_shared_down_w"),
    }
    nc = build_nc()
    in_maps = [dict(base, x=np.ascontiguousarray(x[i])) for i in range(NCORES)]
    res = run_bass_kernel_spmd(nc, in_maps, core_ids=list(range(NCORES)))
    return np.stack([r["out"] for r in res.results], axis=0)


# version 10 (final)
# speedup vs baseline: 1.0423x; 1.0423x over previous
import math
import numpy as np
from contextlib import ExitStack

import concourse.bass as bass
import concourse.mybir as mybir
from concourse.bass_utils import run_bass_kernel_spmd
from concourse import library_config

F32 = mybir.dt.float32
BF16 = mybir.dt.bfloat16
I32 = mybir.dt.int32
AF = mybir.ActivationFunctionType
ALU = mybir.AluOpType

NCORES = 8
S = 2048
DM = 1024
H = 16
D = 64
NEC = DM // 128
NTC = S // 128
NB = 4
EXP = 15
IDIM = 512
NIC = IDIM // 128
EPS = 1e-5

ENGINES = ("tensor", "vector", "scalar", "gpsimd", "sync")

N_HEADS = H
STOP_AFTER = 4
TAPS = set()
TAP_TENSORS = {}

_MS = {}


class Ctr:
    __slots__ = ("h", "n")

    def __init__(self, h):
        self.h = h
        self.n = 0


class Prog:
    def __init__(self, block):
        self.block = block
        self.q = {e: [] for e in ENGINES}

    def op(self, eng, make, ctr=None, k=1):
        if ctr is not None:
            ctr.n += k
            self.q[eng].append(lambda h, m=make, s=ctr.h, kk=k: m(h).then_inc(s, kk))
            return ctr.n
        self.q[eng].append(lambda h, m=make: m(h))
        return None

    def wait(self, eng, ctr, val):
        if val > ctr.n:
            raise RuntimeError(f"wait on future count {val} > {ctr.n} ({eng})")
        self.q[eng].append(lambda h, s=ctr.h, v=val: h.wait_ge(s, v))

    def raw(self, eng, fn):
        self.q[eng].append(fn)

    def flush(self):
        for e in ENGINES:
            ops = self.q[e]
            if not ops:
                continue
            self.q[e] = []
            deco = getattr(self.block, e)

            def body(h, ops=ops):
                for fn in ops:
                    fn(h)
            deco(body)


class DmaPool:

    def __init__(self, p, eng, ctrs):
        self.p = p
        self.eng = eng
        self.ctrs = ctrs
        self.i = 0

    def issue(self, maker):
        c = self.ctrs[self.i % len(self.ctrs)]
        self.i += 1
        if c.n > 0:
            self.p.wait(self.eng, c, c.n)
        self.p.op(self.eng, maker, c, 16)
        return (c, c.n)

    def wait(self, eng, tok):
        self.p.wait(eng, tok[0], tok[1])

    def barrier(self, eng):
        for c in self.ctrs:
            if c.n > 0:
                self.p.wait(eng, c, c.n)


class Env:
    pass


def _bc2(ap, n):
    return ap.unsqueeze(2).to_broadcast((2, 2, n))


def _tap_sbuf(p, E, ctr, val, name, tiles):
    if name not in TAPS:
        return
    if not isinstance(tiles, (list, tuple)):
        tiles = [tiles]
    if name not in TAP_TENSORS:
        ap0 = tiles[0][:, :]
        shape = [len(tiles)] + list(ap0.shape) if len(tiles) > 1 else list(ap0.shape)
        TAP_TENSORS[name] = E.nc.dram_tensor("dbg_" + name, shape, ap0.dtype,
                                             kind="ExternalOutput")
    dram = TAP_TENSORS[name]
    p.wait("sync", ctr, val)
    toks = []
    for i, t in enumerate(tiles):
        toks.append(E.SY.issue(lambda h, i=i, t=t: h.dma_start(
            out=dram.ap()[i] if len(tiles) > 1 else dram.ap(),
            in_=t[:, :])))
    for eng in ("tensor", "vector", "scalar", "gpsimd"):
        for tok in toks:
            E.SY.wait(eng, tok)


def build_nc():
    TAP_TENSORS.clear()
    _MS.clear()
    nc = bass.Bass("TRN2", target_bir_lowering=False)
    E = Env()
    E.nc = nc

    E.xin = nc.dram_tensor("x", [S, DM], F32, kind="ExternalInput")
    E.w_d = {
        "q": nc.dram_tensor("wq", [DM, DM], F32, kind="ExternalInput"),
        "k": nc.dram_tensor("wk", [DM, DM], F32, kind="ExternalInput"),
        "v": nc.dram_tensor("wv", [DM, DM], F32, kind="ExternalInput"),
    }
    E.wo_d = nc.dram_tensor("wo", [DM, DM], F32, kind="ExternalInput")
    E.al_d = [nc.dram_tensor(n, [H, H, 2], F32, kind="ExternalInput")
              for n in ("aq", "ak", "av")]
    E.co_d = nc.dram_tensor("co", [H, H * 2], F32, kind="ExternalInput")
    E.rw_d = nc.dram_tensor("rw", [DM, 16], F32, kind="ExternalInput")
    E.rb_d = nc.dram_tensor("rb", [16], F32, kind="ExternalInput")
    E.gw_d = nc.dram_tensor("gw", [EXP, DM, IDIM], F32, kind="ExternalInput")
    E.uw_d = nc.dram_tensor("uw", [EXP, DM, IDIM], F32, kind="ExternalInput")
    E.dw_d = nc.dram_tensor("dw", [EXP, IDIM, DM], F32, kind="ExternalInput")
    E.cw_d = nc.dram_tensor("cw", [IDIM, DM, 3], F32, kind="ExternalInput")
    E.sup_d = nc.dram_tensor("sup", [DM, IDIM], F32, kind="ExternalInput")
    E.sdn_d = nc.dram_tensor("sdn", [IDIM, DM], F32, kind="ExternalInput")
    E.out_d = nc.dram_tensor("out", [S, DM], F32, kind="ExternalOutput")
    E.wp_d = nc.dram_tensor("wp_scratch", [3, H, 128, DM], BF16, kind="Internal")
    E.wop_d = nc.dram_tensor("wop_scratch", [H, 128, DM], BF16, kind="Internal")

    def tap(name, shape, dtype=F32):
        if name in TAPS:
            TAP_TENSORS[name] = nc.dram_tensor("dbg_" + name, list(shape), dtype,
                                               kind="ExternalOutput")
    E.tap = tap

    with ExitStack() as G:
        al = G.enter_context
        E.ct = Ctr(al(nc.semaphore("c_t")))
        E.cv = Ctr(al(nc.semaphore("c_v")))
        E.cs = Ctr(al(nc.semaphore("c_s")))
        E.cg = Ctr(al(nc.semaphore("c_g")))

        E.xln = [al(nc.sbuf_tensor(f"xln{i}", [128, S + 2], BF16)) for i in range(NEC)]
        E.identb = al(nc.sbuf_tensor("identb", [128, 128], BF16))
        E.identf = al(nc.sbuf_tensor("identf", [128, 128], F32))
        E.onesb = al(nc.sbuf_tensor("onesb", [128, 2], BF16))
        E.ones1b = al(nc.sbuf_tensor("ones1b", [1, 128], BF16))
        E.oc64 = al(nc.sbuf_tensor("oc64", [128, 66], BF16))
        E.ones1f = al(nc.sbuf_tensor("ones1f", [1, 128], F32))
        E.fsc = al(nc.sbuf_tensor("fsc", [128, 64], F32))

        with nc.Block() as block:
            p = Prog(block)
            E.SY = DmaPool(p, "sync",
                           [Ctr(al(nc.semaphore(f"dsy{i}"))) for i in range(3)])
            E.GP = DmaPool(p, "gpsimd",
                           [Ctr(al(nc.semaphore(f"dgp{i}"))) for i in range(3)])

            with ExitStack() as S02:
                sal = S02.enter_context
                E.xbf = [sal(nc.sbuf_tensor(f"xbf{i}", [128, S], BF16))
                         for i in range(NEC)]
                E.selc = sal(nc.sbuf_tensor("selc", [128, 128], F32))
                E.attnT = [sal(nc.sbuf_tensor(f"attnT{i}", [128, S], BF16))
                           for i in range(H)]

                with ExitStack() as S01:
                    E.masks = S01.enter_context(
                        nc.sbuf_tensor("masks", [128, 4 * 2048], BF16))
                    E.amix = S01.enter_context(
                        nc.sbuf_tensor("amix", [2, 3 * 256], BF16))
                    E.sel2 = S01.enter_context(
                        nc.sbuf_tensor("sel2", [2, 128], BF16))
                    E.diag64 = S01.enter_context(
                        nc.sbuf_tensor("diag64", [128, 128], BF16))

                    _phase0(p, E)
                    p.flush()

                    if STOP_AFTER >= 1:
                        _phase1(p, E)
                        p.flush()

                if STOP_AFTER >= 2:
                    _phase2(p, E)
                    p.flush()

            if STOP_AFTER >= 3:
                with ExitStack() as S34:
                    E.acc = [S34.enter_context(
                        nc.sbuf_tensor(f"acc{i}", [128, S], BF16))
                        for i in range(NEC)]
                    _phase3(p, E)
                    p.flush()
                    _phase4(p, E)
                    p.flush()
            else:
                _dummy_out(p, E)
                p.flush()

    return nc


def _phase0(p, E):
    nc = E.nc
    ct, cv, cs, cg = E.ct, E.cv, E.cs, E.cg
    SY, GP = E.SY, E.GP
    with ExitStack() as S0:
        al = S0.enter_context
        xint = [al(nc.sbuf_tensor(f"xint{i}", [128, DM], F32)) for i in range(2)]
        wtmp = al(nc.sbuf_tensor("wtmp", [128, DM], BF16))
        wstg = [al(nc.sbuf_tensor(f"wstg{i}", [128, DM], BF16)) for i in range(2)]
        stile = [al(nc.sbuf_tensor(f"stile{i}", [128, 128], BF16))
                 for i in range(16)]
        co2 = al(nc.sbuf_tensor("co2", [2, 256], BF16))
        psA = al(nc.psum_tensor("p0A", [128, 1024], F32))
        psV = al(nc.psum_tensor("p0V", [128, 1024], F32))
        psT = al(nc.psum_tensor("p0T", [128, 1024], BF16))

        with ExitStack() as Sio:
            iosc = Sio.enter_context(nc.sbuf_tensor("iosc", [128, 2560], I32))

            def g_setup(h):
                h.iota(iosc[:, 0:128], [[-1, 128]], base=0, channel_multiplier=1)
                h.iota(iosc[:, 128:256], [[-1, 128]], base=128, channel_multiplier=1)
                h.iota(iosc[0:2, 2304:2432], [[1, 128]], base=0,
                       channel_multiplier=-64)
                h.iota(iosc[64:66, 2304:2432], [[1, 128]], base=0,
                       channel_multiplier=-64)
                for j in range(4):
                    h.iota(iosc[:, 256 + j * 512: 256 + (j + 1) * 512],
                           [[1, 512]], base=-128 * j, channel_multiplier=-1)
                h.memset(E.onesb[:, :], 1.0)
                h.memset(E.oc64[:, :], 0.0)
                h.memset(E.ones1b[0:1, :], 1.0)
                h.memset(E.ones1f[0:1, :], 1.0)
                h.memset(E.sel2[:, :], 0.0)
                h.memset(E.selc[:, :], 0.0)
                h.memset(E.fsc[:, :], 0.0)
                return h.drain()
            p.op("gpsimd", g_setup, cg)

            def g_setup2(h):
                h.memset(E.oc64[:, 1:2], 1.0)
                for ec in range(NEC):
                    h.memset(E.xln[ec][:, 0:2], 0.0)
                return h.drain()
            p.op("gpsimd", g_setup2, cg)
            g_consts = cg.n

            for idx, t in enumerate(E.al_d):
                GP.issue(lambda h, t=t, idx=idx: h.dma_start(
                    out=E.amix[0:2, idx * 256:(idx + 1) * 256],
                    in_=bass.AP(t, 0, [[32, 2], [64, 8], [1, 32]])))
            GP.issue(lambda h: h.dma_start(
                out=co2[0:2, :],
                in_=bass.AP(E.co_d, 0, [[32, 2], [64, 8], [1, 32]])))
            GP.issue(lambda h: h.dma_start(
                out=E.fsc[0:16, 0:1],
                in_=bass.AP(E.rb_d, 0, [[1, 16], [1, 1]])))
            gp_small = [(c, c.n) for c in GP.ctrs]

            p.wait("vector", cg, g_consts)

            def v_setup(h):
                h.tensor_scalar(E.identb[:, :], iosc[:, 0:128], 0, None,
                                op0=ALU.is_equal)
                h.tensor_scalar(E.identf[:, :], iosc[:, 0:128], 0, None,
                                op0=ALU.is_equal)
                h.tensor_scalar(iosc[0:2, 2432:2560], iosc[0:2, 2304:2432], 64,
                                None, op0=ALU.is_lt)
                h.tensor_scalar(iosc[0:2, 2304:2432], iosc[0:2, 2304:2432], 0,
                                None, op0=ALU.is_ge)
                h.tensor_scalar(iosc[64:66, 2432:2560], iosc[64:66, 2304:2432],
                                64, None, op0=ALU.is_lt)
                h.tensor_scalar(iosc[64:66, 2304:2432], iosc[64:66, 2304:2432],
                                0, None, op0=ALU.is_ge)
                return h.drain()
            p.op("vector", v_setup, cv)

            def v_setup2(h):
                h.tensor_tensor(E.sel2[0:2, :], iosc[0:2, 2304:2432],
                                iosc[0:2, 2432:2560], op=ALU.mult)
                h.tensor_tensor(E.selc[64:66, :], iosc[64:66, 2304:2432],
                                iosc[64:66, 2432:2560], op=ALU.mult)
                h.tensor_scalar(E.diag64[:, :], iosc[:, 128:256], 128, None,
                                op0=ALU.is_equal)
                h.tensor_scalar(stile[0][:, :], iosc[:, 128:256], 64, None,
                                op0=ALU.is_equal)
                h.tensor_scalar(stile[1][:, :], iosc[:, 128:256], 192, None,
                                op0=ALU.is_equal)
                h.drain()
                h.tensor_tensor(E.diag64[:, :], E.diag64[:, :], stile[0][:, :],
                                op=ALU.add)
                h.drain()
                h.tensor_tensor(E.diag64[:, :], E.diag64[:, :], stile[1][:, :],
                                op=ALU.add)
                h.drain()
                last = None
                for j in range(4):
                    srcm = iosc[:, 256 + j * 512: 256 + (j + 1) * 512]
                    for pl in range(4):
                        thr = 1 if pl == 3 else 0
                        last = h.tensor_scalar(
                            E.masks[:, j * 2048 + pl * 512:
                                    j * 2048 + (pl + 1) * 512],
                            srcm, thr, None, op0=ALU.is_ge)
                return last
            p.op("vector", v_setup2, cv)
            v_consts = cv.n

            for c, v in gp_small:
                p.wait("vector", c, v)

            def v_alpha(h):
                return h.tensor_scalar_mul(E.amix[0:2, 0:256], E.amix[0:2, 0:256],
                                           1.0 / math.sqrt(D))
            p.op("vector", v_alpha, cv)
            v_alpha_done = cv.n

            xtr_t0 = ct.n
            xtr_v0 = cv.n
            p.wait("tensor", cv, v_consts)
            for tc in range(NTC):
                buf = tc % 2
                if tc >= 2:
                    p.wait("sync", ct, xtr_t0 + (tc - 1))

                xtok = SY.issue(lambda h, tc=tc, buf=buf: h.dma_start(
                    out=xint[buf][:, :],
                    in_=E.xin.ap()[tc * 128:(tc + 1) * 128, :]))
                SY.wait("tensor", xtok)
                if tc >= 1:
                    p.wait("tensor", cv, xtr_v0 + (tc - 1) * 2 + 2)

                def t_xtr(h, buf=buf):
                    last = None
                    for ec in range(NEC):
                        half = ec // 4
                        last = h.transpose(
                            psA[:, half * 512 + (ec % 4) * 128:
                                half * 512 + (ec % 4) * 128 + 128],
                            xint[buf][:, ec * 128:(ec + 1) * 128], E.identf[:, :])
                    return last
                p.op("tensor", t_xtr, ct)

                p.wait("vector", ct, ct.n)
                for half in range(2):
                    def v_xe(h, tc=tc, half=half):
                        last = None
                        for k4 in range(4):
                            ec = half * 4 + k4
                            last = h.tensor_copy(
                                E.xbf[ec][:, tc * 128:(tc + 1) * 128],
                                psA[:, half * 512 + k4 * 128:
                                    half * 512 + (k4 + 1) * 128])
                        return last
                    p.op("vector", v_xe, cv)
            _MS["x_done_v"] = cv.n
            _tap_sbuf(p, E, cv, cv.n, "xbf", E.xbf)
        iosc_free_v = cv.n

        with ExitStack() as Swo:
            wal = Swo.enter_context
            wonat = [wal(nc.sbuf_tensor(f"wonat{i}", [128, DM], BF16))
                     for i in range(NEC)]
            wot = [wal(nc.sbuf_tensor(f"wot{i}", [128, DM], BF16))
                   for i in range(NEC)]
            p.wait("gpsimd", cv, iosc_free_v)
            wo_tok = {}
            for fc in range(NEC):
                wo_tok[fc] = GP.issue(lambda h, fc=fc: h.dma_start(
                    out=wonat[fc][:, :],
                    in_=E.wo_d.ap()[fc * 128:(fc + 1) * 128, :]))

            for fc in range(NEC):
                GP.wait("tensor", wo_tok[fc])
            for hc in range(NEC):
                def t_wot(h, hc=hc):
                    last = None
                    for fc in range(NEC):
                        last = h.transpose(psT[:, fc * 128:(fc + 1) * 128],
                                           wonat[fc][:, hc * 128:(hc + 1) * 128],
                                           E.identb[:, :])
                    return last
                p.op("tensor", t_wot, ct)
                p.wait("vector", ct, ct.n)

                def v_wot(h, hc=hc):
                    return h.tensor_copy(wot[hc][:, :], psT[:, :])
                p.op("vector", v_wot, cv)
                p.wait("tensor", cv, cv.n)

            p.wait("tensor", cg, g_consts)
            viter_done = {-1: cv.n}
            for i in range(H + 1):
                p.wait("tensor", cv, viter_done[i - 1])
                vbc_done = kr_done = None
                if i < H:
                    def t_vbc8(h, c=i):
                        last = None
                        for mc in range(NEC):
                            last = h.matmul(psV[:, mc * 128:(mc + 1) * 128],
                                            E.sel2[0:2, :],
                                            _bc2(co2[0:2, mc * 32 + 2 * c:
                                                     mc * 32 + 2 * c + 2], 64),
                                            start=True, stop=True)
                        return last
                    p.op("tensor", t_vbc8, ct)
                    vbc_done = ct.n
                if i - 1 >= 0:
                    def t_kr8(h, sb=((i - 1) % 2) * 8):
                        last = None
                        for mc in range(NEC):
                            h.matmul(psA[:, 0:512], stile[sb + mc][:, :],
                                     wot[mc][:, 0:512],
                                     start=(mc == 0), stop=(mc == NEC - 1))
                            last = h.matmul(psA[:, 512:1024],
                                            stile[sb + mc][:, :],
                                            wot[mc][:, 512:1024],
                                            start=(mc == 0), stop=(mc == NEC - 1))
                        return last
                    p.op("tensor", t_kr8, ct)
                    kr_done = ct.n

                if i < H:
                    p.wait("vector", ct, vbc_done)

                    def v_st8(h, sb=(i % 2) * 8):
                        last = None
                        for mc in range(NEC):
                            last = h.tensor_tensor(stile[sb + mc][:, :],
                                                   E.diag64[:, :],
                                                   psV[:, mc * 128:(mc + 1) * 128],
                                                   op=ALU.mult)
                        return last
                    p.op("vector", v_st8, cv)
                if i - 1 >= 0:
                    c = i - 1
                    p.wait("vector", ct, kr_done)
                    stg = wstg[c % 2]
                    if c >= 2:
                        SY.wait("vector", _MS[f"wo_c{c - 2}_dma"])

                    def v_ev(h, stg=stg):
                        h.tensor_copy(stg[:, 0:512], psA[:, 0:512])
                        return h.tensor_copy(stg[:, 512:1024], psA[:, 512:1024])
                    p.op("vector", v_ev, cv)
                    p.wait("sync", cv, cv.n)
                    _MS[f"wo_c{c}_dma"] = SY.issue(
                        lambda h, c=c, stg=stg: h.dma_start(out=E.wop_d.ap()[c],
                                                            in_=stg[:, :]))
                viter_done[i] = cv.n
        wo_free_t = ct.n

        prev_dma = {}
        for ti, t in enumerate("qkv"):
            with ExitStack() as Swt:
                wnat = [Swt.enter_context(
                    nc.sbuf_tensor(f"wnat_{t}{i}", [128, DM], BF16))
                    for i in range(NEC)]
                p.wait("gpsimd", ct, wo_free_t if ti == 0 else _MS[f"wprep_t_{ti - 1}"])
                wn_tok = {}
                for mc in range(NEC):
                    wn_tok[mc] = GP.issue(lambda h, t=t, mc=mc: h.dma_start(
                        out=wnat[mc][:, :],
                        in_=E.w_d[t].ap()[mc * 128:(mc + 1) * 128, :]))
                for mc in range(NEC):
                    GP.wait("tensor", wn_tok[mc])

                viter_done = {-1: cv.n}
                for i in range(H + 2):
                    p.wait("tensor", cv, viter_done[i - 1])
                    vbc_done = kr_done = tr_done = None
                    if i < H:
                        def t_vbc8(h, ti=ti, hd=i):
                            last = None
                            for mc in range(NEC):
                                last = h.matmul(
                                    psV[:, mc * 128:(mc + 1) * 128],
                                    E.sel2[0:2, :],
                                    _bc2(E.amix[0:2,
                                                ti * 256 + mc * 32 + 2 * hd:
                                                ti * 256 + mc * 32 + 2 * hd + 2],
                                         64),
                                    start=True, stop=True)
                            return last
                        p.op("tensor", t_vbc8, ct)
                        vbc_done = ct.n
                    if 0 <= i - 2 < H:
                        def t_tr(h):
                            last = None
                            for ec in range(NEC):
                                last = h.transpose(
                                    psT[:, ec * 128:(ec + 1) * 128],
                                    wtmp[:, ec * 128:(ec + 1) * 128],
                                    E.identb[:, :])
                            return last
                        p.op("tensor", t_tr, ct)
                        tr_done = ct.n
                    if 0 <= i - 1 < H:
                        def t_kr8(h, sb=((i - 1) % 2) * 8):
                            last = None
                            for mc in range(NEC):
                                h.matmul(psA[:, 0:512], stile[sb + mc][:, :],
                                         wnat[mc][:, 0:512],
                                         start=(mc == 0), stop=(mc == NEC - 1))
                                last = h.matmul(psA[:, 512:1024],
                                                stile[sb + mc][:, :],
                                                wnat[mc][:, 512:1024],
                                                start=(mc == 0),
                                                stop=(mc == NEC - 1))
                            return last
                        p.op("tensor", t_kr8, ct)
                        kr_done = ct.n

                    if i < H:
                        p.wait("vector", ct, vbc_done)

                        def v_st8(h, sb=(i % 2) * 8):
                            last = None
                            for mc in range(NEC):
                                last = h.tensor_tensor(
                                    stile[sb + mc][:, :], E.diag64[:, :],
                                    psV[:, mc * 128:(mc + 1) * 128],
                                    op=ALU.mult)
                            return last
                        p.op("vector", v_st8, cv)
                    if 0 <= i - 2 < H:
                        hd = i - 2
                        p.wait("vector", ct, tr_done)
                        stg = wstg[hd % 2]
                        k2 = (ti * H + hd) - 2
                        if k2 >= 0:
                            SY.wait("vector", prev_dma[k2])
                        else:
                            SY.wait("vector", _MS[f"wo_c{H + k2}_dma"])

                        def v_stg(h, stg=stg):
                            return h.tensor_copy(stg[:, :], psT[:, :])
                        p.op("vector", v_stg, cv)
                        p.wait("sync", cv, cv.n)
                        tok = SY.issue(
                            lambda h, ti=ti, hd=hd, stg=stg: h.dma_start(
                                out=E.wp_d.ap()[ti, hd], in_=stg[:, :]))
                        prev_dma[ti * H + hd] = tok
                        _MS[f"w{t}_h{hd}_dma"] = tok
                    if 0 <= i - 1 < H:
                        p.wait("vector", ct, kr_done)

                        def v_ev(h):
                            h.tensor_copy(wtmp[:, 0:512], psA[:, 0:512])
                            return h.tensor_copy(wtmp[:, 512:1024],
                                                 psA[:, 512:1024])
                        p.op("vector", v_ev, cv)
                    viter_done[i] = cv.n
                _MS[f"wprep_t_{ti}"] = ct.n

        _MS["p0_v"] = cv.n
        _MS["p0_t"] = ct.n


def _phase1(p, E):
    nc = E.nc
    ct, cv, cs, cg = E.ct, E.cv, E.cs, E.cg
    SY, GP = E.SY, E.GP
    avd = {}
    E.pend_tail = None
    E.attn_prev = None
    with ExitStack() as S1:
        al = S1.enter_context
        wsl = [[al(nc.sbuf_tensor(f"wsl{t}{b}", [128, DM], BF16)) for b in range(2)]
               for t in range(3)]
        qh = al(nc.sbuf_tensor("qh", [128, S], BF16))
        kh = al(nc.sbuf_tensor("kh", [128, S], BF16))
        vh = al(nc.sbuf_tensor("vh", [128, S], BF16))
        qsw = al(nc.sbuf_tensor("qsw", [128, S], BF16))
        vtok = al(nc.sbuf_tensor("vtok", [128, S], BF16))
        probs = al(nc.sbuf_tensor("probs", [128, 3 * 2048], BF16))
        ccv = al(nc.sbuf_tensor("ccv", [128, 512], F32))
        cinv = al(nc.sbuf_tensor("cinv", [128, 512], F32))
        paF = al(nc.psum_tensor("p1F", [128, 3584], F32))
        paB = al(nc.psum_tensor("p1B", [128, 1024], BF16))

        p.wait("vector", cv, _MS["p0_v"])

        def v_cc0(h):
            return h.memset(ccv[:, :], 1.0)
        p.op("vector", v_cc0, cv)

        tgt = {0: qh, 1: kh, 2: vh}
        for hd in range(N_HEADS):
            buf = hd % 2
            wsl_toks = []
            for ti in range(3):
                SY.wait("sync", _MS[f"w{'qkv'[ti]}_h{hd}_dma"])
                if hd >= 2:
                    p.wait("sync", ct, _MS[f"proj_t_h{hd - 2}"])
                wsl_toks.append(SY.issue(lambda h, ti=ti, hd=hd, buf=buf: h.dma_start(
                    out=wsl[ti][buf][:, :], in_=E.wp_d.ap()[ti, hd])))

            for tok in wsl_toks:
                SY.wait("tensor", tok)
            pevq = []
            for ti in range(3):
                dst = tgt[ti]
                for blk in range(NB):
                    pslot = 1024 + (blk % 2) * 512
                    if len(pevq) >= 2:
                        p.wait("tensor", cv, pevq[-2])
                    else:
                        p.wait("tensor", cv, cv.n)

                    def t_proj(h, ti=ti, blk=blk, pslot=pslot, buf=buf):
                        last = None
                        for ec in range(NEC):
                            last = h.matmul(paF[:, pslot:pslot + 512],
                                            wsl[ti][buf][:, ec * 128:(ec + 1) * 128],
                                            E.xbf[ec][:, blk * 512:(blk + 1) * 512],
                                            start=(ec == 0), stop=(ec == NEC - 1))
                        return last
                    p.op("tensor", t_proj, ct)
                    p.wait("vector", ct, ct.n)

                    def v_proj(h, dst=dst, blk=blk, pslot=pslot):
                        return h.tensor_copy(dst[:, blk * 512:(blk + 1) * 512],
                                             paF[:, pslot:pslot + 512])
                    p.op("vector", v_proj, cv)
                    pevq.append(cv.n)
            _MS[f"proj_t_h{hd}"] = ct.n
            proj_v = cv.n
            p.wait("tensor", cv, proj_v)

            p.wait("gpsimd", cv, proj_v)
            sw1 = GP.issue(lambda h: h.dma_start(out=qsw[0:64, :],
                                                 in_=qh[64:128, :]))
            sw2 = GP.issue(lambda h: h.dma_start(out=qsw[64:128, :],
                                                 in_=qh[0:64, :]))

            for grp in range(4):
                def t_vtr(h, grp=grp):
                    last = None
                    for c4 in range(4):
                        c = grp * 4 + c4
                        last = h.transpose(paB[:, c4 * 128:(c4 + 1) * 128],
                                           vh[:, c * 128:(c + 1) * 128],
                                           E.identb[:, :])
                    return last
                p.op("tensor", t_vtr, ct)
                p.wait("vector", ct, ct.n)

                def v_vtr(h, grp=grp):
                    return h.tensor_copy(vtok[:, grp * 512:(grp + 1) * 512],
                                         paB[:, 0:512])
                p.op("vector", v_vtr, cv)
                p.wait("tensor", cv, cv.n)
            vtok_v = cv.n

            if hd == 0 and "qkvh0" in TAPS:
                GP.wait("sync", sw1)
                GP.wait("sync", sw2)
                _tap_sbuf(p, E, cv, vtok_v, "qkvh0", [qh, kh, vh, qsw, vtok])

            GP.wait("tensor", sw1)
            GP.wait("tensor", sw2)
            p.wait("scalar", cv, vtok_v)

            def emit_tail():
                if E.pend_tail is None:
                    return
                t_nb, t_hd, av_last = E.pend_tail
                E.pend_tail = None
                p.wait("vector", ct, av_last)

                def v_recip(h):
                    return h.reciprocal(ccv[64:66, :], paF[64:66, 3072:3584])
                p.op("vector", v_recip, cv)
                p.wait("tensor", cv, cv.n)

                def t_cinv(h):
                    return h.matmul(paF[:, 3072:3584], E.selc[:, :], ccv[:, :],
                                    start=True, stop=True)
                p.op("tensor", t_cinv, ct)
                p.wait("vector", ct, ct.n)

                def v_attn(h, nb=t_nb, hd=t_hd):
                    h.tensor_copy(cinv[:, :], paF[:, 3072:3584])
                    h.drain()
                    h.tensor_tensor(E.attnT[hd][0:64, nb * 512:(nb + 1) * 512],
                                    paF[0:64, 2048:2560], cinv[0:64, :],
                                    op=ALU.mult)
                    return h.tensor_tensor(
                        E.attnT[hd][64:128, nb * 512:(nb + 1) * 512],
                        paF[64:128, 2560:3072], cinv[64:128, :], op=ALU.mult)
                p.op("vector", v_attn, cv)
                E.attn_prev = cv.n

            for nb in range(NB):
                nkc = 4 * nb + 4
                ready = {}

                def emit_av(c, nkc=nkc):
                    slot = c % 3
                    rc, rv = ready[c]
                    p.wait("tensor", rc, rv)
                    if c == 0 and E.attn_prev is not None:
                        p.wait("tensor", cv, E.attn_prev)

                    def t_av(h, c=c, nkc=nkc, slot=slot):
                        last = None
                        for pp in range(2):
                            st = (c == 0) and (pp == 0)
                            sp = (c == nkc - 1) and (pp == 1)
                            vs = vtok[:, c * 128 + pp * 64: c * 128 + pp * 64 + 64]
                            o0 = slot * 2048 + (0 if pp == 0 else 3) * 512
                            mv0 = probs[:, o0:o0 + 512]
                            h.matmul(paF[0:64, 2048:2560], vs, mv0,
                                     start=st, stop=sp, tile_position=(0, 0))
                            h.matmul(paF[64:128, 3072:3584], E.oc64[:, 1:65], mv0,
                                     start=st, stop=(c == nkc - 1) and (pp == 1)
                                     and False, tile_position=(0, 64))
                            o1 = slot * 2048 + (2 if pp == 0 else 1) * 512
                            mv1 = probs[:, o1:o1 + 512]
                            h.matmul(paF[64:128, 2560:3072], vs, mv1,
                                     start=st, stop=sp, tile_position=(0, 64))
                            last = h.matmul(paF[64:128, 3072:3584],
                                            E.oc64[:, 0:64], mv1,
                                            start=False, stop=sp,
                                            tile_position=(0, 64))
                        return last
                    p.op("tensor", t_av, ct)
                    avd[(hd, nb, c)] = ct.n

                eh1 = {}
                eh2 = {}
                for c in range(nkc):
                    slot = c % 3
                    if c > 0:
                        p.wait("tensor", cs, eh1[c - 1])

                    def t_sc1(h, c=c, nb=nb):
                        h.matmul(paF[:, 0:512], kh[0:64, c * 128:(c + 1) * 128],
                                 qh[0:64, nb * 512:(nb + 1) * 512],
                                 start=True, stop=True, tile_position=(0, 0))
                        return h.matmul(paF[:, 512:1024],
                                        kh[64:128, c * 128:(c + 1) * 128],
                                        qh[64:128, nb * 512:(nb + 1) * 512],
                                        start=True, stop=True,
                                        tile_position=(64, 0))
                    p.op("tensor", t_sc1, ct)
                    sc1_t = ct.n

                    if c > 0:
                        p.wait("tensor", cs, eh2[c - 1])

                    def t_sc2(h, c=c, nb=nb):
                        h.matmul(paF[:, 1024:1536], kh[0:64, c * 128:(c + 1) * 128],
                                 qsw[0:64, nb * 512:(nb + 1) * 512],
                                 start=True, stop=True, tile_position=(0, 0))
                        return h.matmul(paF[:, 1536:2048],
                                        kh[64:128, c * 128:(c + 1) * 128],
                                        qsw[64:128, nb * 512:(nb + 1) * 512],
                                        start=True, stop=True,
                                        tile_position=(64, 0))
                    p.op("tensor", t_sc2, ct)
                    sc2_t = ct.n

                    p.wait("scalar", ct, sc1_t)
                    if c >= 3:
                        p.wait("scalar", ct, avd[(hd, nb, c - 3)])

                    def s_exp1(h, slot=slot):
                        return h.activation(
                            out=probs[:, slot * 2048: slot * 2048 + 1024],
                            in_=paF[:, 0:1024], func=AF.Exp)
                    p.op("scalar", s_exp1, cs)
                    eh1[c] = cs.n

                    p.wait("scalar", ct, sc2_t)

                    def s_exp2(h, slot=slot):
                        return h.activation(
                            out=probs[:, slot * 2048 + 1024: slot * 2048 + 2048],
                            in_=paF[:, 1024:2048], func=AF.Exp)
                    p.op("scalar", s_exp2, cs)
                    eh2[c] = cs.n

                    j = c - 4 * nb
                    if j >= 0:
                        p.wait("vector", cs, eh2[c])

                        def v_mask(h, slot=slot, j=j):
                            return h.tensor_tensor(
                                probs[:, slot * 2048:(slot + 1) * 2048],
                                probs[:, slot * 2048:(slot + 1) * 2048],
                                E.masks[:, j * 2048:(j + 1) * 2048], op=ALU.mult)
                        p.op("vector", v_mask, cv)
                        ready[c] = (cv, cv.n)
                    else:
                        ready[c] = (cs, eh2[c])

                    if c == 1:
                        emit_tail()
                    if c > 0:
                        emit_av(c - 1)
                emit_av(nkc - 1)
                E.pend_tail = (nb, hd, avd[(hd, nb, nkc - 1)])
            emit_tail()

            if hd == 0:
                _tap_sbuf(p, E, cv, cv.n, "attnh0", [E.attnT[0]])

        _MS["p1_v"] = cv.n
        _MS["p1_t"] = ct.n


def _phase2(p, E):
    nc = E.nc
    ct, cv, cs, cg = E.ct, E.cv, E.cs, E.cg
    SY, GP = E.SY, E.GP
    with ExitStack() as S2:
        al = S2.enter_context
        wos = [al(nc.sbuf_tensor(f"wos{i}", [128, DM], BF16)) for i in range(2)]
        x1 = [al(nc.sbuf_tensor(f"x1_{i}", [128, S], BF16)) for i in range(NEC)]
        x1s = [al(nc.sbuf_tensor(f"x1s{i}", [128, S], BF16)) for i in range(2)]
        stat = al(nc.sbuf_tensor("stat", [1, 2 * S], F32))
        mbc = al(nc.sbuf_tensor("mbc", [128, S], F32))
        rbc = al(nc.sbuf_tensor("rbc", [128, S], F32))
        p2F = al(nc.psum_tensor("p2F", [128, 4096], F32))
        p2S = p2F

        p.wait("vector", cv, _MS["p1_v"])
        p.wait("tensor", cv, _MS["p1_v"])
        p.wait("sync", cv, _MS["p1_v"])
        for fp in range(4):
            for oc in range(H):
                b = oc % 2
                if fp * H + oc >= 2:
                    k2 = fp * H + oc - 2
                    p.wait("sync", ct, _MS[f"wo_mm_{k2 // H}_{k2 % H}"])

                wtok = SY.issue(lambda h, oc=oc, b=b: h.dma_start(
                    out=wos[b][:, :], in_=E.wop_d.ap()[oc]))
                SY.wait("tensor", wtok)
                if oc == 0 and fp > 0:
                    p.wait("tensor", cv, _MS[f"x1ev_{fp - 1}"])

                def t_wo(h, fp=fp, oc=oc, b=b):
                    last = None
                    for half in range(2):
                        fc = 2 * fp + half
                        for nb in range(NB):
                            last = h.matmul(
                                p2F[:, half * 2048 + nb * 512:
                                    half * 2048 + (nb + 1) * 512],
                                wos[b][:, fc * 128:(fc + 1) * 128],
                                E.attnT[oc][:, nb * 512:(nb + 1) * 512],
                                start=(oc == 0), stop=(oc == H - 1))
                    return last
                p.op("tensor", t_wo, ct)
                _MS[f"wo_mm_{fp}_{oc}"] = ct.n
            p.wait("vector", ct, ct.n)

            def v_x1(h, fp=fp):
                last = None
                for half in range(2):
                    fc = 2 * fp + half
                    last = h.tensor_tensor(x1[fc][:, :],
                                           p2F[:, half * 2048:(half + 1) * 2048],
                                           E.xbf[fc][:, :], op=ALU.add)
                return last
            p.op("vector", v_x1, cv)
            _MS[f"x1ev_{fp}"] = cv.n
        x1_v = cv.n
        _tap_sbuf(p, E, cv, x1_v, "x1", x1)

        p.wait("scalar", cv, x1_v)
        p.wait("tensor", cv, x1_v)
        for ec in range(NEC):
            def t_sum(h, ec=ec):
                last = None
                for nb in range(NB):
                    last = h.matmul(p2S[0:1, nb * 512:(nb + 1) * 512],
                                    E.onesb[:, 0:1],
                                    x1[ec][:, nb * 512:(nb + 1) * 512],
                                    start=(ec == 0), stop=(ec == NEC - 1))
                return last
            p.op("tensor", t_sum, ct)
        sum_t = ct.n
        sqmm_done = {}
        for ec in range(NEC):
            b = ec % 2
            if ec >= 2:
                p.wait("scalar", ct, sqmm_done[ec - 2])

            def s_sq(h, ec=ec, b=b):
                return h.activation(out=x1s[b][:, :], in_=x1[ec][:, :],
                                    func=AF.Square)
            p.op("scalar", s_sq, cs)
            p.wait("tensor", cs, cs.n)

            def t_sq(h, ec=ec, b=b):
                last = None
                for nb in range(NB):
                    last = h.matmul(p2S[0:1, S + nb * 512: S + (nb + 1) * 512],
                                    E.onesb[:, 0:1],
                                    x1s[b][:, nb * 512:(nb + 1) * 512],
                                    start=(ec == 0), stop=(ec == NEC - 1))
                return last
            p.op("tensor", t_sq, ct)
            sqmm_done[ec] = ct.n
        p.wait("vector", ct, ct.n)

        def v_stat(h):
            h.tensor_copy(stat[0:1, 0:S], p2S[0:1, 0:S])
            return h.tensor_copy(stat[0:1, S:2 * S], p2S[0:1, S:2 * S])
        p.op("vector", v_stat, cv)
        p.wait("tensor", cv, cv.n)

        def v_ln1a(h):
            h.drain()
            h.tensor_scalar_mul(stat[0:1, 0:S], stat[0:1, 0:S], 1.0 / DM)
            h.tensor_scalar_mul(stat[0:1, S:2 * S], stat[0:1, S:2 * S], 1.0 / DM)
            h.drain()
            h.tensor_tensor(mbc[0:1, :], stat[0:1, 0:S], stat[0:1, 0:S],
                            op=ALU.mult)
            h.drain()
            h.tensor_tensor(stat[0:1, S:2 * S], stat[0:1, S:2 * S],
                            mbc[0:1, :], op=ALU.subtract)
            h.drain()
            return h.tensor_scalar_add(stat[0:1, S:2 * S],
                                       stat[0:1, S:2 * S], EPS)
        p.op("vector", v_ln1a, cv)
        p.wait("scalar", cv, cv.n)

        def s_rstd(h):
            h.activation(out=mbc[0:1, :], in_=stat[0:1, S:2 * S], func=AF.Ln)
            h.drain()
            return h.activation(out=stat[0:1, S:2 * S],
                                in_=mbc[0:1, :], func=AF.Exp, scale=-0.5)
        p.op("scalar", s_rstd, cs)
        p.wait("tensor", cs, cs.n)

        def t_bc(h):
            last = None
            for blk in range(4):
                last = h.matmul(p2F[:, blk * 512:(blk + 1) * 512], E.ones1f[0:1, :],
                                stat[0:1, blk * 512:(blk + 1) * 512],
                                start=True, stop=True)
                last = h.matmul(p2F[:, 2048 + blk * 512: 2048 + (blk + 1) * 512],
                                E.ones1f[0:1, :],
                                stat[0:1, S + blk * 512: S + (blk + 1) * 512],
                                start=True, stop=True)
            return last
        p.op("tensor", t_bc, ct)
        p.wait("vector", ct, ct.n)

        def v_bc(h):
            last = None
            for blk in range(4):
                last = h.tensor_copy(mbc[:, blk * 512:(blk + 1) * 512],
                                     p2F[:, blk * 512:(blk + 1) * 512])
                last = h.tensor_copy(rbc[:, blk * 512:(blk + 1) * 512],
                                     p2F[:, 2048 + blk * 512: 2048 + (blk + 1) * 512])
            return last
        p.op("vector", v_bc, cv)
        p.raw("vector", lambda h: h.drain())

        for ec in range(NEC):
            def v_ap1(h, ec=ec):
                return h.tensor_tensor(x1[ec][:, :], x1[ec][:, :], mbc[:, :],
                                       op=ALU.subtract)
            p.op("vector", v_ap1, cv)
            p.raw("vector", lambda h: h.drain())

            def v_ap2(h, ec=ec):
                return h.tensor_tensor(E.xln[ec][:, 2:2 + S], x1[ec][:, :],
                                       rbc[:, :], op=ALU.mult)
            p.op("vector", v_ap2, cv)
        _MS["p2_v"] = cv.n
        _MS["p2_t"] = ct.n
        _MS["p2_s"] = cs.n
        _tap_sbuf(p, E, cv, cv.n, "xln", E.xln)


def _phase3(p, E):
    nc = E.nc
    ct, cv, cs, cg = E.ct, E.cv, E.cs, E.cg
    SY, GP = E.SY, E.GP

    def xs(ec, lo, n):
        return E.xln[ec][:, 2 + lo: 2 + lo + n]

    with ExitStack() as S3:
        al = S3.enter_context
        rwt = al(nc.sbuf_tensor("rwt", [128, 128], BF16))
        combT = al(nc.sbuf_tensor("combT", [16, 2048], F32))
        cstage = al(nc.sbuf_tensor("cstage", [1, 2048], BF16))
        SC = ExitStack()
        cal = SC.enter_context
        cwT = [cal(nc.sbuf_tensor(f"cwT{k}", [128, 4096], BF16)) for k in range(3)]
        sgate = [cal(nc.sbuf_tensor(f"sgate{i}", [128, S], BF16))
                 for i in range(NIC)]
        smid = [cal(nc.sbuf_tensor(f"smid{i}", [128, S], BF16))
                for i in range(NIC)]
        supt = cal(nc.sbuf_tensor("supt", [128, 4096], BF16))
        sdnt = cal(nc.sbuf_tensor("sdnt", [128, 4096], BF16))
        SR = ExitStack()
        rl = SR.enter_context(nc.sbuf_tensor("rl", [16, 2048], F32))
        rt = SR.enter_context(nc.sbuf_tensor("rt", [128, 256], F32))
        comb = SR.enter_context(nc.sbuf_tensor("comb", [128, 256], F32))
        rsc = SR.enter_context(nc.sbuf_tensor("rsc", [128, 64], F32))
        eq1 = SR.enter_context(nc.sbuf_tensor("eq1", [128, 256], F32))
        eq2 = SR.enter_context(nc.sbuf_tensor("eq2", [128, 256], F32))
        eq3 = SR.enter_context(nc.sbuf_tensor("eq3", [128, 256], F32))
        c15b = SR.enter_context(nc.sbuf_tensor("c15b", [128, S], BF16))

        p.wait("gpsimd", cv, _MS["p2_v"])
        p.wait("gpsimd", cs, _MS["p2_s"])
        with ExitStack() as SCW:
            cnat = [SCW.enter_context(nc.sbuf_tensor(f"cnat{i}", [128, DM * 3],
                                                     BF16)) for i in range(NIC)]
            p3Bt = SCW.enter_context(nc.psum_tensor("p3Bt", [128, 1024], BF16))
            cw_toks = []
            for ic in range(NIC):
                cw_toks.append(GP.issue(lambda h, ic=ic: h.dma_start(
                    out=cnat[ic][:, :],
                    in_=E.cw_d.ap()[ic * 128:(ic + 1) * 128])))
            cw_toks.append(GP.issue(lambda h: h.dma_start(
                out=supt[:, :],
                in_=bass.AP(E.sup_d, 0, [[512, 128], [65536, 8], [1, 512]]))))
            cw_toks.append(GP.issue(lambda h: h.dma_start(
                out=sdnt[:, :],
                in_=bass.AP(E.sdn_d, 0, [[1024, 128], [131072, 4], [1, 1024]]))))
            rwtok = GP.issue(lambda h: h.dma_start(
                out=rwt[:, :],
                in_=bass.AP(E.rw_d, 0, [[16, 128], [2048, 8], [1, 16]])))

            p.wait("tensor", cv, _MS["p2_v"])
            p.wait("tensor", cs, _MS["p2_s"])
            for tok in cw_toks:
                GP.wait("tensor", tok)
            for k in range(3):
                for hc in range(NEC):
                    def t_cwt(h, k=k, hc=hc):
                        last = None
                        for ic in range(NIC):
                            base = cnat[ic][:, :]
                            src = bass.AP(base.tensor, hc * 384 + k,
                                          [[DM * 3, 128], [3, 128]])
                            last = h.transpose(p3Bt[:, ic * 128:(ic + 1) * 128],
                                               src, E.identb[:, :])
                        return last
                    p.op("tensor", t_cwt, ct)
                    p.wait("vector", ct, ct.n)

                    def v_cwt(h, k=k, hc=hc):
                        return h.tensor_copy(cwT[k][:, hc * 512:(hc + 1) * 512],
                                             p3Bt[:, 0:512])
                    p.op("vector", v_cwt, cv)
                    p.wait("tensor", cv, cv.n)
        cwt_done_v = cv.n

        p3F = al(nc.psum_tensor("p3F", [128, 4096], F32))
        GP.wait("tensor", rwtok)
        p.wait("tensor", cv, cwt_done_v)

        def t_router(h):
            last = None
            for nb in range(NB):
                for ec in range(NEC):
                    last = h.matmul(p3F[0:16, nb * 512:(nb + 1) * 512],
                                    rwt[:, ec * 16:(ec + 1) * 16],
                                    xs(ec, nb * 512, 512),
                                    start=(ec == 0), stop=(ec == NEC - 1))
            return last
        p.op("tensor", t_router, ct)
        p.wait("vector", ct, ct.n)

        def v_rbias(h):
            return h.tensor_scalar(rl[0:16, :], p3F[0:16, 0:2048],
                                   E.fsc[0:16, 0:1], None, op0=ALU.add)
        p.op("vector", v_rbias, cv)
        rl_done = cv.n
        _tap_sbuf(p, E, cv, rl_done, "logits", [rl])

        p.wait("tensor", cv, rl_done)

        def t_rtr(h):
            last = None
            for tc in range(NTC):
                last = h.transpose(p3F[:, 2048 + tc * 16: 2048 + (tc + 1) * 16],
                                   rl[0:16, tc * 128:(tc + 1) * 128],
                                   E.identf[0:16, 0:16])
            return last
        p.op("tensor", t_rtr, ct)
        p.wait("vector", ct, ct.n)

        def v_rt(h):
            return h.tensor_copy(rt[:, :], p3F[:, 2048:2304])
        p.op("vector", v_rt, cv)

        def s_sm(h):
            return h.activation(out=comb[:, :], in_=rt[:, :], func=AF.Exp)
        p.wait("scalar", cv, cv.n)
        p.op("scalar", s_sm, cs)
        p.wait("vector", cs, cs.n)

        def v_sum(h):
            h.reduce_sum(rsc[:, 0:16],
                         comb[:, :].rearrange("p (t e) -> p t e", e=16),
                         axis=mybir.AxisListType.X)
            h.drain()
            h.reciprocal(rsc[:, 16:32], rsc[:, 0:16])
            h.drain()
            return h.tensor_tensor(
                comb[:, :].rearrange("p (t e) -> p t e", e=16),
                comb[:, :].rearrange("p (t e) -> p t e", e=16),
                rsc[:, 16:32].unsqueeze(2).to_broadcast((128, 16, 16)),
                op=ALU.mult)
        p.op("vector", v_sum, cv)

        def v_top8(h):
            last = None
            for tc in range(NTC):
                last = h.max(out=rsc[:, 32 + 0:32 + 8] if False else
                             rt[:, tc * 16: tc * 16 + 8],
                             in_=comb[:, tc * 16:(tc + 1) * 16])
            return last
        p.raw("vector", lambda h: h.drain())
        p.op("vector", v_top8, cv)
        p.raw("vector", lambda h: h.drain())

        def v_comb(h):
            m1 = rt[:, :].rearrange("p (t e) -> p t e", e=16)[:, :, 0:1]
            m2 = rt[:, :].rearrange("p (t e) -> p t e", e=16)[:, :, 1:2]
            pr = comb[:, :].rearrange("p (t e) -> p t e", e=16)
            h.tensor_tensor(rsc[:, 32:48].rearrange("p (t o) -> p t o", o=1),
                            m1, m2, op=ALU.add)
            h.drain()
            h.tensor_scalar_add(rsc[:, 32:48], rsc[:, 32:48], 1e-6)
            h.drain()
            h.reciprocal(rsc[:, 48:64], rsc[:, 32:48])
            h.drain()
            h.tensor_tensor(rsc[:, 0:16].rearrange("p (t o) -> p t o", o=1), m1,
                            rsc[:, 48:64].rearrange("p (t o) -> p t o", o=1),
                            op=ALU.mult)
            h.tensor_tensor(rsc[:, 16:32].rearrange("p (t o) -> p t o", o=1), m2,
                            rsc[:, 48:64].rearrange("p (t o) -> p t o", o=1),
                            op=ALU.mult)
            h.drain()
            h.tensor_tensor(eq1[:, :].rearrange("p (t e) -> p t e", e=16), pr,
                            m1.to_broadcast((128, 16, 16)), op=ALU.is_equal)
            h.tensor_tensor(eq2[:, :].rearrange("p (t e) -> p t e", e=16), pr,
                            m2.to_broadcast((128, 16, 16)), op=ALU.is_equal)
            h.drain()
            h.tensor_tensor(eq3[:, :], eq2[:, :], eq1[:, :], op=ALU.mult)
            h.drain()
            h.tensor_tensor(eq2[:, :], eq2[:, :], eq3[:, :], op=ALU.subtract)
            h.drain()
            h.tensor_tensor(eq1[:, :].rearrange("p (t e) -> p t e", e=16),
                            eq1[:, :].rearrange("p (t e) -> p t e", e=16),
                            rsc[:, 0:16].unsqueeze(2).to_broadcast((128, 16, 16)),
                            op=ALU.mult)
            h.tensor_tensor(eq2[:, :].rearrange("p (t e) -> p t e", e=16),
                            eq2[:, :].rearrange("p (t e) -> p t e", e=16),
                            rsc[:, 16:32].unsqueeze(2).to_broadcast((128, 16, 16)),
                            op=ALU.mult)
            h.drain()
            return h.tensor_tensor(comb[:, :], eq1[:, :], eq2[:, :], op=ALU.add)
        p.op("vector", v_comb, cv)
        comb_done = cv.n
        _tap_sbuf(p, E, cv, comb_done, "combine", [comb])

        p.wait("tensor", cv, comb_done)

        def t_ctr(h):
            last = None
            for tc in range(NTC):
                last = h.transpose(p3F[0:16, 2048 + tc * 128: 2048 + (tc + 1) * 128],
                                   comb[:, tc * 16:(tc + 1) * 16], E.identf[:, :])
            return last
        p.op("tensor", t_ctr, ct)
        p.wait("vector", ct, ct.n)

        def v_ct(h):
            return h.tensor_copy(combT[0:16, :], p3F[0:16, 2048:4096])
        p.op("vector", v_ct, cv)
        combT_done = cv.n

        p.wait("gpsimd", cv, combT_done)

        c15tok = GP.issue(lambda h: h.dma_start(out=cstage[0:1, :],
                                                in_=combT[15:16, :]))
        GP.wait("tensor", c15tok)

        def t_c15b(h):
            last = None
            for blk in range(4):
                last = h.matmul(p3F[:, blk * 512:(blk + 1) * 512], E.ones1b[0:1, :],
                                cstage[0:1, blk * 512:(blk + 1) * 512],
                                start=True, stop=True)
            return last
        p.op("tensor", t_c15b, ct)
        p.wait("vector", ct, ct.n)

        def v_c15b(h):
            last = None
            for blk in range(4):
                last = h.tensor_copy(c15b[:, blk * 512:(blk + 1) * 512],
                                     p3F[:, blk * 512:(blk + 1) * 512])
            return last
        p.op("vector", v_c15b, cv)
        p.raw("vector", lambda h: h.drain())
        for ec in range(NEC):
            def v_acc0(h, ec=ec):
                return h.tensor_tensor(E.acc[ec][:, :], xs(ec, 0, S), c15b[:, :],
                                       op=ALU.mult)
            p.op("vector", v_acc0, cv)
        acc_init = cv.n
        SR.close()

        if True:
            p.wait("tensor", cv, acc_init)
            for ic in range(NIC):
                for nb in range(NB):
                    kk = ic * NB + nb
                    if kk >= 4:
                        p.wait("tensor", cs, _MS[f"sg_{kk - 4}"])

                    def t_conv(h, ic=ic, nb=nb, kk=kk):
                        last = None
                        first = True
                        pb = (kk % 4) * 512
                        for k in range(3):
                            for hc in range(NEC):
                                last = h.matmul(
                                    p3F[:, pb:pb + 512],
                                    cwT[k][:, hc * 512 + ic * 128:
                                           hc * 512 + ic * 128 + 128],
                                    E.xln[hc][:, nb * 512 + k: nb * 512 + k + 512],
                                    start=first, stop=(k == 2 and hc == NEC - 1))
                                first = False
                        return last
                    p.op("tensor", t_conv, ct)
                    p.wait("scalar", ct, ct.n)

                    def s_sg(h, ic=ic, nb=nb, kk=kk):
                        pb = (kk % 4) * 512
                        return h.activation(
                            out=sgate[ic][:, nb * 512:(nb + 1) * 512],
                            in_=p3F[:, pb:pb + 512], func=AF.Silu)
                    p.op("scalar", s_sg, cs)
                    _MS[f"sg_{kk}"] = cs.n
            _tap_sbuf(p, E, cs, cs.n, "sgate", sgate)

            for ic in range(NIC):
                for nb in range(NB):
                    kk = ic * NB + nb
                    if kk >= 2:
                        p.wait("tensor", cv, _MS[f"smid_{kk - 2}"])

                    def t_sup(h, ic=ic, nb=nb, kk=kk):
                        last = None
                        pb = 2048 + (kk % 2) * 512
                        for ec in range(NEC):
                            last = h.matmul(
                                p3F[:, pb:pb + 512],
                                supt[:, ec * 512 + ic * 128: ec * 512 + ic * 128 + 128],
                                xs(ec, nb * 512, 512),
                                start=(ec == 0), stop=(ec == NEC - 1))
                        return last
                    p.op("tensor", t_sup, ct)
                    p.wait("vector", ct, ct.n)
                    p.wait("vector", cs, _MS[f"sg_{kk}"])

                    def v_smid(h, ic=ic, nb=nb, kk=kk):
                        pb = 2048 + (kk % 2) * 512
                        h.tensor_copy(smid[ic][:, nb * 512:(nb + 1) * 512],
                                      p3F[:, pb:pb + 512])
                        h.drain()
                        return h.tensor_tensor(
                            smid[ic][:, nb * 512:(nb + 1) * 512],
                            smid[ic][:, nb * 512:(nb + 1) * 512],
                            sgate[ic][:, nb * 512:(nb + 1) * 512], op=ALU.mult)
                    p.op("vector", v_smid, cv)
                    _MS[f"smid_{kk}"] = cv.n
            smid_done = cv.n

            p.wait("tensor", cv, smid_done)
            p.wait("tensor", cv, acc_init)
            for fp in range(4):
                if fp > 0:
                    p.wait("tensor", cv, cv.n)

                def t_sdn(h, fp=fp):
                    last = None
                    for half in range(2):
                        fc = 2 * fp + half
                        for nb in range(NB):
                            for ic in range(NIC):
                                last = h.matmul(
                                    p3F[:, half * 2048 + nb * 512:
                                        half * 2048 + (nb + 1) * 512],
                                    sdnt[:, ic * 1024 + fc * 128:
                                         ic * 1024 + fc * 128 + 128],
                                    smid[ic][:, nb * 512:(nb + 1) * 512],
                                    start=(ic == 0), stop=(ic == NIC - 1))
                    return last
                p.op("tensor", t_sdn, ct)
                p.wait("vector", ct, ct.n)

                def v_sdn(h, fp=fp):
                    last = None
                    for half in range(2):
                        fc = 2 * fp + half
                        last = h.tensor_tensor(E.acc[fc][:, :], E.acc[fc][:, :],
                                               p3F[:, half * 2048:(half + 1) * 2048],
                                               op=ALU.add)
                    return last
                p.op("vector", v_sdn, cv)
        shared_done = cv.n
        _tap_sbuf(p, E, cv, shared_done, "acc_shared", E.acc)
        SC.close()

        with ExitStack() as SE:
            eal = SE.enter_context
            pool = [{
                "g": eal(nc.sbuf_tensor(f"egw{s_}", [128, 4096], BF16)),
                "u": eal(nc.sbuf_tensor(f"euw{s_}", [128, 4096], BF16)),
                "d": eal(nc.sbuf_tensor(f"edw{s_}", [128, 4096], BF16)),
            } for s_ in range(4)]
            z = [eal(nc.sbuf_tensor(f"z{i}", [128, 2048], BF16)) for i in range(3)]
            sgz = [eal(nc.sbuf_tensor(f"sgz{i}", [128, 2048], BF16)) for i in range(2)]
            cbc3 = [eal(nc.sbuf_tensor(f"cbc3_{i}", [128, S], BF16)) for i in range(3)]

            def load_expert(e):
                slot = pool[e % 4]
                if e >= 4:
                    p.wait("gpsimd", ct, _MS[f"edone_{e - 4}"])
                toks = [
                    GP.issue(lambda h, e=e, slot=slot: h.dma_start(
                        out=slot["g"][:, :],
                        in_=bass.AP(E.gw_d, e * DM * IDIM,
                                    [[IDIM, 128], [128 * IDIM, 8], [1, IDIM]]))),
                    GP.issue(lambda h, e=e, slot=slot: h.dma_start(
                        out=slot["u"][:, :],
                        in_=bass.AP(E.uw_d, e * DM * IDIM,
                                    [[IDIM, 128], [128 * IDIM, 8], [1, IDIM]]))),
                    GP.issue(lambda h, e=e, slot=slot: h.dma_start(
                        out=slot["d"][:, :],
                        in_=bass.AP(E.dw_d, e * IDIM * DM,
                                    [[DM, 128], [128 * DM, 4], [1, DM]]))),
                ]
                _MS[f"eload_{e}"] = toks

            def stage_cbc(e, slot, gate_cv):
                p.wait("gpsimd", cv, gate_cv)
                p.wait("gpsimd", ct, ct.n)
                tok = GP.issue(lambda h, e=e: h.dma_start(
                    out=cstage[0:1, :], in_=combT[e:e + 1, :]))
                GP.wait("tensor", tok)
                p.wait("tensor", cv, cv.n)

                def t_cb(h):
                    last = None
                    for blk in range(4):
                        last = h.matmul(p3F[:, blk * 512:(blk + 1) * 512],
                                        E.ones1b[0:1, :],
                                        cstage[0:1, blk * 512:(blk + 1) * 512],
                                        start=True, stop=True)
                    return last
                p.op("tensor", t_cb, ct)
                p.wait("vector", ct, ct.n)

                def v_cb(h, slot=slot):
                    last = None
                    for blk in range(4):
                        last = h.tensor_copy(cbc3[slot][:, blk * 512:(blk + 1) * 512],
                                             p3F[:, blk * 512:(blk + 1) * 512])
                    return last
                p.op("vector", v_cb, cv)
                _MS[f"cbc_{e}"] = cv.n

            p.wait("gpsimd", ct, ct.n)
            p.wait("gpsimd", cv, cv.n)
            p.wait("gpsimd", cs, cs.n)
            for e in range(4):
                load_expert(e)

            p.wait("tensor", cv, shared_done)
            prev_silu = cs.n
            prev_zt1 = cv.n
            prev_dn = cv.n
            zt2q = []
            for g in range(5):
                if g == 0:
                    for ei in range(3):
                        stage_cbc(3 * g + ei, ei, cv.n)
                for nb in range(NB):
                    for ei in range(3):
                        e = 3 * g + ei
                        slot = pool[e % 4]
                        for tok in _MS[f"eload_{e}"]:
                            GP.wait("tensor", tok)
                        p.wait("tensor", cs, prev_silu)
                        p.wait("tensor", cv, max(_MS[f"cbc_{3 * g + 2}"], prev_dn))

                        def t_g(h, nb=nb, slot=slot):
                            last = None
                            for ic in range(NIC):
                                for ec in range(NEC):
                                    last = h.matmul(
                                        p3F[:, ic * 512:(ic + 1) * 512],
                                        slot["g"][:, ec * 512 + ic * 128:
                                                  ec * 512 + ic * 128 + 128],
                                        xs(ec, nb * 512, 512),
                                        start=(ec == 0), stop=(ec == NEC - 1))
                            return last
                        p.op("tensor", t_g, ct)
                        tg_done = ct.n
                        p.wait("tensor", cv, prev_zt1)

                        def t_u(h, nb=nb, slot=slot):
                            last = None
                            for ic in range(NIC):
                                for ec in range(NEC):
                                    last = h.matmul(
                                        p3F[:, 2048 + ic * 512:
                                            2048 + (ic + 1) * 512],
                                        slot["u"][:, ec * 512 + ic * 128:
                                                  ec * 512 + ic * 128 + 128],
                                        xs(ec, nb * 512, 512),
                                        start=(ec == 0), stop=(ec == NEC - 1))
                            return last
                        p.op("tensor", t_u, ct)
                        tu_done = ct.n
                        p.wait("scalar", ct, tg_done)
                        sb = sgz[ei % 2]
                        if len(zt2q) >= 2:
                            p.wait("scalar", cv, zt2q[-2])

                        def s_silu(h, sb=sb):
                            return h.activation(out=sb[:, :], in_=p3F[:, 0:2048],
                                                func=AF.Silu)
                        p.op("scalar", s_silu, cs)
                        prev_silu = cs.n
                        p.wait("vector", ct, tu_done)

                        def v_z1(h, nb=nb, ei=ei):
                            zt = z[ei]
                            return h.tensor_tensor(
                                zt[:, :], p3F[:, 2048:4096],
                                cbc3[ei][:, nb * 512:(nb + 1) * 512]
                                .unsqueeze(1).to_broadcast((128, 4, 512)),
                                op=ALU.mult)
                        p.op("vector", v_z1, cv)
                        prev_zt1 = cv.n
                        p.wait("vector", cs, prev_silu)

                        def v_z2(h, ei=ei, sb=sb):
                            zt = z[ei]
                            h.drain()
                            return h.tensor_tensor(zt[:, :], zt[:, :], sb[:, :],
                                                   op=ALU.mult)
                        p.op("vector", v_z2, cv)
                        _MS[f"z_{e}_{nb}"] = cv.n
                        zt2q.append(cv.n)
                    zv = cv.n
                    p.wait("tensor", cv, zv)
                    for half4 in range(2):
                        def t_dn(h, g=g, nb=nb, half4=half4):
                            last = None
                            for fq in range(4):
                                fc = half4 * 4 + fq
                                first = True
                                for ei in range(3):
                                    e = 3 * g + ei
                                    slot = pool[e % 4]
                                    for ic in range(NIC):
                                        last = h.matmul(
                                            p3F[:, fq * 512 + half4 * 2048:
                                                fq * 512 + half4 * 2048 + 512],
                                            slot["d"][:, ic * 1024 + fc * 128:
                                                      ic * 1024 + fc * 128 + 128],
                                            z[ei][:, ic * 512:(ic + 1) * 512],
                                            start=first,
                                            stop=(ei == 2 and ic == NIC - 1))
                                        first = False
                            return last
                        p.op("tensor", t_dn, ct)
                    p.wait("vector", ct, ct.n)

                    def v_dn(h, nb=nb):
                        last = None
                        for fc in range(NEC):
                            last = h.tensor_tensor(
                                E.acc[fc][:, nb * 512:(nb + 1) * 512],
                                E.acc[fc][:, nb * 512:(nb + 1) * 512],
                                p3F[:, fc * 512:(fc + 1) * 512], op=ALU.add)
                        return last
                    p.op("vector", v_dn, cv)
                    prev_dn = cv.n
                for ei in range(3):
                    _MS[f"edone_{3 * g + ei}"] = ct.n
                if g + 1 < 5:
                    for ei in range(3):
                        stage_cbc(3 * (g + 1) + ei, ei, cv.n)
                for e_next in (3 * g + 4, 3 * g + 5, 3 * g + 6):
                    if e_next < EXP:
                        load_expert(e_next)
        _MS["p3_v"] = cv.n
        _tap_sbuf(p, E, cv, cv.n, "ffnacc", E.acc)


def _phase4(p, E):
    nc = E.nc
    ct, cv, cs, cg = E.ct, E.cv, E.cs, E.cg
    SY = E.SY
    with ExitStack() as S4:
        al = S4.enter_context
        x2t = [al(nc.sbuf_tensor(f"x2t{i}", [128, DM], BF16)) for i in range(2)]
        sqt = [al(nc.sbuf_tensor(f"sqt{i}", [128, DM], BF16)) for i in range(2)]
        otile = [al(nc.sbuf_tensor(f"otile{i}", [128, DM], F32)) for i in range(2)]
        st4 = al(nc.sbuf_tensor("st4", [128, 16], F32))
        p4B = al(nc.psum_tensor("p4B", [128, 1024], BF16))

        p.wait("vector", cv, _MS["p3_v"])
        for ec in range(NEC):
            def v_x2(h, ec=ec):
                return h.tensor_tensor(E.acc[ec][:, :], E.acc[ec][:, :],
                                       E.xln[ec][:, 2:2 + S], op=ALU.add)
            p.op("vector", v_x2, cv)
        x2_v = cv.n
        p.wait("tensor", cv, x2_v)

        for tc in range(NTC):
            b = tc % 2

            def t_tr(h, tc=tc):
                last = None
                for ec in range(NEC):
                    last = h.transpose(p4B[:, ec * 128:(ec + 1) * 128],
                                       E.acc[ec][:, tc * 128:(tc + 1) * 128],
                                       E.identb[:, :])
                return last
            if tc >= 1:
                p.wait("tensor", cv, _MS[f"p4ev_{tc - 1}"])
            p.op("tensor", t_tr, ct)
            p.wait("vector", ct, ct.n)

            def v_ev(h, b=b):
                return h.tensor_copy(x2t[b][:, :], p4B[:, 0:1024])
            p.op("vector", v_ev, cv)
            _MS[f"p4ev_{tc}"] = cv.n

            o = (tc % 2) * 8

            def v_stats(h, b=b, o=o):
                h.drain()
                h.tensor_tensor(sqt[b][:, :], x2t[b][:, :], x2t[b][:, :],
                                op=ALU.mult)
                h.reduce_sum(st4[:, o:o + 1], x2t[b][:, :],
                             axis=mybir.AxisListType.X)
                h.drain()
                h.reduce_sum(st4[:, o + 1:o + 2], sqt[b][:, :],
                             axis=mybir.AxisListType.X)
                h.drain()
                h.tensor_scalar_mul(st4[:, o + 2:o + 3], st4[:, o:o + 1], 1.0 / DM)
                h.tensor_scalar_mul(st4[:, o + 3:o + 4], st4[:, o + 1:o + 2],
                                    1.0 / DM)
                h.drain()
                h.tensor_tensor(st4[:, o + 4:o + 5], st4[:, o + 2:o + 3],
                                st4[:, o + 2:o + 3], op=ALU.mult)
                h.drain()
                h.tensor_tensor(st4[:, o + 5:o + 6], st4[:, o + 3:o + 4],
                                st4[:, o + 4:o + 5], op=ALU.subtract)
                h.drain()
                return h.tensor_scalar_add(st4[:, o + 5:o + 6],
                                           st4[:, o + 5:o + 6], EPS)
            p.op("vector", v_stats, cv)
            p.wait("scalar", cv, cv.n)

            def s_rstd(h, o=o):
                h.activation(out=st4[:, o + 6:o + 7], in_=st4[:, o + 5:o + 6],
                             func=AF.Ln)
                h.drain()
                return h.activation(out=st4[:, o + 6:o + 7],
                                    in_=st4[:, o + 6:o + 7], func=AF.Exp,
                                    scale=-0.5)
            p.op("scalar", s_rstd, cs)
            p.wait("vector", cs, cs.n)

            def v_nb(h, o=o):
                h.tensor_tensor(st4[:, o + 7:o + 8], st4[:, o + 2:o + 3],
                                st4[:, o + 6:o + 7], op=ALU.mult)
                h.drain()
                return h.tensor_scalar_mul(st4[:, o + 7:o + 8],
                                           st4[:, o + 7:o + 8], -1.0)
            p.op("vector", v_nb, cv)
            if tc >= 2:
                SY.wait("vector", _MS[f"p4dma_{tc - 2}"])

            def v_apply(h, b=b, o=o):
                h.drain()
                return h.tensor_scalar(otile[b][:, :], x2t[b][:, :],
                                       st4[:, o + 6:o + 7], st4[:, o + 7:o + 8],
                                       op0=ALU.mult, op1=ALU.add)
            p.op("vector", v_apply, cv)
            _MS[f"p4ap_{tc}"] = cv.n
            p.wait("sync", cv, cv.n)
            _MS[f"p4dma_{tc}"] = SY.issue(
                lambda h, tc=tc, b=b: h.dma_start(
                    out=E.out_d.ap()[tc * 128:(tc + 1) * 128, :],
                    in_=otile[b][:, :]))

        SY.barrier("sync")


def _dummy_out(p, E):
    nc = E.nc
    with ExitStack() as SD:
        otile = SD.enter_context(nc.sbuf_tensor("odum", [128, DM], F32))

        def v_z(h):
            return h.memset(otile[:, :], 0.0)
        p.op("vector", v_z, E.cv)
        p.wait("sync", E.cv, E.cv.n)
        for tc in range(NTC):
            E.SY.issue(lambda h, tc=tc: h.dma_start(
                out=E.out_d.ap()[tc * 128:(tc + 1) * 128, :],
                in_=otile[:, :]))
        E.SY.barrier("sync")


def kernel(**inputs):
    def f32(name):
        return np.ascontiguousarray(np.asarray(inputs[name], dtype=np.float32))

    x = f32("L0_attn_hidden_states")
    base = {
        "wq": f32("L0_attn_Wq"), "wk": f32("L0_attn_Wk"), "wv": f32("L0_attn_Wv"),
        "wo": f32("L0_attn_Wo"),
        "aq": f32("L0_attn_alpha_q"), "ak": f32("L0_attn_alpha_k"),
        "av": f32("L0_attn_alpha_v"), "co": f32("L0_attn_collapse"),
        "rw": f32("L0_ffn_router_w"), "rb": f32("L0_ffn_router_bias"),
        "gw": f32("L0_ffn_expert_gate_w"), "uw": f32("L0_ffn_expert_up_w"),
        "dw": f32("L0_ffn_expert_down_w"), "cw": f32("L0_ffn_conv_w"),
        "sup": f32("L0_ffn_shared_up_w"), "sdn": f32("L0_ffn_shared_down_w"),
    }
    nc = build_nc()
    in_maps = [dict(base, x=np.ascontiguousarray(x[i])) for i in range(NCORES)]
    res = run_bass_kernel_spmd(nc, in_maps, core_ids=list(range(NCORES)))
    return np.stack([r["out"] for r in res.results], axis=0)


# version 11
# speedup vs baseline: 1.0458x; 1.0033x over previous
import math
import numpy as np
from contextlib import ExitStack

import concourse.bass as bass
import concourse.mybir as mybir
from concourse.bass_utils import run_bass_kernel_spmd
from concourse import library_config

F32 = mybir.dt.float32
BF16 = mybir.dt.bfloat16
I32 = mybir.dt.int32
AF = mybir.ActivationFunctionType
ALU = mybir.AluOpType

NCORES = 8
S = 2048
DM = 1024
H = 16
D = 64
NEC = DM // 128
NTC = S // 128
NB = 4
EXP = 15
IDIM = 512
NIC = IDIM // 128
EPS = 1e-5

ENGINES = ("tensor", "vector", "scalar", "gpsimd", "sync")

N_HEADS = H
STOP_AFTER = 4
TAPS = set()
TAP_TENSORS = {}

_MS = {}


class Ctr:
    __slots__ = ("h", "n")

    def __init__(self, h):
        self.h = h
        self.n = 0


class Prog:
    def __init__(self, block):
        self.block = block
        self.q = {e: [] for e in ENGINES}

    def op(self, eng, make, ctr=None, k=1):
        if ctr is not None:
            ctr.n += k
            self.q[eng].append(lambda h, m=make, s=ctr.h, kk=k: m(h).then_inc(s, kk))
            return ctr.n
        self.q[eng].append(lambda h, m=make: m(h))
        return None

    def wait(self, eng, ctr, val):
        if val > ctr.n:
            raise RuntimeError(f"wait on future count {val} > {ctr.n} ({eng})")
        self.q[eng].append(lambda h, s=ctr.h, v=val: h.wait_ge(s, v))

    def raw(self, eng, fn):
        self.q[eng].append(fn)

    def flush(self):
        for e in ENGINES:
            ops = self.q[e]
            if not ops:
                continue
            self.q[e] = []
            deco = getattr(self.block, e)

            def body(h, ops=ops):
                for fn in ops:
                    fn(h)
            deco(body)


class DmaPool:

    def __init__(self, p, eng, ctrs):
        self.p = p
        self.eng = eng
        self.ctrs = ctrs
        self.i = 0

    def issue(self, maker):
        c = self.ctrs[self.i % len(self.ctrs)]
        self.i += 1
        if c.n > 0:
            self.p.wait(self.eng, c, c.n)
        self.p.op(self.eng, maker, c, 16)
        return (c, c.n)

    def wait(self, eng, tok):
        self.p.wait(eng, tok[0], tok[1])

    def barrier(self, eng):
        for c in self.ctrs:
            if c.n > 0:
                self.p.wait(eng, c, c.n)


class Env:
    pass


def _bc2(ap, n):
    return ap.unsqueeze(2).to_broadcast((2, 2, n))


def _tap_sbuf(p, E, ctr, val, name, tiles):
    if name not in TAPS:
        return
    if not isinstance(tiles, (list, tuple)):
        tiles = [tiles]
    if name not in TAP_TENSORS:
        ap0 = tiles[0][:, :]
        shape = [len(tiles)] + list(ap0.shape) if len(tiles) > 1 else list(ap0.shape)
        TAP_TENSORS[name] = E.nc.dram_tensor("dbg_" + name, shape, ap0.dtype,
                                             kind="ExternalOutput")
    dram = TAP_TENSORS[name]
    p.wait("sync", ctr, val)
    toks = []
    for i, t in enumerate(tiles):
        toks.append(E.SY.issue(lambda h, i=i, t=t: h.dma_start(
            out=dram.ap()[i] if len(tiles) > 1 else dram.ap(),
            in_=t[:, :])))
    for eng in ("tensor", "vector", "scalar", "gpsimd"):
        for tok in toks:
            E.SY.wait(eng, tok)


def build_nc():
    TAP_TENSORS.clear()
    _MS.clear()
    nc = bass.Bass("TRN2", target_bir_lowering=False)
    E = Env()
    E.nc = nc

    E.xin = nc.dram_tensor("x", [S, DM], F32, kind="ExternalInput")
    E.w_d = {
        "q": nc.dram_tensor("wq", [DM, DM], F32, kind="ExternalInput"),
        "k": nc.dram_tensor("wk", [DM, DM], F32, kind="ExternalInput"),
        "v": nc.dram_tensor("wv", [DM, DM], F32, kind="ExternalInput"),
    }
    E.wo_d = nc.dram_tensor("wo", [DM, DM], F32, kind="ExternalInput")
    E.al_d = [nc.dram_tensor(n, [H, H, 2], F32, kind="ExternalInput")
              for n in ("aq", "ak", "av")]
    E.co_d = nc.dram_tensor("co", [H, H * 2], F32, kind="ExternalInput")
    E.rw_d = nc.dram_tensor("rw", [DM, 16], F32, kind="ExternalInput")
    E.rb_d = nc.dram_tensor("rb", [16], F32, kind="ExternalInput")
    E.gw_d = nc.dram_tensor("gw", [EXP, DM, IDIM], F32, kind="ExternalInput")
    E.uw_d = nc.dram_tensor("uw", [EXP, DM, IDIM], F32, kind="ExternalInput")
    E.dw_d = nc.dram_tensor("dw", [EXP, IDIM, DM], F32, kind="ExternalInput")
    E.cw_d = nc.dram_tensor("cw", [IDIM, DM, 3], F32, kind="ExternalInput")
    E.sup_d = nc.dram_tensor("sup", [DM, IDIM], F32, kind="ExternalInput")
    E.sdn_d = nc.dram_tensor("sdn", [IDIM, DM], F32, kind="ExternalInput")
    E.out_d = nc.dram_tensor("out", [S, DM], F32, kind="ExternalOutput")
    E.wp_d = nc.dram_tensor("wp_scratch", [3, H, 128, DM], BF16, kind="Internal")
    E.wop_d = nc.dram_tensor("wop_scratch", [H, 128, DM], BF16, kind="Internal")

    def tap(name, shape, dtype=F32):
        if name in TAPS:
            TAP_TENSORS[name] = nc.dram_tensor("dbg_" + name, list(shape), dtype,
                                               kind="ExternalOutput")
    E.tap = tap

    with ExitStack() as G:
        al = G.enter_context
        E.ct = Ctr(al(nc.semaphore("c_t")))
        E.cv = Ctr(al(nc.semaphore("c_v")))
        E.cs = Ctr(al(nc.semaphore("c_s")))
        E.cg = Ctr(al(nc.semaphore("c_g")))

        E.xln = [al(nc.sbuf_tensor(f"xln{i}", [128, S + 2], BF16)) for i in range(NEC)]
        E.identb = al(nc.sbuf_tensor("identb", [128, 128], BF16))
        E.identf = al(nc.sbuf_tensor("identf", [128, 128], F32))
        E.onesb = al(nc.sbuf_tensor("onesb", [128, 2], BF16))
        E.ones1b = al(nc.sbuf_tensor("ones1b", [1, 128], BF16))
        E.oc64 = al(nc.sbuf_tensor("oc64", [128, 66], BF16))
        E.ones1f = al(nc.sbuf_tensor("ones1f", [1, 128], F32))
        E.fsc = al(nc.sbuf_tensor("fsc", [128, 64], F32))

        with nc.Block() as block:
            p = Prog(block)
            E.SY = DmaPool(p, "sync",
                           [Ctr(al(nc.semaphore(f"dsy{i}"))) for i in range(3)])
            E.GP = DmaPool(p, "gpsimd",
                           [Ctr(al(nc.semaphore(f"dgp{i}"))) for i in range(3)])

            with ExitStack() as S02:
                sal = S02.enter_context
                E.xbf = [sal(nc.sbuf_tensor(f"xbf{i}", [128, S], BF16))
                         for i in range(NEC)]
                E.selc = sal(nc.sbuf_tensor("selc", [128, 128], F32))
                E.attnT = [sal(nc.sbuf_tensor(f"attnT{i}", [128, S], BF16))
                           for i in range(H)]

                with ExitStack() as S01:
                    E.masks = S01.enter_context(
                        nc.sbuf_tensor("masks", [128, 4 * 2048], BF16))
                    E.amix = S01.enter_context(
                        nc.sbuf_tensor("amix", [2, 3 * 256], BF16))
                    E.sel2 = S01.enter_context(
                        nc.sbuf_tensor("sel2", [2, 128], BF16))
                    E.diag64 = S01.enter_context(
                        nc.sbuf_tensor("diag64", [128, 128], BF16))

                    _phase0(p, E)
                    p.flush()

                    if STOP_AFTER >= 1:
                        _phase1(p, E)
                        p.flush()

                if STOP_AFTER >= 2:
                    _phase2(p, E)
                    p.flush()

            if STOP_AFTER >= 3:
                with ExitStack() as S34:
                    E.acc = [S34.enter_context(
                        nc.sbuf_tensor(f"acc{i}", [128, S], BF16))
                        for i in range(NEC)]
                    _phase3(p, E)
                    p.flush()
                    _phase4(p, E)
                    p.flush()
            else:
                _dummy_out(p, E)
                p.flush()

    return nc


def _phase0(p, E):
    nc = E.nc
    ct, cv, cs, cg = E.ct, E.cv, E.cs, E.cg
    SY, GP = E.SY, E.GP
    with ExitStack() as S0:
        al = S0.enter_context
        xint = [al(nc.sbuf_tensor(f"xint{i}", [128, DM], F32)) for i in range(2)]
        wtmp = al(nc.sbuf_tensor("wtmp", [128, DM], BF16))
        wstg = [al(nc.sbuf_tensor(f"wstg{i}", [128, DM], BF16)) for i in range(2)]
        stile = [al(nc.sbuf_tensor(f"stile{i}", [128, 128], BF16))
                 for i in range(16)]
        co2 = al(nc.sbuf_tensor("co2", [2, 256], BF16))
        psA = al(nc.psum_tensor("p0A", [128, 1024], F32))
        psV = al(nc.psum_tensor("p0V", [128, 1024], F32))
        psT = al(nc.psum_tensor("p0T", [128, 1024], BF16))

        with ExitStack() as Sio:
            iosc = Sio.enter_context(nc.sbuf_tensor("iosc", [128, 2560], I32))

            def g_setup(h):
                h.iota(iosc[:, 0:128], [[-1, 128]], base=0, channel_multiplier=1)
                h.iota(iosc[:, 128:256], [[-1, 128]], base=128, channel_multiplier=1)
                h.iota(iosc[0:2, 2304:2432], [[1, 128]], base=0,
                       channel_multiplier=-64)
                h.iota(iosc[64:66, 2304:2432], [[1, 128]], base=0,
                       channel_multiplier=-64)
                for j in range(4):
                    h.iota(iosc[:, 256 + j * 512: 256 + (j + 1) * 512],
                           [[1, 512]], base=-128 * j, channel_multiplier=-1)
                h.memset(E.onesb[:, :], 1.0)
                h.memset(E.oc64[:, :], 0.0)
                h.memset(E.ones1b[0:1, :], 1.0)
                h.memset(E.ones1f[0:1, :], 1.0)
                h.memset(E.sel2[:, :], 0.0)
                h.memset(E.selc[:, :], 0.0)
                h.memset(E.fsc[:, :], 0.0)
                return h.drain()
            p.op("gpsimd", g_setup, cg)

            def g_setup2(h):
                h.memset(E.oc64[:, 1:2], 1.0)
                for ec in range(NEC):
                    h.memset(E.xln[ec][:, 0:2], 0.0)
                return h.drain()
            p.op("gpsimd", g_setup2, cg)
            g_consts = cg.n

            for idx, t in enumerate(E.al_d):
                GP.issue(lambda h, t=t, idx=idx: h.dma_start(
                    out=E.amix[0:2, idx * 256:(idx + 1) * 256],
                    in_=bass.AP(t, 0, [[32, 2], [64, 8], [1, 32]])))
            GP.issue(lambda h: h.dma_start(
                out=co2[0:2, :],
                in_=bass.AP(E.co_d, 0, [[32, 2], [64, 8], [1, 32]])))
            GP.issue(lambda h: h.dma_start(
                out=E.fsc[0:16, 0:1],
                in_=bass.AP(E.rb_d, 0, [[1, 16], [1, 1]])))
            gp_small = [(c, c.n) for c in GP.ctrs]

            p.wait("vector", cg, g_consts)

            def v_setup(h):
                h.tensor_scalar(E.identb[:, :], iosc[:, 0:128], 0, None,
                                op0=ALU.is_equal)
                h.tensor_scalar(E.identf[:, :], iosc[:, 0:128], 0, None,
                                op0=ALU.is_equal)
                h.tensor_scalar(iosc[0:2, 2432:2560], iosc[0:2, 2304:2432], 64,
                                None, op0=ALU.is_lt)
                h.tensor_scalar(iosc[0:2, 2304:2432], iosc[0:2, 2304:2432], 0,
                                None, op0=ALU.is_ge)
                h.tensor_scalar(iosc[64:66, 2432:2560], iosc[64:66, 2304:2432],
                                64, None, op0=ALU.is_lt)
                h.tensor_scalar(iosc[64:66, 2304:2432], iosc[64:66, 2304:2432],
                                0, None, op0=ALU.is_ge)
                return h.drain()
            p.op("vector", v_setup, cv)

            def v_setup2(h):
                h.tensor_tensor(E.sel2[0:2, :], iosc[0:2, 2304:2432],
                                iosc[0:2, 2432:2560], op=ALU.mult)
                h.tensor_tensor(E.selc[64:66, :], iosc[64:66, 2304:2432],
                                iosc[64:66, 2432:2560], op=ALU.mult)
                h.tensor_scalar(E.diag64[:, :], iosc[:, 128:256], 128, None,
                                op0=ALU.is_equal)
                h.tensor_scalar(stile[0][:, :], iosc[:, 128:256], 64, None,
                                op0=ALU.is_equal)
                h.tensor_scalar(stile[1][:, :], iosc[:, 128:256], 192, None,
                                op0=ALU.is_equal)
                h.drain()
                h.tensor_tensor(E.diag64[:, :], E.diag64[:, :], stile[0][:, :],
                                op=ALU.add)
                h.drain()
                h.tensor_tensor(E.diag64[:, :], E.diag64[:, :], stile[1][:, :],
                                op=ALU.add)
                h.drain()
                last = None
                for j in range(4):
                    srcm = iosc[:, 256 + j * 512: 256 + (j + 1) * 512]
                    for pl in range(4):
                        thr = 1 if pl == 3 else 0
                        last = h.tensor_scalar(
                            E.masks[:, j * 2048 + pl * 512:
                                    j * 2048 + (pl + 1) * 512],
                            srcm, thr, None, op0=ALU.is_ge)
                return last
            p.op("vector", v_setup2, cv)
            v_consts = cv.n

            for c, v in gp_small:
                p.wait("vector", c, v)

            def v_alpha(h):
                return h.tensor_scalar_mul(E.amix[0:2, 0:256], E.amix[0:2, 0:256],
                                           1.0 / math.sqrt(D))
            p.op("vector", v_alpha, cv)
            v_alpha_done = cv.n

            xtr_t0 = ct.n
            xtr_v0 = cv.n
            p.wait("tensor", cv, v_consts)
            for tc in range(NTC):
                buf = tc % 2
                if tc >= 2:
                    p.wait("sync", ct, xtr_t0 + (tc - 1))

                xtok = SY.issue(lambda h, tc=tc, buf=buf: h.dma_start(
                    out=xint[buf][:, :],
                    in_=E.xin.ap()[tc * 128:(tc + 1) * 128, :]))
                SY.wait("tensor", xtok)
                if tc >= 1:
                    p.wait("tensor", cv, xtr_v0 + (tc - 1) * 2 + 2)

                def t_xtr(h, buf=buf):
                    last = None
                    for ec in range(NEC):
                        half = ec // 4
                        last = h.transpose(
                            psA[:, half * 512 + (ec % 4) * 128:
                                half * 512 + (ec % 4) * 128 + 128],
                            xint[buf][:, ec * 128:(ec + 1) * 128], E.identf[:, :])
                    return last
                p.op("tensor", t_xtr, ct)

                p.wait("vector", ct, ct.n)
                for half in range(2):
                    def v_xe(h, tc=tc, half=half):
                        last = None
                        for k4 in range(4):
                            ec = half * 4 + k4
                            last = h.tensor_copy(
                                E.xbf[ec][:, tc * 128:(tc + 1) * 128],
                                psA[:, half * 512 + k4 * 128:
                                    half * 512 + (k4 + 1) * 128])
                        return last
                    p.op("vector", v_xe, cv)
            _MS["x_done_v"] = cv.n
            _tap_sbuf(p, E, cv, cv.n, "xbf", E.xbf)
        iosc_free_v = cv.n

        with ExitStack() as Swo:
            wal = Swo.enter_context
            wonat = [wal(nc.sbuf_tensor(f"wonat{i}", [128, DM], BF16))
                     for i in range(NEC)]
            wot = [wal(nc.sbuf_tensor(f"wot{i}", [128, DM], BF16))
                   for i in range(NEC)]
            p.wait("gpsimd", cv, iosc_free_v)
            wo_tok = {}
            for fc in range(NEC):
                wo_tok[fc] = GP.issue(lambda h, fc=fc: h.dma_start(
                    out=wonat[fc][:, :],
                    in_=E.wo_d.ap()[fc * 128:(fc + 1) * 128, :]))

            for fc in range(NEC):
                GP.wait("tensor", wo_tok[fc])
            for hc in range(NEC):
                def t_wot(h, hc=hc):
                    last = None
                    for fc in range(NEC):
                        last = h.transpose(psT[:, fc * 128:(fc + 1) * 128],
                                           wonat[fc][:, hc * 128:(hc + 1) * 128],
                                           E.identb[:, :])
                    return last
                p.op("tensor", t_wot, ct)
                p.wait("vector", ct, ct.n)

                def v_wot(h, hc=hc):
                    return h.tensor_copy(wot[hc][:, :], psT[:, :])
                p.op("vector", v_wot, cv)
                p.wait("tensor", cv, cv.n)

            p.wait("tensor", cg, g_consts)
            viter_done = {-1: cv.n}
            for i in range(H + 1):
                p.wait("tensor", cv, viter_done[i - 1])
                vbc_done = kr_done = None
                if i < H:
                    def t_vbc8(h, c=i):
                        last = None
                        for mc in range(NEC):
                            last = h.matmul(psV[:, mc * 128:(mc + 1) * 128],
                                            E.sel2[0:2, :],
                                            _bc2(co2[0:2, mc * 32 + 2 * c:
                                                     mc * 32 + 2 * c + 2], 64),
                                            start=True, stop=True)
                        return last
                    p.op("tensor", t_vbc8, ct)
                    vbc_done = ct.n
                if i - 1 >= 0:
                    def t_kr8(h, sb=((i - 1) % 2) * 8):
                        last = None
                        for mc in range(NEC):
                            h.matmul(psA[:, 0:512], stile[sb + mc][:, :],
                                     wot[mc][:, 0:512],
                                     start=(mc == 0), stop=(mc == NEC - 1))
                            last = h.matmul(psA[:, 512:1024],
                                            stile[sb + mc][:, :],
                                            wot[mc][:, 512:1024],
                                            start=(mc == 0), stop=(mc == NEC - 1))
                        return last
                    p.op("tensor", t_kr8, ct)
                    kr_done = ct.n

                if i < H:
                    p.wait("vector", ct, vbc_done)

                    def v_st8(h, sb=(i % 2) * 8):
                        last = None
                        for mc in range(NEC):
                            last = h.tensor_tensor(stile[sb + mc][:, :],
                                                   E.diag64[:, :],
                                                   psV[:, mc * 128:(mc + 1) * 128],
                                                   op=ALU.mult)
                        return last
                    p.op("vector", v_st8, cv)
                if i - 1 >= 0:
                    c = i - 1
                    p.wait("vector", ct, kr_done)
                    stg = wstg[c % 2]
                    if c >= 2:
                        SY.wait("vector", _MS[f"wo_c{c - 2}_dma"])

                    def v_ev(h, stg=stg):
                        h.tensor_copy(stg[:, 0:512], psA[:, 0:512])
                        return h.tensor_copy(stg[:, 512:1024], psA[:, 512:1024])
                    p.op("vector", v_ev, cv)
                    p.wait("sync", cv, cv.n)
                    _MS[f"wo_c{c}_dma"] = SY.issue(
                        lambda h, c=c, stg=stg: h.dma_start(out=E.wop_d.ap()[c],
                                                            in_=stg[:, :]))
                viter_done[i] = cv.n
        wo_free_t = ct.n

        prev_dma = {}
        for ti, t in enumerate("qkv"):
            with ExitStack() as Swt:
                wnat = [Swt.enter_context(
                    nc.sbuf_tensor(f"wnat_{t}{i}", [128, DM], BF16))
                    for i in range(NEC)]
                p.wait("gpsimd", ct, wo_free_t if ti == 0 else _MS[f"wprep_t_{ti - 1}"])
                wn_tok = {}
                for mc in range(NEC):
                    wn_tok[mc] = GP.issue(lambda h, t=t, mc=mc: h.dma_start(
                        out=wnat[mc][:, :],
                        in_=E.w_d[t].ap()[mc * 128:(mc + 1) * 128, :]))
                for mc in range(NEC):
                    GP.wait("tensor", wn_tok[mc])

                viter_done = {-1: cv.n}
                for i in range(H + 2):
                    p.wait("tensor", cv, viter_done[i - 1])
                    vbc_done = kr_done = tr_done = None
                    if i < H:
                        def t_vbc8(h, ti=ti, hd=i):
                            last = None
                            for mc in range(NEC):
                                last = h.matmul(
                                    psV[:, mc * 128:(mc + 1) * 128],
                                    E.sel2[0:2, :],
                                    _bc2(E.amix[0:2,
                                                ti * 256 + mc * 32 + 2 * hd:
                                                ti * 256 + mc * 32 + 2 * hd + 2],
                                         64),
                                    start=True, stop=True)
                            return last
                        p.op("tensor", t_vbc8, ct)
                        vbc_done = ct.n
                    if 0 <= i - 2 < H:
                        def t_tr(h):
                            last = None
                            for ec in range(NEC):
                                last = h.transpose(
                                    psT[:, ec * 128:(ec + 1) * 128],
                                    wtmp[:, ec * 128:(ec + 1) * 128],
                                    E.identb[:, :])
                            return last
                        p.op("tensor", t_tr, ct)
                        tr_done = ct.n
                    if 0 <= i - 1 < H:
                        def t_kr8(h, sb=((i - 1) % 2) * 8):
                            last = None
                            for mc in range(NEC):
                                h.matmul(psA[:, 0:512], stile[sb + mc][:, :],
                                         wnat[mc][:, 0:512],
                                         start=(mc == 0), stop=(mc == NEC - 1))
                                last = h.matmul(psA[:, 512:1024],
                                                stile[sb + mc][:, :],
                                                wnat[mc][:, 512:1024],
                                                start=(mc == 0),
                                                stop=(mc == NEC - 1))
                            return last
                        p.op("tensor", t_kr8, ct)
                        kr_done = ct.n

                    if i < H:
                        p.wait("vector", ct, vbc_done)

                        def v_st8(h, sb=(i % 2) * 8):
                            last = None
                            for mc in range(NEC):
                                last = h.tensor_tensor(
                                    stile[sb + mc][:, :], E.diag64[:, :],
                                    psV[:, mc * 128:(mc + 1) * 128],
                                    op=ALU.mult)
                            return last
                        p.op("vector", v_st8, cv)
                    if 0 <= i - 2 < H:
                        hd = i - 2
                        p.wait("vector", ct, tr_done)
                        stg = wstg[hd % 2]
                        k2 = (ti * H + hd) - 2
                        if k2 >= 0:
                            SY.wait("vector", prev_dma[k2])
                        else:
                            SY.wait("vector", _MS[f"wo_c{H + k2}_dma"])

                        def v_stg(h, stg=stg):
                            return h.tensor_copy(stg[:, :], psT[:, :])
                        p.op("vector", v_stg, cv)
                        p.wait("sync", cv, cv.n)
                        tok = SY.issue(
                            lambda h, ti=ti, hd=hd, stg=stg: h.dma_start(
                                out=E.wp_d.ap()[ti, hd], in_=stg[:, :]))
                        prev_dma[ti * H + hd] = tok
                        _MS[f"w{t}_h{hd}_dma"] = tok
                    if 0 <= i - 1 < H:
                        p.wait("vector", ct, kr_done)

                        def v_ev(h):
                            h.tensor_copy(wtmp[:, 0:512], psA[:, 0:512])
                            return h.tensor_copy(wtmp[:, 512:1024],
                                                 psA[:, 512:1024])
                        p.op("vector", v_ev, cv)
                    viter_done[i] = cv.n
                _MS[f"wprep_t_{ti}"] = ct.n

        _MS["p0_v"] = cv.n
        _MS["p0_t"] = ct.n


def _phase1(p, E):
    nc = E.nc
    ct, cv, cs, cg = E.ct, E.cv, E.cs, E.cg
    SY, GP = E.SY, E.GP
    avd = {}
    E.pend_tail = None
    E.attn_prev = None
    with ExitStack() as S1:
        al = S1.enter_context
        wsl = [[al(nc.sbuf_tensor(f"wsl{t}{b}", [128, DM], BF16)) for b in range(2)]
               for t in range(3)]
        qh = al(nc.sbuf_tensor("qh", [128, S], BF16))
        kh = al(nc.sbuf_tensor("kh", [128, S], BF16))
        vh = al(nc.sbuf_tensor("vh", [128, S], BF16))
        qsw = al(nc.sbuf_tensor("qsw", [128, S], BF16))
        vtok = al(nc.sbuf_tensor("vtok", [128, S], BF16))
        probs = al(nc.sbuf_tensor("probs", [128, 3 * 2048], BF16))
        ccv = al(nc.sbuf_tensor("ccv", [128, 512], F32))
        cinv = al(nc.sbuf_tensor("cinv", [128, 512], F32))
        paF = al(nc.psum_tensor("p1F", [128, 3584], F32))
        paB = al(nc.psum_tensor("p1B", [128, 1024], BF16))

        p.wait("vector", cv, _MS["p0_v"])

        def v_cc0(h):
            return h.memset(ccv[:, :], 1.0)
        p.op("vector", v_cc0, cv)

        tgt = {0: qh, 1: kh, 2: vh}
        for hd in range(N_HEADS):
            buf = hd % 2
            wsl_toks = []
            for ti in range(3):
                SY.wait("sync", _MS[f"w{'qkv'[ti]}_h{hd}_dma"])
                if hd >= 2:
                    p.wait("sync", ct, _MS[f"proj_t_h{hd - 2}"])
                wsl_toks.append(SY.issue(lambda h, ti=ti, hd=hd, buf=buf: h.dma_start(
                    out=wsl[ti][buf][:, :], in_=E.wp_d.ap()[ti, hd])))

            for tok in wsl_toks:
                SY.wait("tensor", tok)
            pevq = []
            for ti in range(3):
                dst = tgt[ti]
                for blk in range(NB):
                    pslot = 1024 + (blk % 2) * 512
                    if len(pevq) >= 2:
                        p.wait("tensor", cv, pevq[-2])
                    else:
                        p.wait("tensor", cv, cv.n)

                    def t_proj(h, ti=ti, blk=blk, pslot=pslot, buf=buf):
                        last = None
                        for ec in range(NEC):
                            last = h.matmul(paF[:, pslot:pslot + 512],
                                            wsl[ti][buf][:, ec * 128:(ec + 1) * 128],
                                            E.xbf[ec][:, blk * 512:(blk + 1) * 512],
                                            start=(ec == 0), stop=(ec == NEC - 1))
                        return last
                    p.op("tensor", t_proj, ct)
                    p.wait("vector", ct, ct.n)

                    def v_proj(h, dst=dst, blk=blk, pslot=pslot):
                        return h.tensor_copy(dst[:, blk * 512:(blk + 1) * 512],
                                             paF[:, pslot:pslot + 512])
                    p.op("vector", v_proj, cv)
                    pevq.append(cv.n)
            _MS[f"proj_t_h{hd}"] = ct.n
            proj_v = cv.n
            p.wait("tensor", cv, proj_v)

            p.wait("gpsimd", cv, proj_v)
            sw1 = GP.issue(lambda h: h.dma_start(out=qsw[0:64, :],
                                                 in_=qh[64:128, :]))
            sw2 = GP.issue(lambda h: h.dma_start(out=qsw[64:128, :],
                                                 in_=qh[0:64, :]))

            for grp in range(4):
                def t_vtr(h, grp=grp):
                    last = None
                    for c4 in range(4):
                        c = grp * 4 + c4
                        last = h.transpose(paB[:, c4 * 128:(c4 + 1) * 128],
                                           vh[:, c * 128:(c + 1) * 128],
                                           E.identb[:, :])
                    return last
                p.op("tensor", t_vtr, ct)
                p.wait("vector", ct, ct.n)

                def v_vtr(h, grp=grp):
                    return h.tensor_copy(vtok[:, grp * 512:(grp + 1) * 512],
                                         paB[:, 0:512])
                p.op("vector", v_vtr, cv)
                p.wait("tensor", cv, cv.n)
            vtok_v = cv.n

            if hd == 0 and "qkvh0" in TAPS:
                GP.wait("sync", sw1)
                GP.wait("sync", sw2)
                _tap_sbuf(p, E, cv, vtok_v, "qkvh0", [qh, kh, vh, qsw, vtok])

            GP.wait("tensor", sw1)
            GP.wait("tensor", sw2)
            p.wait("scalar", cv, vtok_v)

            def emit_tail():
                if E.pend_tail is None:
                    return
                t_nb, t_hd, av_last = E.pend_tail
                E.pend_tail = None
                p.wait("vector", ct, av_last)

                def v_recip(h):
                    return h.reciprocal(ccv[64:66, :], paF[64:66, 3072:3584])
                p.op("vector", v_recip, cv)
                p.wait("tensor", cv, cv.n)

                def t_cinv(h):
                    return h.matmul(paF[:, 3072:3584], E.selc[:, :], ccv[:, :],
                                    start=True, stop=True)
                p.op("tensor", t_cinv, ct)
                p.wait("vector", ct, ct.n)

                def v_attn(h, nb=t_nb, hd=t_hd):
                    h.tensor_copy(cinv[:, :], paF[:, 3072:3584])
                    h.drain()
                    h.tensor_tensor(E.attnT[hd][0:64, nb * 512:(nb + 1) * 512],
                                    paF[0:64, 2048:2560], cinv[0:64, :],
                                    op=ALU.mult)
                    return h.tensor_tensor(
                        E.attnT[hd][64:128, nb * 512:(nb + 1) * 512],
                        paF[64:128, 2560:3072], cinv[64:128, :], op=ALU.mult)
                p.op("vector", v_attn, cv)
                E.attn_prev = cv.n

            for nb in range(NB):
                nkc = 4 * nb + 4
                ready = {}

                def emit_av(c, nkc=nkc):
                    slot = c % 3
                    rc, rv = ready[c]
                    p.wait("tensor", rc, rv)
                    if c == 0 and E.attn_prev is not None:
                        p.wait("tensor", cv, E.attn_prev)

                    def t_av(h, c=c, nkc=nkc, slot=slot):
                        last = None
                        for pp in range(2):
                            st = (c == 0) and (pp == 0)
                            sp = (c == nkc - 1) and (pp == 1)
                            vs = vtok[:, c * 128 + pp * 64: c * 128 + pp * 64 + 64]
                            o0 = slot * 2048 + (0 if pp == 0 else 3) * 512
                            mv0 = probs[:, o0:o0 + 512]
                            h.matmul(paF[0:64, 2048:2560], vs, mv0,
                                     start=st, stop=sp, tile_position=(0, 0))
                            h.matmul(paF[64:128, 3072:3584], E.oc64[:, 1:65], mv0,
                                     start=st, stop=(c == nkc - 1) and (pp == 1)
                                     and False, tile_position=(0, 64))
                            o1 = slot * 2048 + (2 if pp == 0 else 1) * 512
                            mv1 = probs[:, o1:o1 + 512]
                            h.matmul(paF[64:128, 2560:3072], vs, mv1,
                                     start=st, stop=sp, tile_position=(0, 64))
                            last = h.matmul(paF[64:128, 3072:3584],
                                            E.oc64[:, 0:64], mv1,
                                            start=False, stop=sp,
                                            tile_position=(0, 64))
                        return last
                    p.op("tensor", t_av, ct)
                    avd[(hd, nb, c)] = ct.n

                eh1 = {}
                eh2 = {}
                for c in range(nkc):
                    slot = c % 3
                    if c > 0:
                        p.wait("tensor", cs, eh1[c - 1])

                    def t_sc1(h, c=c, nb=nb):
                        h.matmul(paF[:, 0:512], kh[0:64, c * 128:(c + 1) * 128],
                                 qh[0:64, nb * 512:(nb + 1) * 512],
                                 start=True, stop=True, tile_position=(0, 0))
                        return h.matmul(paF[:, 512:1024],
                                        kh[64:128, c * 128:(c + 1) * 128],
                                        qh[64:128, nb * 512:(nb + 1) * 512],
                                        start=True, stop=True,
                                        tile_position=(64, 0))
                    p.op("tensor", t_sc1, ct)
                    sc1_t = ct.n

                    if c > 0:
                        p.wait("tensor", cs, eh2[c - 1])

                    def t_sc2(h, c=c, nb=nb):
                        h.matmul(paF[:, 1024:1536], kh[0:64, c * 128:(c + 1) * 128],
                                 qsw[0:64, nb * 512:(nb + 1) * 512],
                                 start=True, stop=True, tile_position=(0, 0))
                        return h.matmul(paF[:, 1536:2048],
                                        kh[64:128, c * 128:(c + 1) * 128],
                                        qsw[64:128, nb * 512:(nb + 1) * 512],
                                        start=True, stop=True,
                                        tile_position=(64, 0))
                    p.op("tensor", t_sc2, ct)
                    sc2_t = ct.n

                    p.wait("scalar", ct, sc1_t)
                    if c >= 3:
                        p.wait("scalar", ct, avd[(hd, nb, c - 3)])

                    def s_exp1(h, slot=slot):
                        return h.activation(
                            out=probs[:, slot * 2048: slot * 2048 + 1024],
                            in_=paF[:, 0:1024], func=AF.Exp)
                    p.op("scalar", s_exp1, cs)
                    eh1[c] = cs.n

                    p.wait("scalar", ct, sc2_t)

                    def s_exp2(h, slot=slot):
                        return h.activation(
                            out=probs[:, slot * 2048 + 1024: slot * 2048 + 2048],
                            in_=paF[:, 1024:2048], func=AF.Exp)
                    p.op("scalar", s_exp2, cs)
                    eh2[c] = cs.n

                    j = c - 4 * nb
                    if j >= 0:
                        p.wait("vector", cs, eh2[c])

                        def v_mask(h, slot=slot, j=j):
                            return h.tensor_tensor(
                                probs[:, slot * 2048:(slot + 1) * 2048],
                                probs[:, slot * 2048:(slot + 1) * 2048],
                                E.masks[:, j * 2048:(j + 1) * 2048], op=ALU.mult)
                        p.op("vector", v_mask, cv)
                        ready[c] = (cv, cv.n)
                    else:
                        ready[c] = (cs, eh2[c])

                    if c == 1:
                        emit_tail()
                    if c > 0:
                        emit_av(c - 1)
                emit_av(nkc - 1)
                E.pend_tail = (nb, hd, avd[(hd, nb, nkc - 1)])

            if hd == 0 and "attnh0" in TAPS:
                emit_tail()
                _tap_sbuf(p, E, cv, cv.n, "attnh0", [E.attnT[0]])

        emit_tail()
        _MS["p1_v"] = cv.n
        _MS["p1_t"] = ct.n


def _phase2(p, E):
    nc = E.nc
    ct, cv, cs, cg = E.ct, E.cv, E.cs, E.cg
    SY, GP = E.SY, E.GP
    with ExitStack() as S2:
        al = S2.enter_context
        wos = [al(nc.sbuf_tensor(f"wos{i}", [128, DM], BF16)) for i in range(2)]
        x1 = [al(nc.sbuf_tensor(f"x1_{i}", [128, S], BF16)) for i in range(NEC)]
        x1s = [al(nc.sbuf_tensor(f"x1s{i}", [128, S], BF16)) for i in range(2)]
        stat = al(nc.sbuf_tensor("stat", [1, 2 * S], F32))
        mbc = al(nc.sbuf_tensor("mbc", [128, S], F32))
        rbc = al(nc.sbuf_tensor("rbc", [128, S], F32))
        p2F = al(nc.psum_tensor("p2F", [128, 4096], F32))
        p2S = p2F

        p.wait("vector", cv, _MS["p1_v"])
        p.wait("tensor", cv, _MS["p1_v"])
        p.wait("sync", cv, _MS["p1_v"])
        for fp in range(4):
            for oc in range(H):
                b = oc % 2
                if fp * H + oc >= 2:
                    k2 = fp * H + oc - 2
                    p.wait("sync", ct, _MS[f"wo_mm_{k2 // H}_{k2 % H}"])

                wtok = SY.issue(lambda h, oc=oc, b=b: h.dma_start(
                    out=wos[b][:, :], in_=E.wop_d.ap()[oc]))
                SY.wait("tensor", wtok)
                if oc == 0 and fp > 0:
                    p.wait("tensor", cv, _MS[f"x1ev_{fp - 1}"])

                def t_wo(h, fp=fp, oc=oc, b=b):
                    last = None
                    for half in range(2):
                        fc = 2 * fp + half
                        for nb in range(NB):
                            last = h.matmul(
                                p2F[:, half * 2048 + nb * 512:
                                    half * 2048 + (nb + 1) * 512],
                                wos[b][:, fc * 128:(fc + 1) * 128],
                                E.attnT[oc][:, nb * 512:(nb + 1) * 512],
                                start=(oc == 0), stop=(oc == H - 1))
                    return last
                p.op("tensor", t_wo, ct)
                _MS[f"wo_mm_{fp}_{oc}"] = ct.n
            p.wait("vector", ct, ct.n)

            def v_x1(h, fp=fp):
                last = None
                for half in range(2):
                    fc = 2 * fp + half
                    last = h.tensor_tensor(x1[fc][:, :],
                                           p2F[:, half * 2048:(half + 1) * 2048],
                                           E.xbf[fc][:, :], op=ALU.add)
                return last
            p.op("vector", v_x1, cv)
            _MS[f"x1ev_{fp}"] = cv.n
        x1_v = cv.n
        _tap_sbuf(p, E, cv, x1_v, "x1", x1)

        p.wait("scalar", cv, x1_v)
        p.wait("tensor", cv, x1_v)
        for ec in range(NEC):
            def t_sum(h, ec=ec):
                last = None
                for nb in range(NB):
                    last = h.matmul(p2S[0:1, nb * 512:(nb + 1) * 512],
                                    E.onesb[:, 0:1],
                                    x1[ec][:, nb * 512:(nb + 1) * 512],
                                    start=(ec == 0), stop=(ec == NEC - 1))
                return last
            p.op("tensor", t_sum, ct)
        sum_t = ct.n
        sqmm_done = {}
        for ec in range(NEC):
            b = ec % 2
            if ec >= 2:
                p.wait("scalar", ct, sqmm_done[ec - 2])

            def s_sq(h, ec=ec, b=b):
                return h.activation(out=x1s[b][:, :], in_=x1[ec][:, :],
                                    func=AF.Square)
            p.op("scalar", s_sq, cs)
            p.wait("tensor", cs, cs.n)

            def t_sq(h, ec=ec, b=b):
                last = None
                for nb in range(NB):
                    last = h.matmul(p2S[0:1, S + nb * 512: S + (nb + 1) * 512],
                                    E.onesb[:, 0:1],
                                    x1s[b][:, nb * 512:(nb + 1) * 512],
                                    start=(ec == 0), stop=(ec == NEC - 1))
                return last
            p.op("tensor", t_sq, ct)
            sqmm_done[ec] = ct.n
        p.wait("vector", ct, ct.n)

        def v_stat(h):
            h.tensor_copy(stat[0:1, 0:S], p2S[0:1, 0:S])
            return h.tensor_copy(stat[0:1, S:2 * S], p2S[0:1, S:2 * S])
        p.op("vector", v_stat, cv)
        p.wait("tensor", cv, cv.n)

        def v_ln1a(h):
            h.drain()
            h.tensor_scalar_mul(stat[0:1, 0:S], stat[0:1, 0:S], 1.0 / DM)
            h.tensor_scalar_mul(stat[0:1, S:2 * S], stat[0:1, S:2 * S], 1.0 / DM)
            h.drain()
            h.tensor_tensor(mbc[0:1, :], stat[0:1, 0:S], stat[0:1, 0:S],
                            op=ALU.mult)
            h.drain()
            h.tensor_tensor(stat[0:1, S:2 * S], stat[0:1, S:2 * S],
                            mbc[0:1, :], op=ALU.subtract)
            h.drain()
            return h.tensor_scalar_add(stat[0:1, S:2 * S],
                                       stat[0:1, S:2 * S], EPS)
        p.op("vector", v_ln1a, cv)
        p.wait("scalar", cv, cv.n)

        def s_rstd(h):
            h.activation(out=mbc[0:1, :], in_=stat[0:1, S:2 * S], func=AF.Ln)
            h.drain()
            return h.activation(out=stat[0:1, S:2 * S],
                                in_=mbc[0:1, :], func=AF.Exp, scale=-0.5)
        p.op("scalar", s_rstd, cs)
        p.wait("tensor", cs, cs.n)

        def t_bc(h):
            last = None
            for blk in range(4):
                last = h.matmul(p2F[:, blk * 512:(blk + 1) * 512], E.ones1f[0:1, :],
                                stat[0:1, blk * 512:(blk + 1) * 512],
                                start=True, stop=True)
                last = h.matmul(p2F[:, 2048 + blk * 512: 2048 + (blk + 1) * 512],
                                E.ones1f[0:1, :],
                                stat[0:1, S + blk * 512: S + (blk + 1) * 512],
                                start=True, stop=True)
            return last
        p.op("tensor", t_bc, ct)
        p.wait("vector", ct, ct.n)

        def v_bc(h):
            last = None
            for blk in range(4):
                last = h.tensor_copy(mbc[:, blk * 512:(blk + 1) * 512],
                                     p2F[:, blk * 512:(blk + 1) * 512])
                last = h.tensor_copy(rbc[:, blk * 512:(blk + 1) * 512],
                                     p2F[:, 2048 + blk * 512: 2048 + (blk + 1) * 512])
            return last
        p.op("vector", v_bc, cv)
        p.raw("vector", lambda h: h.drain())

        for ec in range(NEC):
            def v_ap1(h, ec=ec):
                return h.tensor_tensor(x1[ec][:, :], x1[ec][:, :], mbc[:, :],
                                       op=ALU.subtract)
            p.op("vector", v_ap1, cv)
            p.raw("vector", lambda h: h.drain())

            def v_ap2(h, ec=ec):
                return h.tensor_tensor(E.xln[ec][:, 2:2 + S], x1[ec][:, :],
                                       rbc[:, :], op=ALU.mult)
            p.op("vector", v_ap2, cv)
        _MS["p2_v"] = cv.n
        _MS["p2_t"] = ct.n
        _MS["p2_s"] = cs.n
        _tap_sbuf(p, E, cv, cv.n, "xln", E.xln)


def _phase3(p, E):
    nc = E.nc
    ct, cv, cs, cg = E.ct, E.cv, E.cs, E.cg
    SY, GP = E.SY, E.GP

    def xs(ec, lo, n):
        return E.xln[ec][:, 2 + lo: 2 + lo + n]

    with ExitStack() as S3:
        al = S3.enter_context
        rwt = al(nc.sbuf_tensor("rwt", [128, 128], BF16))
        combT = al(nc.sbuf_tensor("combT", [16, 2048], F32))
        cstage = al(nc.sbuf_tensor("cstage", [1, 2048], BF16))
        SC = ExitStack()
        cal = SC.enter_context
        cwT = [cal(nc.sbuf_tensor(f"cwT{k}", [128, 4096], BF16)) for k in range(3)]
        sgate = [cal(nc.sbuf_tensor(f"sgate{i}", [128, S], BF16))
                 for i in range(NIC)]
        smid = [cal(nc.sbuf_tensor(f"smid{i}", [128, S], BF16))
                for i in range(NIC)]
        supt = cal(nc.sbuf_tensor("supt", [128, 4096], BF16))
        sdnt = cal(nc.sbuf_tensor("sdnt", [128, 4096], BF16))
        SR = ExitStack()
        rl = SR.enter_context(nc.sbuf_tensor("rl", [16, 2048], F32))
        rt = SR.enter_context(nc.sbuf_tensor("rt", [128, 256], F32))
        comb = SR.enter_context(nc.sbuf_tensor("comb", [128, 256], F32))
        rsc = SR.enter_context(nc.sbuf_tensor("rsc", [128, 64], F32))
        eq1 = SR.enter_context(nc.sbuf_tensor("eq1", [128, 256], F32))
        eq2 = SR.enter_context(nc.sbuf_tensor("eq2", [128, 256], F32))
        eq3 = SR.enter_context(nc.sbuf_tensor("eq3", [128, 256], F32))
        c15b = SR.enter_context(nc.sbuf_tensor("c15b", [128, S], BF16))

        p.wait("gpsimd", cv, _MS["p2_v"])
        p.wait("gpsimd", cs, _MS["p2_s"])
        with ExitStack() as SCW:
            cnat = [SCW.enter_context(nc.sbuf_tensor(f"cnat{i}", [128, DM * 3],
                                                     BF16)) for i in range(NIC)]
            p3Bt = SCW.enter_context(nc.psum_tensor("p3Bt", [128, 1024], BF16))
            cw_toks = []
            for ic in range(NIC):
                cw_toks.append(GP.issue(lambda h, ic=ic: h.dma_start(
                    out=cnat[ic][:, :],
                    in_=E.cw_d.ap()[ic * 128:(ic + 1) * 128])))
            cw_toks.append(GP.issue(lambda h: h.dma_start(
                out=supt[:, :],
                in_=bass.AP(E.sup_d, 0, [[512, 128], [65536, 8], [1, 512]]))))
            cw_toks.append(GP.issue(lambda h: h.dma_start(
                out=sdnt[:, :],
                in_=bass.AP(E.sdn_d, 0, [[1024, 128], [131072, 4], [1, 1024]]))))
            rwtok = GP.issue(lambda h: h.dma_start(
                out=rwt[:, :],
                in_=bass.AP(E.rw_d, 0, [[16, 128], [2048, 8], [1, 16]])))

            p.wait("tensor", cv, _MS["p2_v"])
            p.wait("tensor", cs, _MS["p2_s"])
            for tok in cw_toks:
                GP.wait("tensor", tok)
            for k in range(3):
                for hc in range(NEC):
                    def t_cwt(h, k=k, hc=hc):
                        last = None
                        for ic in range(NIC):
                            base = cnat[ic][:, :]
                            src = bass.AP(base.tensor, hc * 384 + k,
                                          [[DM * 3, 128], [3, 128]])
                            last = h.transpose(p3Bt[:, ic * 128:(ic + 1) * 128],
                                               src, E.identb[:, :])
                        return last
                    p.op("tensor", t_cwt, ct)
                    p.wait("vector", ct, ct.n)

                    def v_cwt(h, k=k, hc=hc):
                        return h.tensor_copy(cwT[k][:, hc * 512:(hc + 1) * 512],
                                             p3Bt[:, 0:512])
                    p.op("vector", v_cwt, cv)
                    p.wait("tensor", cv, cv.n)
        cwt_done_v = cv.n

        p3F = al(nc.psum_tensor("p3F", [128, 4096], F32))
        GP.wait("tensor", rwtok)
        p.wait("tensor", cv, cwt_done_v)

        def t_router(h):
            last = None
            for nb in range(NB):
                for ec in range(NEC):
                    last = h.matmul(p3F[0:16, nb * 512:(nb + 1) * 512],
                                    rwt[:, ec * 16:(ec + 1) * 16],
                                    xs(ec, nb * 512, 512),
                                    start=(ec == 0), stop=(ec == NEC - 1))
            return last
        p.op("tensor", t_router, ct)
        p.wait("vector", ct, ct.n)

        def v_rbias(h):
            return h.tensor_scalar(rl[0:16, :], p3F[0:16, 0:2048],
                                   E.fsc[0:16, 0:1], None, op0=ALU.add)
        p.op("vector", v_rbias, cv)
        rl_done = cv.n
        _tap_sbuf(p, E, cv, rl_done, "logits", [rl])

        p.wait("tensor", cv, rl_done)

        def t_rtr(h):
            last = None
            for tc in range(NTC):
                last = h.transpose(p3F[:, 2048 + tc * 16: 2048 + (tc + 1) * 16],
                                   rl[0:16, tc * 128:(tc + 1) * 128],
                                   E.identf[0:16, 0:16])
            return last
        p.op("tensor", t_rtr, ct)
        p.wait("vector", ct, ct.n)

        def v_rt(h):
            return h.tensor_copy(rt[:, :], p3F[:, 2048:2304])
        p.op("vector", v_rt, cv)

        def s_sm(h):
            return h.activation(out=comb[:, :], in_=rt[:, :], func=AF.Exp)
        p.wait("scalar", cv, cv.n)
        p.op("scalar", s_sm, cs)
        p.wait("vector", cs, cs.n)

        def v_sum(h):
            h.reduce_sum(rsc[:, 0:16],
                         comb[:, :].rearrange("p (t e) -> p t e", e=16),
                         axis=mybir.AxisListType.X)
            h.drain()
            h.reciprocal(rsc[:, 16:32], rsc[:, 0:16])
            h.drain()
            return h.tensor_tensor(
                comb[:, :].rearrange("p (t e) -> p t e", e=16),
                comb[:, :].rearrange("p (t e) -> p t e", e=16),
                rsc[:, 16:32].unsqueeze(2).to_broadcast((128, 16, 16)),
                op=ALU.mult)
        p.op("vector", v_sum, cv)

        def v_top8(h):
            last = None
            for tc in range(NTC):
                last = h.max(out=rsc[:, 32 + 0:32 + 8] if False else
                             rt[:, tc * 16: tc * 16 + 8],
                             in_=comb[:, tc * 16:(tc + 1) * 16])
            return last
        p.raw("vector", lambda h: h.drain())
        p.op("vector", v_top8, cv)
        p.raw("vector", lambda h: h.drain())

        def v_comb(h):
            m1 = rt[:, :].rearrange("p (t e) -> p t e", e=16)[:, :, 0:1]
            m2 = rt[:, :].rearrange("p (t e) -> p t e", e=16)[:, :, 1:2]
            pr = comb[:, :].rearrange("p (t e) -> p t e", e=16)
            h.tensor_tensor(rsc[:, 32:48].rearrange("p (t o) -> p t o", o=1),
                            m1, m2, op=ALU.add)
            h.drain()
            h.tensor_scalar_add(rsc[:, 32:48], rsc[:, 32:48], 1e-6)
            h.drain()
            h.reciprocal(rsc[:, 48:64], rsc[:, 32:48])
            h.drain()
            h.tensor_tensor(rsc[:, 0:16].rearrange("p (t o) -> p t o", o=1), m1,
                            rsc[:, 48:64].rearrange("p (t o) -> p t o", o=1),
                            op=ALU.mult)
            h.tensor_tensor(rsc[:, 16:32].rearrange("p (t o) -> p t o", o=1), m2,
                            rsc[:, 48:64].rearrange("p (t o) -> p t o", o=1),
                            op=ALU.mult)
            h.drain()
            h.tensor_tensor(eq1[:, :].rearrange("p (t e) -> p t e", e=16), pr,
                            m1.to_broadcast((128, 16, 16)), op=ALU.is_equal)
            h.tensor_tensor(eq2[:, :].rearrange("p (t e) -> p t e", e=16), pr,
                            m2.to_broadcast((128, 16, 16)), op=ALU.is_equal)
            h.drain()
            h.tensor_tensor(eq3[:, :], eq2[:, :], eq1[:, :], op=ALU.mult)
            h.drain()
            h.tensor_tensor(eq2[:, :], eq2[:, :], eq3[:, :], op=ALU.subtract)
            h.drain()
            h.tensor_tensor(eq1[:, :].rearrange("p (t e) -> p t e", e=16),
                            eq1[:, :].rearrange("p (t e) -> p t e", e=16),
                            rsc[:, 0:16].unsqueeze(2).to_broadcast((128, 16, 16)),
                            op=ALU.mult)
            h.tensor_tensor(eq2[:, :].rearrange("p (t e) -> p t e", e=16),
                            eq2[:, :].rearrange("p (t e) -> p t e", e=16),
                            rsc[:, 16:32].unsqueeze(2).to_broadcast((128, 16, 16)),
                            op=ALU.mult)
            h.drain()
            return h.tensor_tensor(comb[:, :], eq1[:, :], eq2[:, :], op=ALU.add)
        p.op("vector", v_comb, cv)
        comb_done = cv.n
        _tap_sbuf(p, E, cv, comb_done, "combine", [comb])

        p.wait("tensor", cv, comb_done)

        def t_ctr(h):
            last = None
            for tc in range(NTC):
                last = h.transpose(p3F[0:16, 2048 + tc * 128: 2048 + (tc + 1) * 128],
                                   comb[:, tc * 16:(tc + 1) * 16], E.identf[:, :])
            return last
        p.op("tensor", t_ctr, ct)
        p.wait("vector", ct, ct.n)

        def v_ct(h):
            return h.tensor_copy(combT[0:16, :], p3F[0:16, 2048:4096])
        p.op("vector", v_ct, cv)
        combT_done = cv.n

        p.wait("gpsimd", cv, combT_done)

        c15tok = GP.issue(lambda h: h.dma_start(out=cstage[0:1, :],
                                                in_=combT[15:16, :]))
        GP.wait("tensor", c15tok)

        def t_c15b(h):
            last = None
            for blk in range(4):
                last = h.matmul(p3F[:, blk * 512:(blk + 1) * 512], E.ones1b[0:1, :],
                                cstage[0:1, blk * 512:(blk + 1) * 512],
                                start=True, stop=True)
            return last
        p.op("tensor", t_c15b, ct)
        p.wait("vector", ct, ct.n)

        def v_c15b(h):
            last = None
            for blk in range(4):
                last = h.tensor_copy(c15b[:, blk * 512:(blk + 1) * 512],
                                     p3F[:, blk * 512:(blk + 1) * 512])
            return last
        p.op("vector", v_c15b, cv)
        p.raw("vector", lambda h: h.drain())
        for ec in range(NEC):
            def v_acc0(h, ec=ec):
                return h.tensor_tensor(E.acc[ec][:, :], xs(ec, 0, S), c15b[:, :],
                                       op=ALU.mult)
            p.op("vector", v_acc0, cv)
        acc_init = cv.n
        SR.close()

        if True:
            p.wait("tensor", cv, acc_init)
            for ic in range(NIC):
                for nb in range(NB):
                    kk = ic * NB + nb
                    if kk >= 4:
                        p.wait("tensor", cs, _MS[f"sg_{kk - 4}"])

                    def t_conv(h, ic=ic, nb=nb, kk=kk):
                        last = None
                        first = True
                        pb = (kk % 4) * 512
                        for k in range(3):
                            for hc in range(NEC):
                                last = h.matmul(
                                    p3F[:, pb:pb + 512],
                                    cwT[k][:, hc * 512 + ic * 128:
                                           hc * 512 + ic * 128 + 128],
                                    E.xln[hc][:, nb * 512 + k: nb * 512 + k + 512],
                                    start=first, stop=(k == 2 and hc == NEC - 1))
                                first = False
                        return last
                    p.op("tensor", t_conv, ct)
                    p.wait("scalar", ct, ct.n)

                    def s_sg(h, ic=ic, nb=nb, kk=kk):
                        pb = (kk % 4) * 512
                        return h.activation(
                            out=sgate[ic][:, nb * 512:(nb + 1) * 512],
                            in_=p3F[:, pb:pb + 512], func=AF.Silu)
                    p.op("scalar", s_sg, cs)
                    _MS[f"sg_{kk}"] = cs.n
            _tap_sbuf(p, E, cs, cs.n, "sgate", sgate)

            for ic in range(NIC):
                for nb in range(NB):
                    kk = ic * NB + nb
                    if kk >= 2:
                        p.wait("tensor", cv, _MS[f"smid_{kk - 2}"])

                    def t_sup(h, ic=ic, nb=nb, kk=kk):
                        last = None
                        pb = 2048 + (kk % 2) * 512
                        for ec in range(NEC):
                            last = h.matmul(
                                p3F[:, pb:pb + 512],
                                supt[:, ec * 512 + ic * 128: ec * 512 + ic * 128 + 128],
                                xs(ec, nb * 512, 512),
                                start=(ec == 0), stop=(ec == NEC - 1))
                        return last
                    p.op("tensor", t_sup, ct)
                    p.wait("vector", ct, ct.n)
                    p.wait("vector", cs, _MS[f"sg_{kk}"])

                    def v_smid(h, ic=ic, nb=nb, kk=kk):
                        pb = 2048 + (kk % 2) * 512
                        h.tensor_copy(smid[ic][:, nb * 512:(nb + 1) * 512],
                                      p3F[:, pb:pb + 512])
                        h.drain()
                        return h.tensor_tensor(
                            smid[ic][:, nb * 512:(nb + 1) * 512],
                            smid[ic][:, nb * 512:(nb + 1) * 512],
                            sgate[ic][:, nb * 512:(nb + 1) * 512], op=ALU.mult)
                    p.op("vector", v_smid, cv)
                    _MS[f"smid_{kk}"] = cv.n
            smid_done = cv.n

            p.wait("tensor", cv, smid_done)
            p.wait("tensor", cv, acc_init)
            for fp in range(4):
                if fp > 0:
                    p.wait("tensor", cv, cv.n)

                def t_sdn(h, fp=fp):
                    last = None
                    for half in range(2):
                        fc = 2 * fp + half
                        for nb in range(NB):
                            for ic in range(NIC):
                                last = h.matmul(
                                    p3F[:, half * 2048 + nb * 512:
                                        half * 2048 + (nb + 1) * 512],
                                    sdnt[:, ic * 1024 + fc * 128:
                                         ic * 1024 + fc * 128 + 128],
                                    smid[ic][:, nb * 512:(nb + 1) * 512],
                                    start=(ic == 0), stop=(ic == NIC - 1))
                    return last
                p.op("tensor", t_sdn, ct)
                p.wait("vector", ct, ct.n)

                def v_sdn(h, fp=fp):
                    last = None
                    for half in range(2):
                        fc = 2 * fp + half
                        last = h.tensor_tensor(E.acc[fc][:, :], E.acc[fc][:, :],
                                               p3F[:, half * 2048:(half + 1) * 2048],
                                               op=ALU.add)
                    return last
                p.op("vector", v_sdn, cv)
        shared_done = cv.n
        _tap_sbuf(p, E, cv, shared_done, "acc_shared", E.acc)
        SC.close()

        with ExitStack() as SE:
            eal = SE.enter_context
            pool = [{
                "g": eal(nc.sbuf_tensor(f"egw{s_}", [128, 4096], BF16)),
                "u": eal(nc.sbuf_tensor(f"euw{s_}", [128, 4096], BF16)),
                "d": eal(nc.sbuf_tensor(f"edw{s_}", [128, 4096], BF16)),
            } for s_ in range(4)]
            z = [eal(nc.sbuf_tensor(f"z{i}", [128, 2048], BF16)) for i in range(3)]
            sgz = [eal(nc.sbuf_tensor(f"sgz{i}", [128, 2048], BF16)) for i in range(2)]
            cbc3 = [eal(nc.sbuf_tensor(f"cbc3_{i}", [128, S], BF16)) for i in range(3)]

            def load_expert(e):
                slot = pool[e % 4]
                if e >= 4:
                    p.wait("gpsimd", ct, _MS[f"edone_{e - 4}"])
                toks = [
                    GP.issue(lambda h, e=e, slot=slot: h.dma_start(
                        out=slot["g"][:, :],
                        in_=bass.AP(E.gw_d, e * DM * IDIM,
                                    [[IDIM, 128], [128 * IDIM, 8], [1, IDIM]]))),
                    GP.issue(lambda h, e=e, slot=slot: h.dma_start(
                        out=slot["u"][:, :],
                        in_=bass.AP(E.uw_d, e * DM * IDIM,
                                    [[IDIM, 128], [128 * IDIM, 8], [1, IDIM]]))),
                    GP.issue(lambda h, e=e, slot=slot: h.dma_start(
                        out=slot["d"][:, :],
                        in_=bass.AP(E.dw_d, e * IDIM * DM,
                                    [[DM, 128], [128 * DM, 4], [1, DM]]))),
                ]
                _MS[f"eload_{e}"] = toks

            def stage_cbc(e, slot, gate_cv):
                p.wait("gpsimd", cv, gate_cv)
                p.wait("gpsimd", ct, ct.n)
                tok = GP.issue(lambda h, e=e: h.dma_start(
                    out=cstage[0:1, :], in_=combT[e:e + 1, :]))
                GP.wait("tensor", tok)
                p.wait("tensor", cv, cv.n)

                def t_cb(h):
                    last = None
                    for blk in range(4):
                        last = h.matmul(p3F[:, blk * 512:(blk + 1) * 512],
                                        E.ones1b[0:1, :],
                                        cstage[0:1, blk * 512:(blk + 1) * 512],
                                        start=True, stop=True)
                    return last
                p.op("tensor", t_cb, ct)
                p.wait("vector", ct, ct.n)

                def v_cb(h, slot=slot):
                    last = None
                    for blk in range(4):
                        last = h.tensor_copy(cbc3[slot][:, blk * 512:(blk + 1) * 512],
                                             p3F[:, blk * 512:(blk + 1) * 512])
                    return last
                p.op("vector", v_cb, cv)
                _MS[f"cbc_{e}"] = cv.n

            p.wait("gpsimd", ct, ct.n)
            p.wait("gpsimd", cv, cv.n)
            p.wait("gpsimd", cs, cs.n)
            for e in range(4):
                load_expert(e)

            p.wait("tensor", cv, shared_done)
            prev_silu = cs.n
            prev_zt1 = cv.n
            prev_dn = cv.n
            zt2q = []
            for g in range(5):
                if g == 0:
                    for ei in range(3):
                        stage_cbc(3 * g + ei, ei, cv.n)
                for nb in range(NB):
                    for ei in range(3):
                        e = 3 * g + ei
                        slot = pool[e % 4]
                        for tok in _MS[f"eload_{e}"]:
                            GP.wait("tensor", tok)
                        p.wait("tensor", cs, prev_silu)
                        p.wait("tensor", cv, max(_MS[f"cbc_{3 * g + 2}"], prev_dn))

                        def t_g(h, nb=nb, slot=slot):
                            last = None
                            for ic in range(NIC):
                                for ec in range(NEC):
                                    last = h.matmul(
                                        p3F[:, ic * 512:(ic + 1) * 512],
                                        slot["g"][:, ec * 512 + ic * 128:
                                                  ec * 512 + ic * 128 + 128],
                                        xs(ec, nb * 512, 512),
                                        start=(ec == 0), stop=(ec == NEC - 1))
                            return last
                        p.op("tensor", t_g, ct)
                        tg_done = ct.n
                        p.wait("tensor", cv, prev_zt1)

                        def t_u(h, nb=nb, slot=slot):
                            last = None
                            for ic in range(NIC):
                                for ec in range(NEC):
                                    last = h.matmul(
                                        p3F[:, 2048 + ic * 512:
                                            2048 + (ic + 1) * 512],
                                        slot["u"][:, ec * 512 + ic * 128:
                                                  ec * 512 + ic * 128 + 128],
                                        xs(ec, nb * 512, 512),
                                        start=(ec == 0), stop=(ec == NEC - 1))
                            return last
                        p.op("tensor", t_u, ct)
                        tu_done = ct.n
                        p.wait("scalar", ct, tg_done)
                        sb = sgz[ei % 2]
                        if len(zt2q) >= 2:
                            p.wait("scalar", cv, zt2q[-2])

                        def s_silu(h, sb=sb):
                            return h.activation(out=sb[:, :], in_=p3F[:, 0:2048],
                                                func=AF.Silu)
                        p.op("scalar", s_silu, cs)
                        prev_silu = cs.n
                        p.wait("vector", ct, tu_done)

                        def v_z1(h, nb=nb, ei=ei):
                            zt = z[ei]
                            return h.tensor_tensor(
                                zt[:, :], p3F[:, 2048:4096],
                                cbc3[ei][:, nb * 512:(nb + 1) * 512]
                                .unsqueeze(1).to_broadcast((128, 4, 512)),
                                op=ALU.mult)
                        p.op("vector", v_z1, cv)
                        prev_zt1 = cv.n
                        p.wait("vector", cs, prev_silu)

                        def v_z2(h, ei=ei, sb=sb):
                            zt = z[ei]
                            h.drain()
                            return h.tensor_tensor(zt[:, :], zt[:, :], sb[:, :],
                                                   op=ALU.mult)
                        p.op("vector", v_z2, cv)
                        _MS[f"z_{e}_{nb}"] = cv.n
                        zt2q.append(cv.n)
                    zv = cv.n
                    p.wait("tensor", cv, zv)
                    for half4 in range(2):
                        def t_dn(h, g=g, nb=nb, half4=half4):
                            last = None
                            for fq in range(4):
                                fc = half4 * 4 + fq
                                first = True
                                for ei in range(3):
                                    e = 3 * g + ei
                                    slot = pool[e % 4]
                                    for ic in range(NIC):
                                        last = h.matmul(
                                            p3F[:, fq * 512 + half4 * 2048:
                                                fq * 512 + half4 * 2048 + 512],
                                            slot["d"][:, ic * 1024 + fc * 128:
                                                      ic * 1024 + fc * 128 + 128],
                                            z[ei][:, ic * 512:(ic + 1) * 512],
                                            start=first,
                                            stop=(ei == 2 and ic == NIC - 1))
                                        first = False
                            return last
                        p.op("tensor", t_dn, ct)
                    p.wait("vector", ct, ct.n)

                    def v_dn(h, nb=nb):
                        last = None
                        for fc in range(NEC):
                            last = h.tensor_tensor(
                                E.acc[fc][:, nb * 512:(nb + 1) * 512],
                                E.acc[fc][:, nb * 512:(nb + 1) * 512],
                                p3F[:, fc * 512:(fc + 1) * 512], op=ALU.add)
                        return last
                    p.op("vector", v_dn, cv)
                    prev_dn = cv.n
                for ei in range(3):
                    _MS[f"edone_{3 * g + ei}"] = ct.n
                if g + 1 < 5:
                    for ei in range(3):
                        stage_cbc(3 * (g + 1) + ei, ei, cv.n)
                for e_next in (3 * g + 4, 3 * g + 5, 3 * g + 6):
                    if e_next < EXP:
                        load_expert(e_next)
        _MS["p3_v"] = cv.n
        _tap_sbuf(p, E, cv, cv.n, "ffnacc", E.acc)


def _phase4(p, E):
    nc = E.nc
    ct, cv, cs, cg = E.ct, E.cv, E.cs, E.cg
    SY = E.SY
    with ExitStack() as S4:
        al = S4.enter_context
        x2t = [al(nc.sbuf_tensor(f"x2t{i}", [128, DM], BF16)) for i in range(2)]
        sqt = [al(nc.sbuf_tensor(f"sqt{i}", [128, DM], BF16)) for i in range(2)]
        otile = [al(nc.sbuf_tensor(f"otile{i}", [128, DM], F32)) for i in range(2)]
        st4 = al(nc.sbuf_tensor("st4", [128, 16], F32))
        p4B = al(nc.psum_tensor("p4B", [128, 1024], BF16))

        p.wait("vector", cv, _MS["p3_v"])
        for ec in range(NEC):
            def v_x2(h, ec=ec):
                return h.tensor_tensor(E.acc[ec][:, :], E.acc[ec][:, :],
                                       E.xln[ec][:, 2:2 + S], op=ALU.add)
            p.op("vector", v_x2, cv)
        x2_v = cv.n
        p.wait("tensor", cv, x2_v)

        for tc in range(NTC):
            b = tc % 2

            def t_tr(h, tc=tc):
                last = None
                for ec in range(NEC):
                    last = h.transpose(p4B[:, ec * 128:(ec + 1) * 128],
                                       E.acc[ec][:, tc * 128:(tc + 1) * 128],
                                       E.identb[:, :])
                return last
            if tc >= 1:
                p.wait("tensor", cv, _MS[f"p4ev_{tc - 1}"])
            p.op("tensor", t_tr, ct)
            p.wait("vector", ct, ct.n)

            def v_ev(h, b=b):
                return h.tensor_copy(x2t[b][:, :], p4B[:, 0:1024])
            p.op("vector", v_ev, cv)
            _MS[f"p4ev_{tc}"] = cv.n

            o = (tc % 2) * 8

            def v_stats(h, b=b, o=o):
                h.drain()
                h.tensor_tensor(sqt[b][:, :], x2t[b][:, :], x2t[b][:, :],
                                op=ALU.mult)
                h.reduce_sum(st4[:, o:o + 1], x2t[b][:, :],
                             axis=mybir.AxisListType.X)
                h.drain()
                h.reduce_sum(st4[:, o + 1:o + 2], sqt[b][:, :],
                             axis=mybir.AxisListType.X)
                h.drain()
                h.tensor_scalar_mul(st4[:, o + 2:o + 3], st4[:, o:o + 1], 1.0 / DM)
                h.tensor_scalar_mul(st4[:, o + 3:o + 4], st4[:, o + 1:o + 2],
                                    1.0 / DM)
                h.drain()
                h.tensor_tensor(st4[:, o + 4:o + 5], st4[:, o + 2:o + 3],
                                st4[:, o + 2:o + 3], op=ALU.mult)
                h.drain()
                h.tensor_tensor(st4[:, o + 5:o + 6], st4[:, o + 3:o + 4],
                                st4[:, o + 4:o + 5], op=ALU.subtract)
                h.drain()
                return h.tensor_scalar_add(st4[:, o + 5:o + 6],
                                           st4[:, o + 5:o + 6], EPS)
            p.op("vector", v_stats, cv)
            p.wait("scalar", cv, cv.n)

            def s_rstd(h, o=o):
                h.activation(out=st4[:, o + 6:o + 7], in_=st4[:, o + 5:o + 6],
                             func=AF.Ln)
                h.drain()
                return h.activation(out=st4[:, o + 6:o + 7],
                                    in_=st4[:, o + 6:o + 7], func=AF.Exp,
                                    scale=-0.5)
            p.op("scalar", s_rstd, cs)
            p.wait("vector", cs, cs.n)

            def v_nb(h, o=o):
                h.tensor_tensor(st4[:, o + 7:o + 8], st4[:, o + 2:o + 3],
                                st4[:, o + 6:o + 7], op=ALU.mult)
                h.drain()
                return h.tensor_scalar_mul(st4[:, o + 7:o + 8],
                                           st4[:, o + 7:o + 8], -1.0)
            p.op("vector", v_nb, cv)
            if tc >= 2:
                SY.wait("vector", _MS[f"p4dma_{tc - 2}"])

            def v_apply(h, b=b, o=o):
                h.drain()
                return h.tensor_scalar(otile[b][:, :], x2t[b][:, :],
                                       st4[:, o + 6:o + 7], st4[:, o + 7:o + 8],
                                       op0=ALU.mult, op1=ALU.add)
            p.op("vector", v_apply, cv)
            _MS[f"p4ap_{tc}"] = cv.n
            p.wait("sync", cv, cv.n)
            _MS[f"p4dma_{tc}"] = SY.issue(
                lambda h, tc=tc, b=b: h.dma_start(
                    out=E.out_d.ap()[tc * 128:(tc + 1) * 128, :],
                    in_=otile[b][:, :]))

        SY.barrier("sync")


def _dummy_out(p, E):
    nc = E.nc
    with ExitStack() as SD:
        otile = SD.enter_context(nc.sbuf_tensor("odum", [128, DM], F32))

        def v_z(h):
            return h.memset(otile[:, :], 0.0)
        p.op("vector", v_z, E.cv)
        p.wait("sync", E.cv, E.cv.n)
        for tc in range(NTC):
            E.SY.issue(lambda h, tc=tc: h.dma_start(
                out=E.out_d.ap()[tc * 128:(tc + 1) * 128, :],
                in_=otile[:, :]))
        E.SY.barrier("sync")


def kernel(**inputs):
    def f32(name):
        return np.ascontiguousarray(np.asarray(inputs[name], dtype=np.float32))

    x = f32("L0_attn_hidden_states")
    base = {
        "wq": f32("L0_attn_Wq"), "wk": f32("L0_attn_Wk"), "wv": f32("L0_attn_Wv"),
        "wo": f32("L0_attn_Wo"),
        "aq": f32("L0_attn_alpha_q"), "ak": f32("L0_attn_alpha_k"),
        "av": f32("L0_attn_alpha_v"), "co": f32("L0_attn_collapse"),
        "rw": f32("L0_ffn_router_w"), "rb": f32("L0_ffn_router_bias"),
        "gw": f32("L0_ffn_expert_gate_w"), "uw": f32("L0_ffn_expert_up_w"),
        "dw": f32("L0_ffn_expert_down_w"), "cw": f32("L0_ffn_conv_w"),
        "sup": f32("L0_ffn_shared_up_w"), "sdn": f32("L0_ffn_shared_down_w"),
    }
    nc = build_nc()
    in_maps = [dict(base, x=np.ascontiguousarray(x[i])) for i in range(NCORES)]
    res = run_bass_kernel_spmd(nc, in_maps, core_ids=list(range(NCORES)))
    return np.stack([r["out"] for r in res.results], axis=0)
